# version 13
# baseline (speedup 1.0000x reference)
import numpy as np
import ml_dtypes
from contextlib import ExitStack
import concourse.bass as bass
import concourse.mybir as mybir
from concourse.bass_utils import run_bass_kernel_spmd

F32 = mybir.dt.float32
BF16 = mybir.dt.bfloat16
AF = mybir.ActivationFunctionType
ALU = mybir.AluOpType
AX = mybir.AxisListType
BF = ml_dtypes.bfloat16

D = 1024
T = 4096
NH = 16
DH = 64
NE = 8
INTER = 4096
NC_ = 8
NTT = 32
NGRP = 8
GRP = 512
NEG = -1.0e9

_DEBUG_TAPS = []
_VERBOSE = False


class Prog:
    ENGINES = ("sync", "tensor", "vector", "scalar", "gpsimd")

    def __init__(self, nc):
        self.nc = nc
        self.streams = {e: [] for e in self.ENGINES}
        self.sems = {}
        self.cnt = {}
        self.last_wait = {}

    def sem(self, name, stack):
        h = stack.enter_context(self.nc.semaphore(name))
        self.sems[name] = h
        self.cnt[name] = 0

    def wait(self, eng, sem, val=None):
        v = self.cnt[sem] if val is None else val
        if v <= 0:
            return
        key = (eng, sem)
        if self.last_wait.get(key, -1) >= v:
            return
        self.last_wait[key] = v
        h = self.sems[sem]
        self.streams[eng].append(lambda e, h=h, v=v: e.wait_ge(h, v))

    def op(self, eng, build, inc=None, by=1):
        if inc is None:
            self.streams[eng].append(lambda e, b=build: b(e))
            return None
        h = self.sems[inc]
        self.cnt[inc] += by
        self.streams[eng].append(lambda e, b=build, h=h, by=by: b(e).then_inc(h, by))
        return self.cnt[inc]

    def c(self, sem):
        return self.cnt[sem]

    def flush(self):
        nc = self.nc
        streams = self.streams
        self.streams = {e: [] for e in self.ENGINES}
        with nc.Block() as block:
            if streams["sync"]:
                @block.sync
                def _(e, ops=streams["sync"]):
                    for f in ops:
                        f(e)
            if streams["tensor"]:
                @block.tensor
                def _(e, ops=streams["tensor"]):
                    for f in ops:
                        f(e)
            if streams["vector"]:
                @block.vector
                def _(e, ops=streams["vector"]):
                    for f in ops:
                        f(e)
            if streams["scalar"]:
                @block.scalar
                def _(e, ops=streams["scalar"]):
                    for f in ops:
                        f(e)
            if streams["gpsimd"]:
                @block.gpsimd
                def _(e, ops=streams["gpsimd"]):
                    for f in ops:
                        f(e)


def build_program():
    nc = bass.Bass("TRN2", target_bir_lowering=False)

    scratch_regs = {}

    def dram(name, shape, dt, kind):
        ap = nc.dram_tensor(name, shape, dt, kind=kind).ap()
        if kind == "Internal":
            scratch_regs[name] = (ap, shape)
        return ap

    X = dram("x", [T, D], F32, "ExternalInput")
    WQT = dram("wqt", [D, D], F32, "ExternalInput")
    WKT = dram("wkt", [D, D], F32, "ExternalInput")
    WVT = dram("wvt", [D, D], F32, "ExternalInput")
    WOT = dram("wot", [D, D], F32, "ExternalInput")
    WGT = dram("wgt", [D, NE], F32, "ExternalInput")
    BQ = dram("bq", [128, NC_], F32, "ExternalInput")
    BK = dram("bk", [128, NC_], F32, "ExternalInput")
    BV = dram("bv", [1, D], F32, "ExternalInput")
    BO = dram("bo", [1, D], F32, "ExternalInput")
    W1P = dram("w1p", [NE, 32, 128, NC_ * 128], F32, "ExternalInput")
    W2P = dram("w2p", [NE, 128, 32 * D], F32, "ExternalInput")
    B1F = dram("b1f", [NE, 128, 32], F32, "ExternalInput")
    B2 = dram("b2", [NE, D], F32, "ExternalInput")
    CC = dram("cc", [128, T], BF16, "ExternalInput")
    SS = dram("ss", [128, T], BF16, "ExternalInput")
    PSW = dram("psw", [128, 128], BF16, "ExternalInput")
    IDB = dram("idb", [128, 128], BF16, "ExternalInput")
    IDF = dram("idf", [128, 128], F32, "ExternalInput")
    ONEB = dram("oneb", [1, 128], BF16, "ExternalInput")
    ONEF = dram("onef", [1, 128], F32, "ExternalInput")
    MASK = dram("mask", [128, 4 * 512], BF16, "ExternalInput")

    OUT = dram("out", [T, D], F32, "ExternalOutput")

    XNT = dram("xnt_h", [D, T], BF16, "Internal")
    QT = dram("qt_h", [D, T], BF16, "Internal")
    KT = dram("kt_h", [D, T], BF16, "Internal")
    VH = dram("v_h", [T, D], BF16, "Internal")
    CATH = dram("cat_h", [D, T], BF16, "Internal")
    ATTH = dram("att_h", [T, D], F32, "Internal")
    YH = dram("y_h", [T, D], F32, "Internal")
    W1C = dram("w1c", [NE, 32, 128, NC_ * 128], BF16, "Internal")
    W2C = dram("w2c", [NE, 128, 32 * D], BF16, "Internal")

    with ExitStack() as outer:
        p = Prog(nc)
        for s in ["s_ld", "s_cast", "s_st", "s_pe", "s_act", "s_dve", "s_pl"]:
            p.sem(s, outer)

        idb = outer.enter_context(nc.sbuf_tensor("idb_sb", [128, 128], BF16))
        idf = outer.enter_context(nc.sbuf_tensor("idf_sb", [128, 128], F32))
        oneb = outer.enter_context(nc.sbuf_tensor("oneb_sb", [1, 128], BF16))
        onef = outer.enter_context(nc.sbuf_tensor("onef_sb", [1, 128], F32))
        gates = outer.enter_context(nc.sbuf_tensor("gates_sb", [128, NTT * NE], F32))
        gatesT = outer.enter_context(nc.sbuf_tensor("gatesT_sb", [NE, T], BF16))

        if _VERBOSE:
            print("P01 sbuf_remaining", nc.sbuf_bytes_remaining)
        with ExitStack() as ph:
            sb = lambda n, s, d: ph.enter_context(nc.sbuf_tensor(n, s, d))
            ps = lambda n, s, d: ph.enter_context(nc.psum_tensor(n, s, d))
            x_b = [sb(f"x_b{i}", [128, D], F32) for i in range(2)]
            sq = [sb(f"sq{i}", [128, D], F32) for i in range(2)]
            s2 = [sb(f"s2_{i}", [128, 1], F32) for i in range(2)]
            nsum = [sb(f"nsum{i}", [128, 1], F32) for i in range(2)]
            m_neg = [sb(f"mneg{i}", [128, 1], F32) for i in range(2)]
            m2t = [sb(f"m2t{i}", [128, 1], F32) for i in range(2)]
            var = [sb(f"var{i}", [128, 1], F32) for i in range(2)]
            rstd = [sb(f"rstd{i}", [128, 1], F32) for i in range(2)]
            xn_b = [sb(f"xn_b{i}", [128, D], BF16) for i in range(2)]
            xnT = sb("xnT", [128, NC_ * T], BF16)
            wq = sb("wq", [128, NC_ * D], BF16)
            wk = sb("wk", [128, NC_ * D], BF16)
            wv = sb("wv", [128, NC_ * D], BF16)
            wg = sb("wg", [128, NC_ * NE], BF16)
            bqf = sb("bqf", [128, NC_], F32)
            bkf = sb("bkf", [128, NC_], F32)
            bvr = sb("bvr", [1, D], BF16)
            cc = sb("cc_sb", [128, T], BF16)
            ss = sb("ss_sb", [128, T], BF16)
            psw = sb("psw_sb", [128, 128], BF16)
            q_sb = [sb(f"q_sb{i}", [128, 512], BF16) for i in range(2)]
            t1b = [sb(f"t1b{i}", [128, 512], BF16) for i in range(2)]
            t2b = [sb(f"t2b{i}", [128, 512], BF16) for i in range(2)]
            rope_b = [sb(f"rope{i}", [128, 512], BF16) for i in range(2)]
            v_b = [sb(f"v_b{i}", [128, 512], BF16) for i in range(2)]
            logT = sb("logT", [NE, T], F32)
            lt_b = sb("lt_b", [128, NE], F32)
            g_m1 = sb("g_m1", [128, 1], F32)
            g_m2 = sb("g_m2", [128, 1], F32)
            g_d12 = sb("g_d12", [128, 1], F32)
            g_w1 = sb("g_w1", [128, 1], F32)
            g_w2 = sb("g_w2", [128, 1], F32)
            g_is1 = sb("g_is1", [128, NE], F32)
            g_is2 = sb("g_is2", [128, NE], F32)
            g_msk = sb("g_msk", [128, NE], F32)
            g_g1 = sb("g_g1", [128, NE], F32)
            gt_b = sb("gt_b", [128, NE], F32)
            ps_tr = [ps(f"ps_tr{i}", [128, 1024], BF16) for i in range(2)]
            ps_pr = [ps(f"ps_pr{i}", [128, 512], F32) for i in range(2)]
            ps_sw = [ps(f"ps_sw{i}", [128, 512], F32) for i in range(2)]
            ps_g = ps("ps_g", [128, 512], F32)

            p.op("sync", lambda e: e.dma_start(out=idb[:], in_=IDB), "s_ld", 16)
            p.op("sync", lambda e: e.dma_start(out=idf[:], in_=IDF), "s_ld", 16)
            p.op("sync", lambda e: e.dma_start(out=oneb[:], in_=ONEB), "s_ld", 16)
            p.op("sync", lambda e: e.dma_start(out=onef[:], in_=ONEF), "s_ld", 16)
            p.op("sync", lambda e: e.dma_start(out=cc[:], in_=CC), "s_ld", 16)
            p.op("sync", lambda e: e.dma_start(out=ss[:], in_=SS), "s_ld", 16)
            p.op("sync", lambda e: e.dma_start(out=psw[:], in_=PSW), "s_ld", 16)
            p.op("sync", lambda e: e.dma_start(out=bqf[:], in_=BQ), "s_ld", 16)
            c_tabs = p.op("sync", lambda e: e.dma_start(out=bkf[:], in_=BK), "s_ld", 16)
            for w_sb_, W_ in [(wq, WQT), (wk, WKT), (wv, WVT)]:
                p.op("gpsimd", lambda e, o=w_sb_, i=W_: e.dma_start(
                    out=o[:].rearrange("p (c m) -> p c m", c=NC_),
                    in_=i.rearrange("(c p) m -> p c m", p=128)), "s_cast", 16)
            p.op("gpsimd", lambda e: e.dma_start(
                out=wg[:].rearrange("p (c m) -> p c m", c=NC_),
                in_=WGT.rearrange("(c p) m -> p c m", p=128)), "s_cast", 16)
            c_wproj = p.op("gpsimd", lambda e: e.dma_start(out=bvr[:], in_=BV), "s_cast", 16)
            for e_ in range(NE):
                for q4 in range(4):
                    sl = slice(q4 * 8, q4 * 8 + 8)
                    p.op("gpsimd", lambda e, e_=e_, sl=sl: e.dma_start(
                        out=W1C[e_, sl], in_=W1P[e_, sl]), "s_cast", 16)
                    sl2 = slice(q4 * 32, q4 * 32 + 32)
                    p.op("gpsimd", lambda e, e_=e_, sl2=sl2: e.dma_start(
                        out=W2C[e_, sl2], in_=W2P[e_, sl2]), "s_cast", 16)
            c_wcache = p.c("s_cast")

            ln_done = [0, 0]
            tr_done = [0, 0]
            xn_done = [0, 0]
            for t in range(NTT):
                b = t % 2
                p.wait("sync", "s_dve", ln_done[b])
                c_x = p.op("sync", lambda e, t=t, b=b: e.dma_start(
                    out=x_b[b][:], in_=X[t * 128:(t + 1) * 128, :]), "s_ld", 16)
                p.wait("scalar", "s_ld", c_x)
                p.wait("scalar", "s_dve", ln_done[b])
                c_sq = p.op("scalar", lambda e, b=b: e.activation(
                    sq[b][:], x_b[b][:], AF.Square, accum_out=s2[b][:]), "s_act")
                p.wait("vector", "s_ld", c_x)
                c_ns = p.op("vector", lambda e, b=b: e.tensor_reduce(
                    nsum[b][:], x_b[b][:], axis=AX.X, op=ALU.add, negate=True), "s_dve")
                p.wait("vector", "s_dve", c_ns)
                p.op("vector", lambda e, b=b: e.tensor_scalar(
                    m_neg[b][:], nsum[b][:], 1.0 / D, None, op0=ALU.mult))
                c_mn = p.op("vector", lambda e, b=b: e.scalar_tensor_tensor(
                    m2t[b][:], nsum[b][:], 1.0 / (D * D), nsum[b][:],
                    op0=ALU.mult, op1=ALU.mult), "s_dve")
                p.wait("vector", "s_act", c_sq)
                p.wait("vector", "s_dve", c_mn)
                c_va = p.op("vector", lambda e, b=b: e.tensor_scalar(
                    var[b][:], s2[b][:], 1.0 / D, 1e-5, op0=ALU.mult, op1=ALU.add), "s_dve")
                p.wait("vector", "s_dve", c_va)
                c_var = p.op("vector", lambda e, b=b: e.tensor_tensor(
                    var[b][:], var[b][:], m2t[b][:], op=ALU.subtract), "s_dve")
                p.wait("scalar", "s_dve", c_var)
                c_sqr = p.op("scalar", lambda e, b=b: e.activation(
                    rstd[b][:], var[b][:], AF.Sqrt), "s_act")
                p.wait("vector", "s_act", c_sqr)
                c_rs = p.op("vector", lambda e, b=b: e.reciprocal(
                    rstd[b][:], rstd[b][:]), "s_dve")
                p.wait("vector", "s_dve", c_rs)
                p.wait("vector", "s_pe", xn_done[b])
                c_xn = p.op("vector", lambda e, b=b: e.tensor_scalar(
                    xn_b[b][:], x_b[b][:], m_neg[b][:], rstd[b][:],
                    op0=ALU.add, op1=ALU.mult), "s_dve")
                ln_done[b] = c_xn
                p.wait("tensor", "s_dve", c_xn)
                p.wait("tensor", "s_ld", c_tabs)
                for c_ in range(NC_):
                    tb = c_ % 2
                    p.wait("tensor", "s_dve", tr_done[tb])
                    c_tr = p.op("tensor", lambda e, b=b, c_=c_, tb=tb: e.transpose(
                        ps_tr[tb][:, 0:128], xn_b[b][:, c_ * 128:(c_ + 1) * 128],
                        idb[:]), "s_pe")
                    if c_ == NC_ - 1:
                        xn_done[b] = c_tr
                    p.wait("vector", "s_pe", c_tr)
                    tr_done[tb] = p.op("vector", lambda e, t=t, c_=c_, tb=tb: e.tensor_copy(
                        xnT[:, c_ * T + t * 128: c_ * T + (t + 1) * 128],
                        ps_tr[tb][:, 0:128]), "s_dve")
            c_xnT_all = p.c("s_dve")

            pr_done = [0, 0]
            sw_done = [0, 0]
            qs_done = [0, 0]
            rope_st = [0, 0]
            p.wait("tensor", "s_cast", c_wproj)
            p.wait("tensor", "s_dve", c_xnT_all)
            for (w_sb, b_fm, OUT_T) in [(wq, bqf, QT), (wk, bkf, KT)]:
                for m in range(NC_):
                    for g in range(NC_):
                        pb = (m * NC_ + g) % 2
                        p.wait("tensor", "s_dve", pr_done[pb])
                        c_mm = None
                        for c_ in range(NC_):
                            if c_ == NC_ - 1:
                                c_mm = p.op("tensor", lambda e, w_sb=w_sb, m=m, g=g, c_=c_, pb=pb: e.matmul(
                                    ps_pr[pb][:], w_sb[:, c_ * D + m * 128:c_ * D + (m + 1) * 128],
                                    xnT[:, c_ * T + g * 512:c_ * T + (g + 1) * 512],
                                    start=False, stop=True), "s_pe")
                            else:
                                p.op("tensor", lambda e, w_sb=w_sb, m=m, g=g, c_=c_, pb=pb: e.matmul(
                                    ps_pr[pb][:], w_sb[:, c_ * D + m * 128:c_ * D + (m + 1) * 128],
                                    xnT[:, c_ * T + g * 512:c_ * T + (g + 1) * 512],
                                    start=(c_ == 0), stop=False))
                        p.wait("vector", "s_pe", c_mm)
                        p.wait("vector", "s_pe", qs_done[pb])
                        p.wait("vector", "s_dve", sw_done[pb])
                        c_ev = p.op("vector", lambda e, b_fm=b_fm, m=m, pb=pb: e.tensor_scalar(
                            q_sb[pb][:], ps_pr[pb][:], b_fm[:, m:m + 1], None,
                            op0=ALU.add), "s_dve")
                        pr_done[pb] = c_ev
                        p.wait("tensor", "s_dve", c_ev)
                        p.wait("tensor", "s_dve", sw_done[pb])
                        c_sw = p.op("tensor", lambda e, pb=pb: e.matmul(
                            ps_sw[pb][:], psw[:], q_sb[pb][:], start=True, stop=True), "s_pe")
                        qs_done[pb] = c_sw
                        p.wait("vector", "s_dve", c_ev)
                        p.op("vector", lambda e, g=g, pb=pb: e.tensor_tensor(
                            t1b[pb][:], q_sb[pb][:], cc[:, g * 512:(g + 1) * 512], op=ALU.mult))
                        p.wait("vector", "s_pe", c_sw)
                        c_t2 = p.op("vector", lambda e, g=g, pb=pb: e.tensor_tensor(
                            t2b[pb][:], ps_sw[pb][:], ss[:, g * 512:(g + 1) * 512], op=ALU.mult), "s_dve")
                        sw_done[pb] = c_t2
                        p.wait("vector", "s_dve", c_t2)
                        p.wait("vector", "s_st", rope_st[pb])
                        c_rp = p.op("vector", lambda e, pb=pb: e.tensor_tensor(
                            rope_b[pb][:], t1b[pb][:], t2b[pb][:], op=ALU.add), "s_dve")
                        p.wait("scalar", "s_dve", c_rp)
                        rope_st[pb] = p.op("scalar", lambda e, OUT_T=OUT_T, m=m, g=g, pb=pb: e.dma_start(
                            out=OUT_T[m * 128:(m + 1) * 128, g * 512:(g + 1) * 512],
                            in_=rope_b[pb][:]), "s_st", 16)

            v_ev_done = [0, 0]
            v_st = [0, 0]
            for tt in range(NTT):
                for n in range(2):
                    pb = (tt * 2 + n) % 2
                    p.wait("tensor", "s_dve", v_ev_done[pb])
                    p.wait("tensor", "s_dve", pr_done[pb])
                    p.op("tensor", lambda e, n=n, pb=pb: e.matmul(
                        ps_pr[pb][:], oneb[:], bvr[:, n * 512:(n + 1) * 512],
                        start=True, stop=False))
                    for c_ in range(NC_):
                        if c_ == NC_ - 1:
                            c_mm = p.op("tensor", lambda e, tt=tt, n=n, c_=c_, pb=pb: e.matmul(
                                ps_pr[pb][:], xnT[:, c_ * T + tt * 128:c_ * T + (tt + 1) * 128],
                                wv[:, c_ * D + n * 512:c_ * D + (n + 1) * 512],
                                start=False, stop=True), "s_pe")
                        else:
                            p.op("tensor", lambda e, tt=tt, n=n, c_=c_, pb=pb: e.matmul(
                                ps_pr[pb][:], xnT[:, c_ * T + tt * 128:c_ * T + (tt + 1) * 128],
                                wv[:, c_ * D + n * 512:c_ * D + (n + 1) * 512],
                                start=False, stop=False))
                    p.wait("vector", "s_pe", c_mm)
                    p.wait("vector", "s_st", v_st[pb])
                    c_ev = p.op("vector", lambda e, pb=pb: e.tensor_copy(
                        v_b[pb][:], ps_pr[pb][:]), "s_dve")
                    v_ev_done[pb] = c_ev
                    p.wait("scalar", "s_dve", c_ev)
                    v_st[pb] = p.op("scalar", lambda e, tt=tt, n=n, pb=pb: e.dma_start(
                        out=VH[tt * 128:(tt + 1) * 128, n * 512:(n + 1) * 512],
                        in_=v_b[pb][:]), "s_st", 16)

            c_lg = 0
            for g in range(NC_):
                p.wait("tensor", "s_dve", c_lg)
                for c_ in range(NC_):
                    if c_ == NC_ - 1:
                        c_mm = p.op("tensor", lambda e, g=g, c_=c_: e.matmul(
                            ps_g[0:NE, 0:512], wg[:, c_ * NE:(c_ + 1) * NE],
                            xnT[:, c_ * T + g * 512:c_ * T + (g + 1) * 512],
                            start=False, stop=True), "s_pe")
                    else:
                        p.op("tensor", lambda e, g=g, c_=c_: e.matmul(
                            ps_g[0:NE, 0:512], wg[:, c_ * NE:(c_ + 1) * NE],
                            xnT[:, c_ * T + g * 512:c_ * T + (g + 1) * 512],
                            start=(c_ == 0), stop=False))
                p.wait("vector", "s_pe", c_mm)
                c_lg = p.op("vector", lambda e, g=g: e.tensor_copy(
                    logT[0:NE, g * 512:(g + 1) * 512], ps_g[0:NE, 0:512]), "s_dve")
            for tt in range(NTT):
                p.wait("tensor", "s_dve", c_lg)
                c_tr = p.op("tensor", lambda e, tt=tt: e.transpose(
                    ps_g[:, 0:NE], logT[0:NE, tt * 128:(tt + 1) * 128],
                    idf[0:NE, 0:NE]), "s_pe")
                p.wait("vector", "s_pe", c_tr)
                c_lt = p.op("vector", lambda e: e.tensor_copy(
                    lt_b[:], ps_g[:, 0:NE]), "s_dve")
                p.wait("vector", "s_dve", c_lt)
                c_m1 = p.op("vector", lambda e: e.tensor_reduce(
                    g_m1[:], lt_b[:], axis=AX.X, op=ALU.max), "s_dve")
                p.wait("vector", "s_dve", c_m1)
                c_is1 = p.op("vector", lambda e: e.tensor_scalar(
                    g_is1[:], lt_b[:], g_m1[:], None, op0=ALU.is_ge), "s_dve")
                p.wait("vector", "s_dve", c_is1)
                c_msk = p.op("vector", lambda e: e.scalar_tensor_tensor(
                    g_msk[:], g_is1[:], NEG, lt_b[:], op0=ALU.mult, op1=ALU.add), "s_dve")
                p.wait("vector", "s_dve", c_msk)
                c_m2 = p.op("vector", lambda e: e.tensor_reduce(
                    g_m2[:], g_msk[:], axis=AX.X, op=ALU.max), "s_dve")
                p.wait("vector", "s_dve", c_m2)
                c_d = p.op("vector", lambda e: e.tensor_tensor(
                    g_d12[:], g_m1[:], g_m2[:], op=ALU.subtract), "s_dve")
                p.wait("scalar", "s_dve", c_d)
                p.op("scalar", lambda e: e.activation(g_w1[:], g_d12[:], AF.Sigmoid))
                c_w2 = p.op("scalar", lambda e: e.activation(
                    g_w2[:], g_d12[:], AF.Sigmoid, scale=-1.0), "s_act")
                c_is2 = p.op("vector", lambda e: e.tensor_scalar(
                    g_is2[:], g_msk[:], g_m2[:], None, op0=ALU.is_ge), "s_dve")
                p.wait("vector", "s_act", c_w2)
                p.wait("vector", "s_dve", c_is2)
                c_g1 = p.op("vector", lambda e: e.tensor_scalar(
                    g_g1[:], g_is1[:], g_w1[:], None, op0=ALU.mult), "s_dve")
                p.wait("vector", "s_dve", c_g1)
                c_gt = p.op("vector", lambda e, tt=tt: e.scalar_tensor_tensor(
                    gates[:, tt * NE:(tt + 1) * NE], g_is2[:], g_w2[:], g_g1[:],
                    op0=ALU.mult, op1=ALU.add), "s_dve")
                p.wait("vector", "s_dve", c_gt)
                c_gtb = p.op("vector", lambda e, tt=tt: e.tensor_copy(
                    gt_b[:], gates[:, tt * NE:(tt + 1) * NE]), "s_dve")
                p.wait("tensor", "s_dve", c_gtb)
                c_tr2 = p.op("tensor", lambda e: e.transpose(
                    ps_g[0:NE, 0:128], gt_b[:], idf[:]), "s_pe")
                p.wait("vector", "s_pe", c_tr2)
                c_lg = p.op("vector", lambda e, tt=tt: e.tensor_copy(
                    gatesT[0:NE, tt * 128:(tt + 1) * 128], ps_g[0:NE, 0:128]), "s_dve")

            p.wait("scalar", "s_dve", c_xnT_all)
            c_xnt_sp = 0
            for half in range(2):
                c_xnt_sp = p.op("scalar", lambda e, half=half: e.dma_start(
                    out=XNT.rearrange("(c p) t -> p c t", p=128)[:, half * 4:(half + 1) * 4, :],
                    in_=xnT[:].rearrange("p (c t) -> p c t", c=NC_)[:, half * 4:(half + 1) * 4, :]),
                    "s_st", 16)
            p.flush()

        if _VERBOSE:
            print("P2 sbuf_remaining", nc.sbuf_bytes_remaining)
        with ExitStack() as ph:
            sb = lambda n, s, d: ph.enter_context(nc.sbuf_tensor(n, s, d))
            ps = lambda n, s, d: ph.enter_context(nc.psum_tensor(n, s, d))
            qp = sb("qp", [128, T], BF16)
            kp = sb("kp", [128, T], BF16)
            vp = sb("vp", [128, NTT * 128], BF16)
            pbA = sb("pbA", [128, T], BF16)
            pbB = sb("pbB", [128, T], BF16)
            pT = sb("pT", [128, NTT * 128], BF16)
            mask = sb("mask_sb", [128, 4 * 512], BF16)
            mpA = sb("mpA", [128, 8], F32)
            mpB = sb("mpB", [128, 8], F32)
            mj8A = sb("mj8A", [128, 8], F32)
            mj8B = sb("mj8B", [128, 8], F32)
            nmA = sb("nmA", [128, 1], F32)
            nmB = sb("nmB", [128, 1], F32)
            nm8A = sb("nm8A", [128, 1], F32)
            nm8B = sb("nm8B", [128, 1], F32)
            csA = sb("csA", [128, 8], F32)
            csB = sb("csB", [128, 8], F32)
            lpA = sb("lpA", [128, 8], F32)
            lpB = sb("lpB", [128, 8], F32)
            lwA = sb("lwA", [128, 8], F32)
            lwB = sb("lwB", [128, 8], F32)
            lA2 = sb("lA2", [128, 1], F32)
            lB2 = sb("lB2", [128, 1], F32)
            li_rA = sb("li_rA", [1, 128], F32)
            li_rB = sb("li_rB", [1, 128], F32)
            bc_sb = sb("bc_sb", [128, 128], F32)
            cat_b = [sb(f"cat_b{i}", [128, 128], BF16) for i in range(2)]
            w2 = sb("w2", [128, 32 * D], BF16)
            w1blk = [sb(f"w1blk{i}", [128, NC_ * 128], BF16) for i in range(2)]
            xng = sb("xng", [128, NC_ * GRP], BF16)
            h_sb = sb("h_sb", [128, 32 * GRP], BF16)
            y_sb = sb("y_sb", [128, 4 * D], F32)
            b1f = [sb(f"b1f{i}", [128, 32], F32) for i in range(2)]
            b2s = sb("b2s", [NE, D], BF16)
            ps_sA = ps("ps_sA", [128, 512], F32)
            ps_sB = ps("ps_sB", [128, 512], F32)
            ps_t = [ps(f"ps_t{i}", [128, 1024], BF16) for i in range(2)]
            ps_pv = ps("ps_pv", [128, 512], F32)
            ps_h = [ps(f"ps_h{i}", [128, 512], F32) for i in range(2)]
            ps_oo = ps("ps_oo", [128, 512], F32)

            p.wait("sync", "s_st", p.c("s_st"))
            p.wait("gpsimd", "s_st", p.c("s_st"))
            p.wait("sync", "s_cast", c_wcache)
            c_mask = p.op("sync", lambda e: e.dma_start(out=mask[:], in_=MASK), "s_ld", 16)
            c_b2s = p.op("gpsimd", lambda e: e.dma_start(out=b2s[:], in_=B2), "s_cast", 16)
            c_cat_stores = [0]
            c_y_stores = [0]

            def gen_att():
                sc_evA = 0
                sc_evB = 0
                pb_used = 0
                mp_free = 0
                l_pe = 0
                li_pe = 0
                tr_ev = [("s_dve", 0), ("s_act", 0)]
                pv_ev = 0
                pair_free = 0
                cat_st = [0, 0]
                lw_done = 0
                for pr in range(NH // 2):
                    p.wait("sync", "s_pe", pair_free)
                    p.op("sync", lambda e, pr=pr: e.dma_start(
                        out=qp[:], in_=QT[pr * 128:(pr + 1) * 128, :]), "s_ld", 16)
                    p.op("sync", lambda e, pr=pr: e.dma_start(
                        out=kp[:], in_=KT[pr * 128:(pr + 1) * 128, :]), "s_ld", 16)
                    c_pld = None
                    for half in range(2):
                        c_pld = p.op("sync", lambda e, pr=pr, half=half: e.dma_start(
                            out=vp[:].rearrange("p (tt d) -> p tt d", tt=NTT)[:, half * 16:(half + 1) * 16, :],
                            in_=VH.rearrange("(tt p) d -> p tt d", p=128)[:, half * 16:(half + 1) * 16,
                                pr * 128:(pr + 1) * 128]), "s_ld", 16)
                    for i in range(NTT):
                        jd = i // 4
                        r = i % 4
                        ktiles = i + 1
                        nch = jd + 1
                        for j in range(nch):
                            diag = (j == jd)
                            p.wait("tensor", "s_ld", c_pld)
                            p.wait("tensor", "s_act", sc_evB)
                            p.op("tensor", lambda e, i=i, j=j: e.matmul(
                                ps_sA[:], qp[0:64, i * 128:(i + 1) * 128],
                                kp[0:64, j * 512:(j + 1) * 512],
                                start=True, stop=True, tile_position=(0, 0)))
                            c_s = p.op("tensor", lambda e, i=i, j=j: e.matmul(
                                ps_sB[:], qp[64:128, i * 128:(i + 1) * 128],
                                kp[64:128, j * 512:(j + 1) * 512],
                                start=True, stop=True, tile_position=(64, 0)), "s_pe")
                            p.wait("vector", "s_pe", c_s)
                            p.wait("vector", "s_ld", c_mask)
                            if j == 0:
                                p.wait("vector", "s_act", mp_free)
                            if diag:
                                p.op("vector", lambda e, r=r: e.tensor_tensor(
                                    ps_sA[:], ps_sA[:], mask[:, r * 512:(r + 1) * 512], op=ALU.add))
                                p.op("vector", lambda e, r=r: e.tensor_tensor(
                                    ps_sB[:], ps_sB[:], mask[:, r * 512:(r + 1) * 512], op=ALU.add))
                            p.op("vector", lambda e, j=j: e.tensor_reduce(
                                mpA[:, j:j + 1], ps_sA[:], axis=AX.X, op=ALU.max))
                            c_mp = p.op("vector", lambda e, j=j: e.tensor_reduce(
                                mpB[:, j:j + 1], ps_sB[:], axis=AX.X, op=ALU.max), "s_dve")
                            p.wait("vector", "s_dve", c_mp)
                            p.op("vector", lambda e, j=j: e.tensor_scalar(
                                mj8A[:, j:j + 1], mpA[:, j:j + 1], -0.125, None, op0=ALU.mult))
                            c_mj = p.op("vector", lambda e, j=j: e.tensor_scalar(
                                mj8B[:, j:j + 1], mpB[:, j:j + 1], -0.125, None, op0=ALU.mult), "s_dve")
                            p.wait("scalar", "s_dve", c_mj)
                            if j == 0:
                                p.wait("scalar", "s_pe", pb_used)
                                p.wait("scalar", "s_dve", lw_done)
                            sc_evA = p.op("scalar", lambda e, j=j: e.activation(
                                pbA[:, j * 512:(j + 1) * 512], ps_sA[:], AF.Exp,
                                bias=mj8A[:, j:j + 1], scale=0.125,
                                accum_out=lpA[:, j:j + 1]), "s_act")
                            sc_evB = p.op("scalar", lambda e, j=j: e.activation(
                                pbB[:, j * 512:(j + 1) * 512], ps_sB[:], AF.Exp,
                                bias=mj8B[:, j:j + 1], scale=0.125,
                                accum_out=lpB[:, j:j + 1]), "s_act")
                            yield 1.2
                        p.wait("vector", "s_act", sc_evB)
                        p.op("vector", lambda e, nch=nch: e.tensor_reduce(
                            nmA[:], mpA[:, 0:nch], axis=AX.X, op=ALU.max))
                        c_nm = p.op("vector", lambda e, nch=nch: e.tensor_reduce(
                            nmB[:], mpB[:, 0:nch], axis=AX.X, op=ALU.max), "s_dve")
                        p.wait("vector", "s_dve", c_nm)
                        p.op("vector", lambda e: e.tensor_scalar(
                            nm8A[:], nmA[:], -0.125, None, op0=ALU.mult))
                        c_nm8 = p.op("vector", lambda e: e.tensor_scalar(
                            nm8B[:], nmB[:], -0.125, None, op0=ALU.mult), "s_dve")
                        p.wait("scalar", "s_dve", c_nm8)
                        p.op("scalar", lambda e, nch=nch: e.activation(
                            csA[:, 0:nch], mpA[:, 0:nch], AF.Exp, bias=nm8A[:], scale=0.125))
                        c_cs = p.op("scalar", lambda e, nch=nch: e.activation(
                            csB[:, 0:nch], mpB[:, 0:nch], AF.Exp, bias=nm8B[:], scale=0.125), "s_act")
                        mp_free = c_cs
                        p.wait("vector", "s_act", c_cs)
                        p.op("vector", lambda e, nch=nch: e.tensor_tensor(
                            lwA[:, 0:nch], lpA[:, 0:nch], csA[:, 0:nch], op=ALU.mult))
                        c_lw = p.op("vector", lambda e, nch=nch: e.tensor_tensor(
                            lwB[:, 0:nch], lpB[:, 0:nch], csB[:, 0:nch], op=ALU.mult), "s_dve")
                        p.wait("vector", "s_dve", c_lw)
                        p.wait("vector", "s_pe", l_pe)
                        p.op("vector", lambda e, nch=nch: e.reduce_sum(
                            lA2[:], lwA[:, 0:nch], axis=AX.X))
                        c_l = p.op("vector", lambda e, nch=nch: e.reduce_sum(
                            lB2[:], lwB[:, 0:nch], axis=AX.X), "s_dve")
                        lw_done = c_l
                        c_corr = None
                        for j in range(nch):
                            p.op("vector", lambda e, j=j: e.tensor_scalar(
                                pbA[:, j * 512:(j + 1) * 512], pbA[:, j * 512:(j + 1) * 512],
                                csA[:, j:j + 1], None, op0=ALU.mult))
                            c_corr = p.op("vector", lambda e, j=j: e.tensor_scalar(
                                pbB[:, j * 512:(j + 1) * 512], pbB[:, j * 512:(j + 1) * 512],
                                csB[:, j:j + 1], None, op0=ALU.mult), "s_dve")
                        yield 0.5
                        p.wait("tensor", "s_dve", c_l)
                        p.wait("tensor", "s_dve", pv_ev)
                        p.op("tensor", lambda e: e.transpose(
                            ps_pv[0:1, 256:384], lA2[:], idf[:]))
                        c_lt = p.op("tensor", lambda e: e.transpose(
                            ps_pv[0:1, 384:512], lB2[:], idf[:]), "s_pe")
                        l_pe = c_lt
                        p.wait("vector", "s_pe", c_lt)
                        p.op("vector", lambda e: e.reciprocal(li_rA[:], ps_pv[0:1, 256:384]))
                        c_li = p.op("vector", lambda e: e.reciprocal(li_rB[:], ps_pv[0:1, 384:512]), "s_dve")
                        yield 0.2
                        c_pv = None
                        for hh in range(2):
                            pbuf = pbA if hh == 0 else pbB
                            p.wait("tensor", "s_dve", c_corr)
                            last_dve = 0
                            last_act = 0
                            ngrp = (ktiles + 3) // 4
                            for gi in range(ngrp):
                                kt0 = gi * 4
                                kt1 = min(ktiles, kt0 + 4)
                                tb = gi % 2
                                sem_n, cnt = tr_ev[tb]
                                p.wait("tensor", sem_n, cnt)
                                c_tr = None
                                for kt in range(kt0, kt1):
                                    c_tr = p.op("tensor", lambda e, kt=kt, kt0=kt0, tb=tb, pbuf=pbuf: e.transpose(
                                        ps_t[tb][:, (kt - kt0) * 128:(kt - kt0 + 1) * 128],
                                        pbuf[:, kt * 128:(kt + 1) * 128],
                                        idb[:]), "s_pe")
                                if kt1 == ktiles and hh == 1:
                                    pb_used = c_tr
                                eng = "vector" if tb == 0 else "scalar"
                                p.wait(eng, "s_pe", c_tr)
                                w = (kt1 - kt0) * 128
                                if eng == "vector":
                                    c_ev = p.op("vector", lambda e, kt0=kt0, w=w, tb=tb: e.tensor_copy(
                                        pT[:, kt0 * 128:kt0 * 128 + w], ps_t[tb][:, 0:w]), "s_dve")
                                    last_dve = c_ev
                                    tr_ev[tb] = ("s_dve", c_ev)
                                else:
                                    c_ev = p.op("scalar", lambda e, kt0=kt0, w=w, tb=tb: e.copy(
                                        pT[:, kt0 * 128:kt0 * 128 + w], ps_t[tb][:, 0:w]), "s_act")
                                    last_act = c_ev
                                    tr_ev[tb] = ("s_act", c_ev)
                                yield 0.8
                            p.wait("tensor", "s_dve", last_dve)
                            p.wait("tensor", "s_act", last_act)
                            if hh == 0:
                                p.wait("tensor", "s_dve", c_li)
                            off = 0 if hh == 0 else 64
                            tp = (0, 0) if hh == 0 else (0, 64)
                            for kt in range(ktiles):
                                if kt == ktiles - 1:
                                    c_pv = p.op("tensor", lambda e, kt=kt, off=off, tp=tp: e.matmul(
                                        ps_pv[off:off + 64, 0:128],
                                        vp[:, kt * 128 + off:kt * 128 + off + 64],
                                        pT[:, kt * 128:(kt + 1) * 128],
                                        start=(kt == 0), stop=True, tile_position=tp), "s_pe")
                                else:
                                    p.op("tensor", lambda e, kt=kt, off=off, tp=tp: e.matmul(
                                        ps_pv[off:off + 64, 0:128],
                                        vp[:, kt * 128 + off:kt * 128 + off + 64],
                                        pT[:, kt * 128:(kt + 1) * 128],
                                        start=(kt == 0), stop=False, tile_position=tp))
                            yield 0.2 + 0.12 * ktiles
                        p.op("tensor", lambda e: e.matmul(
                            ps_pv[0:64, 128:256], onef[0:1, 0:64], li_rA[:],
                            start=True, stop=True, tile_position=(0, 0)))
                        c_bc = p.op("tensor", lambda e: e.matmul(
                            ps_pv[64:128, 128:256], onef[0:1, 0:64], li_rB[:],
                            start=True, stop=True, tile_position=(0, 64)), "s_pe")
                        li_pe = c_bc
                        if i == NTT - 1:
                            pair_free = c_bc
                        cb = i % 2
                        p.wait("vector", "s_pe", c_bc)
                        c_bce = p.op("vector", lambda e: e.tensor_copy(
                            bc_sb[:], ps_pv[:, 128:256]), "s_dve")
                        p.wait("vector", "s_dve", c_bce)
                        p.wait("vector", "s_st", cat_st[cb])
                        c_cat = p.op("vector", lambda e, cb=cb: e.tensor_tensor(
                            cat_b[cb][:], ps_pv[:, 0:128], bc_sb[:], op=ALU.mult), "s_dve")
                        pv_ev = c_cat
                        p.wait("scalar", "s_dve", c_cat)
                        cat_st[cb] = p.op("scalar", lambda e, pr=pr, i=i, cb=cb: e.dma_start(
                            out=CATH[pr * 128:(pr + 1) * 128, i * 128:(i + 1) * 128],
                            in_=cat_b[cb][:]), "s_st", 16)
                        c_cat_stores[0] = cat_st[cb]
                        yield 0.3

            def gen_moe():
                h_ev = [0, 0]
                o_ev = 0
                i_ev = 0
                w1_ld = [0, 0]
                w1_free = [0, 0]
                w2_ld = 0
                w2_free = 0
                b1_ld = [0, 0]
                b1_free = [0, 0]
                xng_ld = 0
                xng_free = 0
                y_init = [0] * 8
                y_last = [0] * 8
                y_spill_st = 0

                def load_w2(e_):
                    c = None
                    for q4 in range(4):
                        c = p.op("sync", lambda e, e_=e_, q4=q4: e.dma_start(
                            out=w2[:, q4 * 8 * D:(q4 + 1) * 8 * D],
                            in_=W2C[e_, :, q4 * 8 * D:(q4 + 1) * 8 * D]), "s_ld", 16)
                    return c

                def load_w1(e_, mi):
                    rs = mi % 2
                    p.wait("sync", "s_pe", w1_free[rs])
                    w1_ld[rs] = p.op("sync", lambda e, e_=e_, mi=mi, rs=rs: e.dma_start(
                        out=w1blk[rs][:], in_=W1C[e_, mi]), "s_ld", 16)

                def emit_y_init(idx, g):
                    nonlocal i_ev
                    tt, n = idx // 2, idx % 2
                    p.wait("tensor", "s_dve", max(i_ev, o_ev))
                    p.wait("tensor", "s_cast", c_b2s)
                    c_mm = p.op("tensor", lambda e, g=g, tt=tt, n=n: e.matmul(
                        ps_oo[:], gatesT[0:NE, (g * 4 + tt) * 128:(g * 4 + tt + 1) * 128],
                        b2s[0:NE, n * 512:(n + 1) * 512], start=True, stop=True), "s_pe")
                    p.wait("vector", "s_pe", c_mm)
                    p.wait("vector", "s_st", y_spill_st)
                    i_ev = p.op("vector", lambda e, tt=tt, n=n: e.tensor_copy(
                        y_sb[:, tt * D + n * 512:tt * D + (n + 1) * 512], ps_oo[:]), "s_dve")
                    y_init[idx] = i_ev

                for g in range(NGRP):
                    p.wait("sync", "s_pe", xng_free)
                    xng_ld = p.op("sync", lambda e, g=g: e.dma_start(
                        out=xng[:].rearrange("p (c t) -> p c t", c=NC_),
                        in_=XNT.rearrange("(c p) t -> p c t", p=128)[:, :, g * GRP:(g + 1) * GRP]),
                        "s_ld", 16)
                    init_q = list(range(8))
                    for e_ in range(NE):
                        bb = e_ % 2
                        p.wait("sync", "s_pe", w2_free)
                        w2_ld = load_w2(e_)
                        p.wait("sync", "s_act", b1_free[bb])
                        b1_ld[bb] = p.op("sync", lambda e, e_=e_, bb=bb: e.dma_start(
                            out=b1f[bb][:], in_=B1F[e_]), "s_ld", 16)
                        load_w1(e_, 0)
                        load_w1(e_, 1)
                        for mi in range(32):
                            rs = mi % 2
                            hb = mi % 2
                            p.wait("tensor", "s_ld", w1_ld[rs])
                            if mi == 0:
                                p.wait("tensor", "s_ld", xng_ld)
                                p.wait("tensor", "s_ld", b1_ld[bb])
                            p.wait("tensor", "s_act", h_ev[hb])
                            c_mm = None
                            for c_ in range(NC_):
                                if c_ == NC_ - 1:
                                    c_mm = p.op("tensor", lambda e, mi=mi, c_=c_, rs=rs, hb=hb: e.matmul(
                                        ps_h[hb][:], w1blk[rs][:, c_ * 128:(c_ + 1) * 128],
                                        xng[:, c_ * GRP:(c_ + 1) * GRP],
                                        start=False, stop=True), "s_pe")
                                else:
                                    p.op("tensor", lambda e, mi=mi, c_=c_, rs=rs, hb=hb: e.matmul(
                                        ps_h[hb][:], w1blk[rs][:, c_ * 128:(c_ + 1) * 128],
                                        xng[:, c_ * GRP:(c_ + 1) * GRP],
                                        start=(c_ == 0), stop=False))
                            p.wait("scalar", "s_pe", c_mm)
                            h_ev[hb] = p.op("scalar", lambda e, mi=mi, hb=hb, bb=bb: e.activation(
                                h_sb[:, mi * GRP:(mi + 1) * GRP],
                                ps_h[hb][:], AF.Gelu, bias=b1f[bb][:, mi:mi + 1]), "s_act")
                            w1_free[rs] = c_mm
                            xng_free = c_mm
                            if e_ == 0 and mi % 4 == 0 and init_q:
                                emit_y_init(init_q.pop(0), g)
                            if mi + 2 < 32:
                                load_w1(e_, mi + 2)
                            yield 1.8
                        b1_free[bb] = h_ev[1]
                        c_h_all = p.c("s_act")
                        p.wait("tensor", "s_act", c_h_all)
                        p.wait("tensor", "s_ld", w2_ld)
                        for n2 in range(2):
                            for tt in range(4):
                                idx = tt * 2 + n2
                                p.wait("tensor", "s_dve", max(o_ev, i_ev))
                                c_mm = None
                                for ic in range(32):
                                    if ic == 31:
                                        c_mm = p.op("tensor", lambda e, n2=n2, tt=tt, ic=ic: e.matmul(
                                            ps_oo[:], h_sb[:, ic * GRP + tt * 128:ic * GRP + (tt + 1) * 128],
                                            w2[:, ic * D + n2 * 512:ic * D + (n2 + 1) * 512],
                                            start=False, stop=True), "s_pe")
                                    else:
                                        p.op("tensor", lambda e, n2=n2, tt=tt, ic=ic: e.matmul(
                                            ps_oo[:], h_sb[:, ic * GRP + tt * 128:ic * GRP + (tt + 1) * 128],
                                            w2[:, ic * D + n2 * 512:ic * D + (n2 + 1) * 512],
                                            start=(ic == 0), stop=False))
                                if n2 == 1 and tt == 3:
                                    w2_free = c_mm
                                p.wait("vector", "s_pe", c_mm)
                                if e_ == 0:
                                    p.wait("vector", "s_dve", y_init[idx])
                                else:
                                    p.wait("vector", "s_dve", y_last[idx])
                                gcol = (g * 4 + tt) * NE + e_
                                o_ev = y_last[idx] = p.op("vector", lambda e, n2=n2, tt=tt, gcol=gcol: e.scalar_tensor_tensor(
                                    y_sb[:, tt * D + n2 * 512:tt * D + (n2 + 1) * 512],
                                    ps_oo[:], gates[:, gcol:gcol + 1],
                                    y_sb[:, tt * D + n2 * 512:tt * D + (n2 + 1) * 512],
                                    op0=ALU.mult, op1=ALU.add), "s_dve")
                                yield 7.0
                    c_moe_done = p.c("s_dve")
                    p.wait("scalar", "s_dve", c_moe_done)
                    for tt in range(4):
                        y_spill_st = p.op("scalar", lambda e, g=g, tt=tt: e.dma_start(
                            out=YH[(g * 4 + tt) * 128:(g * 4 + tt + 1) * 128, :],
                            in_=y_sb[:, tt * D:(tt + 1) * D]), "s_st", 16)
                    c_y_stores[0] = y_spill_st
                    yield 0.5

            att = gen_att()
            moe = gen_moe()
            RATIO = 2.4
            cum_a = 0.0
            cum_m = 0.0
            a_alive = True
            m_alive = True
            while a_alive or m_alive:
                if a_alive and (not m_alive or cum_m >= cum_a * RATIO):
                    try:
                        cum_a += next(att)
                    except StopIteration:
                        a_alive = False
                else:
                    try:
                        cum_m += next(moe)
                    except StopIteration:
                        m_alive = False
            p.flush()

        with ExitStack() as ph:
            sb = lambda n, s, d: ph.enter_context(nc.sbuf_tensor(n, s, d))
            ps = lambda n, s, d: ph.enter_context(nc.psum_tensor(n, s, d))
            cat = sb("cat", [128, NC_ * T], BF16)
            wo = sb("wo", [128, NC_ * D], BF16)
            bor = sb("bor", [1, D], BF16)
            at_f = [sb(f"at_f{i}", [128, D], F32) for i in range(2)]
            x_t = [sb(f"x_t{i}", [128, D], F32) for i in range(2)]
            y_t = [sb(f"y_t{i}", [128, D], F32) for i in range(2)]
            o_t = [sb(f"o_t{i}", [128, D], F32) for i in range(2)]
            ps_o = [ps(f"ps_o{i}", [128, 512], F32) for i in range(2)]

            p.wait("sync", "s_st", p.c("s_st"))
            p.wait("gpsimd", "s_st", p.c("s_st"))
            p.op("gpsimd", lambda e: e.dma_start(
                out=wo[:].rearrange("p (c m) -> p c m", c=NC_),
                in_=WOT.rearrange("(c p) m -> p c m", p=128)), "s_cast", 16)
            c_bor = p.op("gpsimd", lambda e: e.dma_start(out=bor[:], in_=BO), "s_cast", 16)
            c_cat_ld = 0
            for half in range(2):
                c_cat_ld = p.op("sync", lambda e, half=half: e.dma_start(
                    out=cat[:].rearrange("p (c t) -> p c t", c=NC_)[:, half * 4:(half + 1) * 4, :],
                    in_=CATH.rearrange("(c p) t -> p c t", p=128)[:, half * 4:(half + 1) * 4, :]),
                    "s_ld", 16)
            at_ev = [0, 0]
            at_last = 0
            xy_free = [0, 0]
            out_st = [0, 0]
            p.wait("tensor", "s_ld", c_cat_ld)
            p.wait("tensor", "s_cast", c_bor)
            for tt in range(NTT):
                tb = tt % 2
                p.wait("sync", "s_dve", xy_free[tb])
                p.op("sync", lambda e, tt=tt, tb=tb: e.dma_start(
                    out=x_t[tb][:], in_=X[tt * 128:(tt + 1) * 128, :]), "s_ld", 16)
                c_xy = p.op("sync", lambda e, tt=tt, tb=tb: e.dma_start(
                    out=y_t[tb][:], in_=YH[tt * 128:(tt + 1) * 128, :]), "s_ld", 16)
                c_mm2 = [None, None]
                for n in range(2):
                    pb = n
                    p.wait("tensor", "s_dve", at_last)
                    p.op("tensor", lambda e, n=n, pb=pb: e.matmul(
                        ps_o[pb][:], oneb[:], bor[:, n * 512:(n + 1) * 512],
                        start=True, stop=False))
                    for c_ in range(NC_):
                        if c_ == NC_ - 1:
                            c_mm2[n] = p.op("tensor", lambda e, tt=tt, n=n, c_=c_, pb=pb: e.matmul(
                                ps_o[pb][:], cat[:, c_ * T + tt * 128:c_ * T + (tt + 1) * 128],
                                wo[:, c_ * D + n * 512:c_ * D + (n + 1) * 512],
                                start=False, stop=True), "s_pe")
                        else:
                            p.op("tensor", lambda e, tt=tt, n=n, c_=c_, pb=pb: e.matmul(
                                ps_o[pb][:], cat[:, c_ * T + tt * 128:c_ * T + (tt + 1) * 128],
                                wo[:, c_ * D + n * 512:c_ * D + (n + 1) * 512],
                                start=False, stop=False))
                p.wait("vector", "s_pe", c_mm2[1])
                p.op("vector", lambda e, tb=tb: e.tensor_copy(
                    at_f[tb][:, 0:512], ps_o[0][:]))
                c_ate = p.op("vector", lambda e, tb=tb: e.tensor_copy(
                    at_f[tb][:, 512:1024], ps_o[1][:]), "s_dve")
                at_ev[tb] = c_ate
                at_last = c_ate
                p.wait("vector", "s_ld", c_xy)
                p.wait("vector", "s_st", out_st[tb])
                c_s1 = p.op("vector", lambda e, tb=tb: e.tensor_tensor(
                    o_t[tb][:], x_t[tb][:], y_t[tb][:], op=ALU.add), "s_dve")
                p.wait("vector", "s_dve", c_s1)
                c_o = p.op("vector", lambda e, tb=tb: e.tensor_tensor(
                    o_t[tb][:], o_t[tb][:], at_f[tb][:], op=ALU.add), "s_dve")
                xy_free[tb] = c_o
                p.wait("scalar", "s_dve", c_o)
                out_st[tb] = p.op("scalar", lambda e, tt=tt, tb=tb: e.dma_start(
                    out=OUT[tt * 128:(tt + 1) * 128, :], in_=o_t[tb][:]), "s_st", 16)
            if _DEBUG_TAPS:
                p.wait("gpsimd", "s_st", p.c("s_st"))
                p.wait("gpsimd", "s_dve", p.c("s_dve"))
                for tname in _DEBUG_TAPS:
                    ap, shape = scratch_regs[tname]
                    dbg = nc.dram_tensor("dbg_" + tname, list(shape), F32,
                                         kind="ExternalOutput").ap()
                    p.op("gpsimd", lambda e, dbg=dbg, ap=ap: e.dma_start(
                        out=dbg, in_=ap), "s_cast", 16)
            p.wait("scalar", "s_st", p.c("s_st"))
            p.wait("gpsimd", "s_cast", p.c("s_cast"))
            p.flush()

    return nc


def _host_prep(L0_attn_W_q, L0_attn_b_q, L0_attn_W_k, L0_attn_b_k, L0_attn_W_v,
               L0_attn_W_o, L0_ffn_Wg, L0_ffn_W1, L0_ffn_b1, L0_ffn_W2):
    perm = []
    for h in range(NH):
        base = DH * h
        perm += [base + 2 * j for j in range(32)] + [base + 2 * j + 1 for j in range(32)]
    perm = np.array(perm)
    ac = np.ascontiguousarray
    Wq = np.asarray(L0_attn_W_q)
    Wk = np.asarray(L0_attn_W_k)
    common = {}
    common["wqt"] = ac(Wq.T[:, perm])
    common["wkt"] = ac(Wk.T[:, perm])
    common["wvt"] = ac(np.asarray(L0_attn_W_v).T)
    common["wot"] = ac(np.asarray(L0_attn_W_o).T)
    common["wgt"] = ac(np.asarray(L0_ffn_Wg).T)
    common["bq"] = ac(np.asarray(L0_attn_b_q)[perm].reshape(NC_, 128).T)
    common["bk"] = ac(np.asarray(L0_attn_b_k)[perm].reshape(NC_, 128).T)
    common["w1p"] = ac(np.asarray(L0_ffn_W1).reshape(NE, NC_, 128, 32, 128)
                       .transpose(0, 3, 2, 1, 4).reshape(NE, 32, 128, NC_ * 128))
    common["w2p"] = ac(np.asarray(L0_ffn_W2).reshape(NE, 32, 128, D)
                       .transpose(0, 2, 1, 3).reshape(NE, 128, 32 * D))
    common["b1f"] = ac(np.asarray(L0_ffn_b1).reshape(NE, 32, 128).transpose(0, 2, 1))
    j = np.arange(32, dtype=np.float64)
    theta = 10000.0 ** (-j / 32.0)
    t = np.arange(T, dtype=np.float64)
    ang = t[None, :] * theta[:, None]
    cosv = np.cos(ang)
    sinv = np.sin(ang)
    ccr = np.tile(cosv, (4, 1))
    ssr = np.concatenate([-sinv, sinv, -sinv, sinv], axis=0)
    common["cc"] = ccr.astype(BF)
    common["ss"] = ssr.astype(BF)
    psw = np.zeros((128, 128), dtype=BF)
    for pp in range(128):
        jj = pp // 32
        swp = pp + 32 if jj % 2 == 0 else pp - 32
        psw[swp, pp] = 1.0
    common["psw"] = psw
    common["idb"] = np.eye(128, dtype=BF)
    common["idf"] = np.eye(128, dtype=np.float32)
    common["oneb"] = np.ones((1, 128), dtype=BF)
    common["onef"] = np.ones((1, 128), dtype=np.float32)
    m = np.zeros((128, 4 * 512), dtype=np.float32)
    qq = np.arange(128)[:, None]
    kk = np.arange(512)[None, :]
    for r in range(4):
        m[:, r * 512:(r + 1) * 512] = np.where(kk <= 128 * r + qq, 0.0, NEG)
    common["mask"] = m.astype(BF)
    return common


def kernel(L0_attn_x, L0_attn_W_q, L0_attn_b_q, L0_attn_W_k, L0_attn_b_k,
           L0_attn_W_v, L0_attn_b_v, L0_attn_W_o, L0_attn_b_o,
           L0_ffn_Wg, L0_ffn_W1, L0_ffn_b1, L0_ffn_W2, L0_ffn_b2):
    common = _host_prep(L0_attn_W_q, L0_attn_b_q, L0_attn_W_k, L0_attn_b_k,
                        L0_attn_W_v, L0_attn_W_o, L0_ffn_Wg, L0_ffn_W1,
                        L0_ffn_b1, L0_ffn_W2)
    common["bv"] = np.ascontiguousarray(np.asarray(L0_attn_b_v, dtype=np.float32)[None, :])
    common["bo"] = np.ascontiguousarray(np.asarray(L0_attn_b_o, dtype=np.float32)[None, :])
    common["b2"] = np.ascontiguousarray(np.asarray(L0_ffn_b2, dtype=np.float32))
    common = {k: (np.asarray(v, dtype=np.float32) if np.asarray(v).dtype == np.float64 else np.asarray(v))
              for k, v in common.items()}
    x = np.asarray(L0_attn_x, dtype=np.float32)
    in_maps = []
    for c in range(8):
        m = dict(common)
        m["x"] = np.ascontiguousarray(x[c])
        in_maps.append(m)
    nc = build_program()
    res = run_bass_kernel_spmd(nc, in_maps, core_ids=list(range(8)))
    out = np.stack([np.asarray(res.results[c]["out"]) for c in range(8)], axis=0)
    return out


# version 16
# speedup vs baseline: 1.0029x; 1.0029x over previous
import numpy as np
import ml_dtypes
from contextlib import ExitStack
import concourse.bass as bass
import concourse.mybir as mybir
from concourse.bass_utils import run_bass_kernel_spmd

F32 = mybir.dt.float32
BF16 = mybir.dt.bfloat16
AF = mybir.ActivationFunctionType
ALU = mybir.AluOpType
AX = mybir.AxisListType
BF = ml_dtypes.bfloat16

D = 1024
T = 4096
NH = 16
DH = 64
NE = 8
INTER = 4096
NC_ = 8
NTT = 32
NGRP = 8
GRP = 512
NEG = -1.0e9

_DEBUG_TAPS = []
_VERBOSE = False
_DISABLE_ATT = False
_DISABLE_MOE = False


class Prog:
    ENGINES = ("sync", "tensor", "vector", "scalar", "gpsimd")

    def __init__(self, nc):
        self.nc = nc
        self.streams = {e: [] for e in self.ENGINES}
        self.sems = {}
        self.cnt = {}
        self.last_wait = {}

    def sem(self, name, stack):
        h = stack.enter_context(self.nc.semaphore(name))
        self.sems[name] = h
        self.cnt[name] = 0

    def wait(self, eng, sem, val=None):
        v = self.cnt[sem] if val is None else val
        if v <= 0:
            return
        key = (eng, sem)
        if self.last_wait.get(key, -1) >= v:
            return
        self.last_wait[key] = v
        h = self.sems[sem]
        self.streams[eng].append(lambda e, h=h, v=v: e.wait_ge(h, v))

    def op(self, eng, build, inc=None, by=1):
        if inc is None:
            self.streams[eng].append(lambda e, b=build: b(e))
            return None
        h = self.sems[inc]
        self.cnt[inc] += by
        self.streams[eng].append(lambda e, b=build, h=h, by=by: b(e).then_inc(h, by))
        return self.cnt[inc]

    def c(self, sem):
        return self.cnt[sem]

    def flush(self):
        nc = self.nc
        streams = self.streams
        self.streams = {e: [] for e in self.ENGINES}
        with nc.Block() as block:
            if streams["sync"]:
                @block.sync
                def _(e, ops=streams["sync"]):
                    for f in ops:
                        f(e)
            if streams["tensor"]:
                @block.tensor
                def _(e, ops=streams["tensor"]):
                    for f in ops:
                        f(e)
            if streams["vector"]:
                @block.vector
                def _(e, ops=streams["vector"]):
                    for f in ops:
                        f(e)
            if streams["scalar"]:
                @block.scalar
                def _(e, ops=streams["scalar"]):
                    for f in ops:
                        f(e)
            if streams["gpsimd"]:
                @block.gpsimd
                def _(e, ops=streams["gpsimd"]):
                    for f in ops:
                        f(e)


def build_program():
    nc = bass.Bass("TRN2", target_bir_lowering=False)

    scratch_regs = {}

    def dram(name, shape, dt, kind):
        ap = nc.dram_tensor(name, shape, dt, kind=kind).ap()
        if kind == "Internal":
            scratch_regs[name] = (ap, shape)
        return ap

    X = dram("x", [T, D], F32, "ExternalInput")
    WQT = dram("wqt", [D, D], F32, "ExternalInput")
    WKT = dram("wkt", [D, D], F32, "ExternalInput")
    WVT = dram("wvt", [D, D], F32, "ExternalInput")
    WOT = dram("wot", [D, D], F32, "ExternalInput")
    WGT = dram("wgt", [D, NE], F32, "ExternalInput")
    BQ = dram("bq", [128, NC_], F32, "ExternalInput")
    BK = dram("bk", [128, NC_], F32, "ExternalInput")
    BV = dram("bv", [1, D], F32, "ExternalInput")
    BO = dram("bo", [1, D], F32, "ExternalInput")
    W1P = dram("w1p", [NE, 32, 128, NC_ * 128], F32, "ExternalInput")
    W2P = dram("w2p", [NE, 128, 32 * D], F32, "ExternalInput")
    B1F = dram("b1f", [NE, 128, 32], F32, "ExternalInput")
    B2 = dram("b2", [NE, D], F32, "ExternalInput")
    CC = dram("cc", [128, T], BF16, "ExternalInput")
    SS = dram("ss", [128, T], BF16, "ExternalInput")
    PSW = dram("psw", [128, 128], BF16, "ExternalInput")
    IDB = dram("idb", [128, 128], BF16, "ExternalInput")
    IDF = dram("idf", [128, 128], F32, "ExternalInput")
    ONEB = dram("oneb", [1, 128], BF16, "ExternalInput")
    ONEF = dram("onef", [1, 128], F32, "ExternalInput")
    MASK = dram("mask", [128, 4 * 512], BF16, "ExternalInput")

    OUT = dram("out", [T, D], F32, "ExternalOutput")

    XNT = dram("xnt_h", [D, T], BF16, "Internal")
    QT = dram("qt_h", [D, T], BF16, "Internal")
    KT = dram("kt_h", [D, T], BF16, "Internal")
    VH = dram("v_h", [T, D], BF16, "Internal")
    CATH = dram("cat_h", [D, T], BF16, "Internal")
    ATTH = dram("att_h", [T, D], F32, "Internal")
    YH = dram("y_h", [T, D], F32, "Internal")
    W1C = dram("w1c", [NE, 32, 128, NC_ * 128], BF16, "Internal")
    W2C = dram("w2c", [NE, 128, 32 * D], BF16, "Internal")

    with ExitStack() as outer:
        p = Prog(nc)
        for s in ["s_ld", "s_cast", "s_st", "s_pe", "s_act", "s_dve", "s_pl"]:
            p.sem(s, outer)

        idb = outer.enter_context(nc.sbuf_tensor("idb_sb", [128, 128], BF16))
        idf = outer.enter_context(nc.sbuf_tensor("idf_sb", [128, 128], F32))
        oneb = outer.enter_context(nc.sbuf_tensor("oneb_sb", [1, 128], BF16))
        onef = outer.enter_context(nc.sbuf_tensor("onef_sb", [1, 128], F32))
        gates = outer.enter_context(nc.sbuf_tensor("gates_sb", [128, NTT * NE], F32))
        gatesT = outer.enter_context(nc.sbuf_tensor("gatesT_sb", [NE, T], BF16))

        if _VERBOSE:
            print("P01 sbuf_remaining", nc.sbuf_bytes_remaining)
        with ExitStack() as ph:
            sb = lambda n, s, d: ph.enter_context(nc.sbuf_tensor(n, s, d))
            ps = lambda n, s, d: ph.enter_context(nc.psum_tensor(n, s, d))
            x_b = [sb(f"x_b{i}", [128, D], F32) for i in range(2)]
            sq = [sb(f"sq{i}", [128, D], F32) for i in range(2)]
            s2 = [sb(f"s2_{i}", [128, 1], F32) for i in range(2)]
            nsum = [sb(f"nsum{i}", [128, 1], F32) for i in range(2)]
            m_neg = [sb(f"mneg{i}", [128, 1], F32) for i in range(2)]
            m2t = [sb(f"m2t{i}", [128, 1], F32) for i in range(2)]
            var = [sb(f"var{i}", [128, 1], F32) for i in range(2)]
            rstd = [sb(f"rstd{i}", [128, 1], F32) for i in range(2)]
            xn_b = [sb(f"xn_b{i}", [128, D], BF16) for i in range(2)]
            xnT = sb("xnT", [128, NC_ * T], BF16)
            wq = sb("wq", [128, NC_ * D], BF16)
            wk = sb("wk", [128, NC_ * D], BF16)
            wv = sb("wv", [128, NC_ * D], BF16)
            wg = sb("wg", [128, NC_ * NE], BF16)
            bqf = sb("bqf", [128, NC_], F32)
            bkf = sb("bkf", [128, NC_], F32)
            bvr = sb("bvr", [1, D], BF16)
            cc = sb("cc_sb", [128, T], BF16)
            ss = sb("ss_sb", [128, T], BF16)
            psw = sb("psw_sb", [128, 128], BF16)
            q_sb = [sb(f"q_sb{i}", [128, 512], BF16) for i in range(2)]
            t1b = [sb(f"t1b{i}", [128, 512], BF16) for i in range(2)]
            t2b = [sb(f"t2b{i}", [128, 512], BF16) for i in range(2)]
            rope_b = [sb(f"rope{i}", [128, 512], BF16) for i in range(2)]
            v_b = [sb(f"v_b{i}", [128, 512], BF16) for i in range(2)]
            logT = sb("logT", [NE, T], F32)
            lt_b = sb("lt_b", [128, NE], F32)
            g_m1 = sb("g_m1", [128, 1], F32)
            g_m2 = sb("g_m2", [128, 1], F32)
            g_d12 = sb("g_d12", [128, 1], F32)
            g_w1 = sb("g_w1", [128, 1], F32)
            g_w2 = sb("g_w2", [128, 1], F32)
            g_is1 = sb("g_is1", [128, NE], F32)
            g_is2 = sb("g_is2", [128, NE], F32)
            g_msk = sb("g_msk", [128, NE], F32)
            g_g1 = sb("g_g1", [128, NE], F32)
            gt_b = sb("gt_b", [128, NE], F32)
            ps_tr = [ps(f"ps_tr{i}", [128, 1024], BF16) for i in range(2)]
            ps_pr = [ps(f"ps_pr{i}", [128, 512], F32) for i in range(2)]
            ps_sw = [ps(f"ps_sw{i}", [128, 512], F32) for i in range(2)]
            ps_g = ps("ps_g", [128, 512], F32)

            p.op("sync", lambda e: e.dma_start(out=idb[:], in_=IDB), "s_ld", 16)
            p.op("sync", lambda e: e.dma_start(out=idf[:], in_=IDF), "s_ld", 16)
            p.op("sync", lambda e: e.dma_start(out=oneb[:], in_=ONEB), "s_ld", 16)
            p.op("sync", lambda e: e.dma_start(out=onef[:], in_=ONEF), "s_ld", 16)
            p.op("sync", lambda e: e.dma_start(out=cc[:], in_=CC), "s_ld", 16)
            p.op("sync", lambda e: e.dma_start(out=ss[:], in_=SS), "s_ld", 16)
            p.op("sync", lambda e: e.dma_start(out=psw[:], in_=PSW), "s_ld", 16)
            p.op("sync", lambda e: e.dma_start(out=bqf[:], in_=BQ), "s_ld", 16)
            c_tabs = p.op("sync", lambda e: e.dma_start(out=bkf[:], in_=BK), "s_ld", 16)
            for w_sb_, W_ in [(wq, WQT), (wk, WKT), (wv, WVT)]:
                p.op("gpsimd", lambda e, o=w_sb_, i=W_: e.dma_start(
                    out=o[:].rearrange("p (c m) -> p c m", c=NC_),
                    in_=i.rearrange("(c p) m -> p c m", p=128)), "s_cast", 16)
            p.op("gpsimd", lambda e: e.dma_start(
                out=wg[:].rearrange("p (c m) -> p c m", c=NC_),
                in_=WGT.rearrange("(c p) m -> p c m", p=128)), "s_cast", 16)
            c_wproj = p.op("gpsimd", lambda e: e.dma_start(out=bvr[:], in_=BV), "s_cast", 16)
            for e_ in range(NE):
                for q4 in range(4):
                    sl = slice(q4 * 8, q4 * 8 + 8)
                    p.op("gpsimd", lambda e, e_=e_, sl=sl: e.dma_start(
                        out=W1C[e_, sl], in_=W1P[e_, sl]), "s_cast", 16)
                    sl2 = slice(q4 * 32, q4 * 32 + 32)
                    p.op("gpsimd", lambda e, e_=e_, sl2=sl2: e.dma_start(
                        out=W2C[e_, sl2], in_=W2P[e_, sl2]), "s_cast", 16)
            c_wcache = p.c("s_cast")

            ln_done = [0, 0]
            tr_done = [0, 0]
            xn_done = [0, 0]
            for t in range(NTT):
                b = t % 2
                p.wait("sync", "s_dve", ln_done[b])
                c_x = p.op("sync", lambda e, t=t, b=b: e.dma_start(
                    out=x_b[b][:], in_=X[t * 128:(t + 1) * 128, :]), "s_ld", 16)
                p.wait("scalar", "s_ld", c_x)
                p.wait("scalar", "s_dve", ln_done[b])
                c_sq = p.op("scalar", lambda e, b=b: e.activation(
                    sq[b][:], x_b[b][:], AF.Square, accum_out=s2[b][:]), "s_act")
                p.wait("vector", "s_ld", c_x)
                c_ns = p.op("vector", lambda e, b=b: e.tensor_reduce(
                    nsum[b][:], x_b[b][:], axis=AX.X, op=ALU.add, negate=True), "s_dve")
                p.wait("vector", "s_dve", c_ns)
                p.op("vector", lambda e, b=b: e.tensor_scalar(
                    m_neg[b][:], nsum[b][:], 1.0 / D, None, op0=ALU.mult))
                c_mn = p.op("vector", lambda e, b=b: e.scalar_tensor_tensor(
                    m2t[b][:], nsum[b][:], 1.0 / (D * D), nsum[b][:],
                    op0=ALU.mult, op1=ALU.mult), "s_dve")
                p.wait("vector", "s_act", c_sq)
                p.wait("vector", "s_dve", c_mn)
                c_va = p.op("vector", lambda e, b=b: e.tensor_scalar(
                    var[b][:], s2[b][:], 1.0 / D, 1e-5, op0=ALU.mult, op1=ALU.add), "s_dve")
                p.wait("vector", "s_dve", c_va)
                c_var = p.op("vector", lambda e, b=b: e.tensor_tensor(
                    var[b][:], var[b][:], m2t[b][:], op=ALU.subtract), "s_dve")
                p.wait("scalar", "s_dve", c_var)
                c_sqr = p.op("scalar", lambda e, b=b: e.activation(
                    rstd[b][:], var[b][:], AF.Sqrt), "s_act")
                p.wait("vector", "s_act", c_sqr)
                c_rs = p.op("vector", lambda e, b=b: e.reciprocal(
                    rstd[b][:], rstd[b][:]), "s_dve")
                p.wait("vector", "s_dve", c_rs)
                p.wait("vector", "s_pe", xn_done[b])
                c_xn = p.op("vector", lambda e, b=b: e.tensor_scalar(
                    xn_b[b][:], x_b[b][:], m_neg[b][:], rstd[b][:],
                    op0=ALU.add, op1=ALU.mult), "s_dve")
                ln_done[b] = c_xn
                p.wait("tensor", "s_dve", c_xn)
                p.wait("tensor", "s_ld", c_tabs)
                for c_ in range(NC_):
                    tb = c_ % 2
                    p.wait("tensor", "s_dve", tr_done[tb])
                    c_tr = p.op("tensor", lambda e, b=b, c_=c_, tb=tb: e.transpose(
                        ps_tr[tb][:, 0:128], xn_b[b][:, c_ * 128:(c_ + 1) * 128],
                        idb[:]), "s_pe")
                    if c_ == NC_ - 1:
                        xn_done[b] = c_tr
                    p.wait("vector", "s_pe", c_tr)
                    tr_done[tb] = p.op("vector", lambda e, t=t, c_=c_, tb=tb: e.tensor_copy(
                        xnT[:, c_ * T + t * 128: c_ * T + (t + 1) * 128],
                        ps_tr[tb][:, 0:128]), "s_dve")
            c_xnT_all = p.c("s_dve")

            pr_done = [0, 0]
            sw_done = [0, 0]
            qs_done = [0, 0]
            rope_st = [0, 0]
            p.wait("tensor", "s_cast", c_wproj)
            p.wait("tensor", "s_dve", c_xnT_all)
            for (w_sb, b_fm, OUT_T) in [(wq, bqf, QT), (wk, bkf, KT)]:
                for m in range(NC_):
                    for g in range(NC_):
                        pb = (m * NC_ + g) % 2
                        p.wait("tensor", "s_dve", pr_done[pb])
                        c_mm = None
                        for c_ in range(NC_):
                            if c_ == NC_ - 1:
                                c_mm = p.op("tensor", lambda e, w_sb=w_sb, m=m, g=g, c_=c_, pb=pb: e.matmul(
                                    ps_pr[pb][:], w_sb[:, c_ * D + m * 128:c_ * D + (m + 1) * 128],
                                    xnT[:, c_ * T + g * 512:c_ * T + (g + 1) * 512],
                                    start=False, stop=True), "s_pe")
                            else:
                                p.op("tensor", lambda e, w_sb=w_sb, m=m, g=g, c_=c_, pb=pb: e.matmul(
                                    ps_pr[pb][:], w_sb[:, c_ * D + m * 128:c_ * D + (m + 1) * 128],
                                    xnT[:, c_ * T + g * 512:c_ * T + (g + 1) * 512],
                                    start=(c_ == 0), stop=False))
                        p.wait("vector", "s_pe", c_mm)
                        p.wait("vector", "s_pe", qs_done[pb])
                        p.wait("vector", "s_dve", sw_done[pb])
                        c_ev = p.op("vector", lambda e, b_fm=b_fm, m=m, pb=pb: e.tensor_scalar(
                            q_sb[pb][:], ps_pr[pb][:], b_fm[:, m:m + 1], None,
                            op0=ALU.add), "s_dve")
                        pr_done[pb] = c_ev
                        p.wait("tensor", "s_dve", c_ev)
                        p.wait("tensor", "s_dve", sw_done[pb])
                        c_sw = p.op("tensor", lambda e, pb=pb: e.matmul(
                            ps_sw[pb][:], psw[:], q_sb[pb][:], start=True, stop=True), "s_pe")
                        qs_done[pb] = c_sw
                        p.wait("vector", "s_dve", c_ev)
                        p.op("vector", lambda e, g=g, pb=pb: e.tensor_tensor(
                            t1b[pb][:], q_sb[pb][:], cc[:, g * 512:(g + 1) * 512], op=ALU.mult))
                        p.wait("vector", "s_pe", c_sw)
                        c_t2 = p.op("vector", lambda e, g=g, pb=pb: e.tensor_tensor(
                            t2b[pb][:], ps_sw[pb][:], ss[:, g * 512:(g + 1) * 512], op=ALU.mult), "s_dve")
                        sw_done[pb] = c_t2
                        p.wait("vector", "s_dve", c_t2)
                        p.wait("vector", "s_st", rope_st[pb])
                        c_rp = p.op("vector", lambda e, pb=pb: e.tensor_tensor(
                            rope_b[pb][:], t1b[pb][:], t2b[pb][:], op=ALU.add), "s_dve")
                        p.wait("scalar", "s_dve", c_rp)
                        rope_st[pb] = p.op("scalar", lambda e, OUT_T=OUT_T, m=m, g=g, pb=pb: e.dma_start(
                            out=OUT_T[m * 128:(m + 1) * 128, g * 512:(g + 1) * 512],
                            in_=rope_b[pb][:]), "s_st", 16)

            v_ev_done = [0, 0]
            v_st = [0, 0]
            for tt in range(NTT):
                for n in range(2):
                    pb = (tt * 2 + n) % 2
                    p.wait("tensor", "s_dve", v_ev_done[pb])
                    p.wait("tensor", "s_dve", pr_done[pb])
                    p.op("tensor", lambda e, n=n, pb=pb: e.matmul(
                        ps_pr[pb][:], oneb[:], bvr[:, n * 512:(n + 1) * 512],
                        start=True, stop=False))
                    for c_ in range(NC_):
                        if c_ == NC_ - 1:
                            c_mm = p.op("tensor", lambda e, tt=tt, n=n, c_=c_, pb=pb: e.matmul(
                                ps_pr[pb][:], xnT[:, c_ * T + tt * 128:c_ * T + (tt + 1) * 128],
                                wv[:, c_ * D + n * 512:c_ * D + (n + 1) * 512],
                                start=False, stop=True), "s_pe")
                        else:
                            p.op("tensor", lambda e, tt=tt, n=n, c_=c_, pb=pb: e.matmul(
                                ps_pr[pb][:], xnT[:, c_ * T + tt * 128:c_ * T + (tt + 1) * 128],
                                wv[:, c_ * D + n * 512:c_ * D + (n + 1) * 512],
                                start=False, stop=False))
                    p.wait("vector", "s_pe", c_mm)
                    p.wait("vector", "s_st", v_st[pb])
                    c_ev = p.op("vector", lambda e, pb=pb: e.tensor_copy(
                        v_b[pb][:], ps_pr[pb][:]), "s_dve")
                    v_ev_done[pb] = c_ev
                    p.wait("scalar", "s_dve", c_ev)
                    v_st[pb] = p.op("scalar", lambda e, tt=tt, n=n, pb=pb: e.dma_start(
                        out=VH[tt * 128:(tt + 1) * 128, n * 512:(n + 1) * 512],
                        in_=v_b[pb][:]), "s_st", 16)

            c_lg = 0
            for g in range(NC_):
                p.wait("tensor", "s_dve", c_lg)
                for c_ in range(NC_):
                    if c_ == NC_ - 1:
                        c_mm = p.op("tensor", lambda e, g=g, c_=c_: e.matmul(
                            ps_g[0:NE, 0:512], wg[:, c_ * NE:(c_ + 1) * NE],
                            xnT[:, c_ * T + g * 512:c_ * T + (g + 1) * 512],
                            start=False, stop=True), "s_pe")
                    else:
                        p.op("tensor", lambda e, g=g, c_=c_: e.matmul(
                            ps_g[0:NE, 0:512], wg[:, c_ * NE:(c_ + 1) * NE],
                            xnT[:, c_ * T + g * 512:c_ * T + (g + 1) * 512],
                            start=(c_ == 0), stop=False))
                p.wait("vector", "s_pe", c_mm)
                c_lg = p.op("vector", lambda e, g=g: e.tensor_copy(
                    logT[0:NE, g * 512:(g + 1) * 512], ps_g[0:NE, 0:512]), "s_dve")
            for tt in range(NTT):
                p.wait("tensor", "s_dve", c_lg)
                c_tr = p.op("tensor", lambda e, tt=tt: e.transpose(
                    ps_g[:, 0:NE], logT[0:NE, tt * 128:(tt + 1) * 128],
                    idf[0:NE, 0:NE]), "s_pe")
                p.wait("vector", "s_pe", c_tr)
                c_lt = p.op("vector", lambda e: e.tensor_copy(
                    lt_b[:], ps_g[:, 0:NE]), "s_dve")
                p.wait("vector", "s_dve", c_lt)
                c_m1 = p.op("vector", lambda e: e.tensor_reduce(
                    g_m1[:], lt_b[:], axis=AX.X, op=ALU.max), "s_dve")
                p.wait("vector", "s_dve", c_m1)
                c_is1 = p.op("vector", lambda e: e.tensor_scalar(
                    g_is1[:], lt_b[:], g_m1[:], None, op0=ALU.is_ge), "s_dve")
                p.wait("vector", "s_dve", c_is1)
                c_msk = p.op("vector", lambda e: e.scalar_tensor_tensor(
                    g_msk[:], g_is1[:], NEG, lt_b[:], op0=ALU.mult, op1=ALU.add), "s_dve")
                p.wait("vector", "s_dve", c_msk)
                c_m2 = p.op("vector", lambda e: e.tensor_reduce(
                    g_m2[:], g_msk[:], axis=AX.X, op=ALU.max), "s_dve")
                p.wait("vector", "s_dve", c_m2)
                c_d = p.op("vector", lambda e: e.tensor_tensor(
                    g_d12[:], g_m1[:], g_m2[:], op=ALU.subtract), "s_dve")
                p.wait("scalar", "s_dve", c_d)
                p.op("scalar", lambda e: e.activation(g_w1[:], g_d12[:], AF.Sigmoid))
                c_w2 = p.op("scalar", lambda e: e.activation(
                    g_w2[:], g_d12[:], AF.Sigmoid, scale=-1.0), "s_act")
                c_is2 = p.op("vector", lambda e: e.tensor_scalar(
                    g_is2[:], g_msk[:], g_m2[:], None, op0=ALU.is_ge), "s_dve")
                p.wait("vector", "s_act", c_w2)
                p.wait("vector", "s_dve", c_is2)
                c_g1 = p.op("vector", lambda e: e.tensor_scalar(
                    g_g1[:], g_is1[:], g_w1[:], None, op0=ALU.mult), "s_dve")
                p.wait("vector", "s_dve", c_g1)
                c_gt = p.op("vector", lambda e, tt=tt: e.scalar_tensor_tensor(
                    gates[:, tt * NE:(tt + 1) * NE], g_is2[:], g_w2[:], g_g1[:],
                    op0=ALU.mult, op1=ALU.add), "s_dve")
                p.wait("vector", "s_dve", c_gt)
                c_gtb = p.op("vector", lambda e, tt=tt: e.tensor_copy(
                    gt_b[:], gates[:, tt * NE:(tt + 1) * NE]), "s_dve")
                p.wait("tensor", "s_dve", c_gtb)
                c_tr2 = p.op("tensor", lambda e: e.transpose(
                    ps_g[0:NE, 0:128], gt_b[:], idf[:]), "s_pe")
                p.wait("vector", "s_pe", c_tr2)
                c_lg = p.op("vector", lambda e, tt=tt: e.tensor_copy(
                    gatesT[0:NE, tt * 128:(tt + 1) * 128], ps_g[0:NE, 0:128]), "s_dve")

            p.wait("scalar", "s_dve", c_xnT_all)
            c_xnt_sp = 0
            for half in range(2):
                c_xnt_sp = p.op("scalar", lambda e, half=half: e.dma_start(
                    out=XNT.rearrange("(c p) t -> p c t", p=128)[:, half * 4:(half + 1) * 4, :],
                    in_=xnT[:].rearrange("p (c t) -> p c t", c=NC_)[:, half * 4:(half + 1) * 4, :]),
                    "s_st", 16)
            p.flush()

        if _VERBOSE:
            print("P2 sbuf_remaining", nc.sbuf_bytes_remaining)
        with ExitStack() as ph:
            sb = lambda n, s, d: ph.enter_context(nc.sbuf_tensor(n, s, d))
            ps = lambda n, s, d: ph.enter_context(nc.psum_tensor(n, s, d))
            qp = sb("qp", [128, T], BF16)
            kp = sb("kp", [128, T], BF16)
            vp = sb("vp", [128, NTT * 128], BF16)
            pbA = sb("pbA", [128, T], BF16)
            pbB = sb("pbB", [128, T], BF16)
            pT = sb("pT", [128, NTT * 128], BF16)
            mask = sb("mask_sb", [128, 4 * 512], BF16)
            sdA = sb("sdA", [128, 512], F32)
            sdB = sb("sdB", [128, 512], F32)
            mpA = sb("mpA", [128, 8], F32)
            mpB = sb("mpB", [128, 8], F32)
            mj8A = sb("mj8A", [128, 8], F32)
            mj8B = sb("mj8B", [128, 8], F32)
            nmA = sb("nmA", [128, 1], F32)
            nmB = sb("nmB", [128, 1], F32)
            nm8A = sb("nm8A", [128, 1], F32)
            nm8B = sb("nm8B", [128, 1], F32)
            csA = sb("csA", [128, 8], F32)
            csB = sb("csB", [128, 8], F32)
            lpA = sb("lpA", [128, 8], F32)
            lpB = sb("lpB", [128, 8], F32)
            lwA = sb("lwA", [128, 8], F32)
            lwB = sb("lwB", [128, 8], F32)
            lA2 = sb("lA2", [128, 1], F32)
            lB2 = sb("lB2", [128, 1], F32)
            li_rA = sb("li_rA", [1, 128], F32)
            li_rB = sb("li_rB", [1, 128], F32)
            bc_sb = sb("bc_sb", [128, 128], F32)
            cat_b = [sb(f"cat_b{i}", [128, 128], BF16) for i in range(2)]
            w2 = sb("w2", [128, 32 * D], BF16)
            w1blk = [sb(f"w1blk{i}", [128, 4 * NC_ * 128], BF16) for i in range(2)]
            xng = sb("xng", [128, NC_ * GRP], BF16)
            h_sb = sb("h_sb", [128, 32 * GRP], BF16)
            y_sb = sb("y_sb", [128, 4 * D], F32)
            b1f = [sb(f"b1f{i}", [128, 32], F32) for i in range(2)]
            b2s = sb("b2s", [NE, D], BF16)
            ps_sA = ps("ps_sA", [128, 512], F32)
            ps_sB = ps("ps_sB", [128, 512], F32)
            ps_t = [ps(f"ps_t{i}", [128, 1024], BF16) for i in range(2)]
            ps_pv = ps("ps_pv", [128, 512], F32)
            ps_h = [ps(f"ps_h{i}", [128, 512], F32) for i in range(2)]
            ps_oo = ps("ps_oo", [128, 512], F32)

            p.wait("sync", "s_st", p.c("s_st"))
            p.wait("gpsimd", "s_st", p.c("s_st"))
            p.wait("sync", "s_cast", c_wcache)
            c_mask = p.op("sync", lambda e: e.dma_start(out=mask[:], in_=MASK), "s_ld", 16)
            c_b2s = p.op("gpsimd", lambda e: e.dma_start(out=b2s[:], in_=B2), "s_cast", 16)
            c_cat_stores = [0]
            c_y_stores = [0]

            def gen_att():
                sc_rel = ("s_act", 0)
                last_exp = 0
                pb_used = 0
                mp_free = 0
                l_pe = 0
                li_pe = 0
                tr_ev = [("s_dve", 0), ("s_act", 0)]
                pv_ev = 0
                pair_free = 0
                cat_st = [0, 0]
                lw_done = 0
                for pr in range(NH // 2):
                    p.wait("sync", "s_pe", pair_free)
                    p.op("sync", lambda e, pr=pr: e.dma_start(
                        out=qp[:], in_=QT[pr * 128:(pr + 1) * 128, :]), "s_ld", 16)
                    p.op("sync", lambda e, pr=pr: e.dma_start(
                        out=kp[:], in_=KT[pr * 128:(pr + 1) * 128, :]), "s_ld", 16)
                    c_pld = None
                    for half in range(2):
                        c_pld = p.op("sync", lambda e, pr=pr, half=half: e.dma_start(
                            out=vp[:].rearrange("p (tt d) -> p tt d", tt=NTT)[:, half * 16:(half + 1) * 16, :],
                            in_=VH.rearrange("(tt p) d -> p tt d", p=128)[:, half * 16:(half + 1) * 16,
                                pr * 128:(pr + 1) * 128]), "s_ld", 16)
                    for i in range(NTT):
                        jd = i // 4
                        r = i % 4
                        ktiles = i + 1
                        nch = jd + 1
                        for j in range(nch):
                            diag = (j == jd)
                            p.wait("tensor", "s_ld", c_pld)
                            sem_r, cnt_r = sc_rel
                            p.wait("tensor", sem_r, cnt_r)
                            p.op("tensor", lambda e, i=i, j=j: e.matmul(
                                ps_sA[:], qp[0:64, i * 128:(i + 1) * 128],
                                kp[0:64, j * 512:(j + 1) * 512],
                                start=True, stop=True, tile_position=(0, 0)))
                            c_s = p.op("tensor", lambda e, i=i, j=j: e.matmul(
                                ps_sB[:], qp[64:128, i * 128:(i + 1) * 128],
                                kp[64:128, j * 512:(j + 1) * 512],
                                start=True, stop=True, tile_position=(64, 0)), "s_pe")
                            p.wait("vector", "s_pe", c_s)
                            p.wait("vector", "s_ld", c_mask)
                            if j == 0:
                                p.wait("vector", "s_act", mp_free)
                            if diag:
                                p.wait("vector", "s_act", last_exp)
                                p.op("vector", lambda e, r=r: e.tensor_tensor(
                                    sdA[:], ps_sA[:], mask[:, r * 512:(r + 1) * 512], op=ALU.add))
                                c_dg = p.op("vector", lambda e, r=r: e.tensor_tensor(
                                    sdB[:], ps_sB[:], mask[:, r * 512:(r + 1) * 512], op=ALU.add), "s_dve")
                                sc_rel = ("s_dve", c_dg)
                                p.wait("vector", "s_dve", c_dg)
                                srcA, srcB = sdA, sdB
                            else:
                                srcA, srcB = ps_sA, ps_sB
                            p.op("vector", lambda e, j=j, srcA=srcA: e.tensor_reduce(
                                mpA[:, j:j + 1], srcA[:], axis=AX.X, op=ALU.max))
                            c_mp = p.op("vector", lambda e, j=j, srcB=srcB: e.tensor_reduce(
                                mpB[:, j:j + 1], srcB[:], axis=AX.X, op=ALU.max), "s_dve")
                            p.wait("vector", "s_dve", c_mp)
                            p.op("vector", lambda e, j=j: e.tensor_scalar(
                                mj8A[:, j:j + 1], mpA[:, j:j + 1], -0.125, None, op0=ALU.mult))
                            c_mj = p.op("vector", lambda e, j=j: e.tensor_scalar(
                                mj8B[:, j:j + 1], mpB[:, j:j + 1], -0.125, None, op0=ALU.mult), "s_dve")
                            p.wait("scalar", "s_dve", c_mj)
                            if j == 0:
                                p.wait("scalar", "s_pe", pb_used)
                                p.wait("scalar", "s_dve", lw_done)
                            c_eA = p.op("scalar", lambda e, j=j, srcA=srcA: e.activation(
                                pbA[:, j * 512:(j + 1) * 512], srcA[:], AF.Exp,
                                bias=mj8A[:, j:j + 1], scale=0.125,
                                accum_out=lpA[:, j:j + 1]), "s_act")
                            c_eB = p.op("scalar", lambda e, j=j, srcB=srcB: e.activation(
                                pbB[:, j * 512:(j + 1) * 512], srcB[:], AF.Exp,
                                bias=mj8B[:, j:j + 1], scale=0.125,
                                accum_out=lpB[:, j:j + 1]), "s_act")
                            if not diag:
                                sc_rel = ("s_act", c_eB)
                            last_exp = c_eB
                            yield 1.2
                        p.wait("vector", "s_act", last_exp)
                        p.op("vector", lambda e, nch=nch: e.tensor_reduce(
                            nmA[:], mpA[:, 0:nch], axis=AX.X, op=ALU.max))
                        c_nm = p.op("vector", lambda e, nch=nch: e.tensor_reduce(
                            nmB[:], mpB[:, 0:nch], axis=AX.X, op=ALU.max), "s_dve")
                        p.wait("vector", "s_dve", c_nm)
                        p.op("vector", lambda e: e.tensor_scalar(
                            nm8A[:], nmA[:], -0.125, None, op0=ALU.mult))
                        c_nm8 = p.op("vector", lambda e: e.tensor_scalar(
                            nm8B[:], nmB[:], -0.125, None, op0=ALU.mult), "s_dve")
                        p.wait("scalar", "s_dve", c_nm8)
                        p.op("scalar", lambda e, nch=nch: e.activation(
                            csA[:, 0:nch], mpA[:, 0:nch], AF.Exp, bias=nm8A[:], scale=0.125))
                        c_cs = p.op("scalar", lambda e, nch=nch: e.activation(
                            csB[:, 0:nch], mpB[:, 0:nch], AF.Exp, bias=nm8B[:], scale=0.125), "s_act")
                        mp_free = c_cs
                        p.wait("vector", "s_act", c_cs)
                        p.op("vector", lambda e, nch=nch: e.tensor_tensor(
                            lwA[:, 0:nch], lpA[:, 0:nch], csA[:, 0:nch], op=ALU.mult))
                        c_lw = p.op("vector", lambda e, nch=nch: e.tensor_tensor(
                            lwB[:, 0:nch], lpB[:, 0:nch], csB[:, 0:nch], op=ALU.mult), "s_dve")
                        p.wait("vector", "s_dve", c_lw)
                        p.wait("vector", "s_pe", l_pe)
                        p.op("vector", lambda e, nch=nch: e.reduce_sum(
                            lA2[:], lwA[:, 0:nch], axis=AX.X))
                        c_l = p.op("vector", lambda e, nch=nch: e.reduce_sum(
                            lB2[:], lwB[:, 0:nch], axis=AX.X), "s_dve")
                        lw_done = c_l
                        c_corr = None
                        for j in range(nch):
                            p.op("vector", lambda e, j=j: e.tensor_scalar(
                                pbA[:, j * 512:(j + 1) * 512], pbA[:, j * 512:(j + 1) * 512],
                                csA[:, j:j + 1], None, op0=ALU.mult))
                            c_corr = p.op("vector", lambda e, j=j: e.tensor_scalar(
                                pbB[:, j * 512:(j + 1) * 512], pbB[:, j * 512:(j + 1) * 512],
                                csB[:, j:j + 1], None, op0=ALU.mult), "s_dve")
                        yield 0.5
                        p.wait("tensor", "s_dve", c_l)
                        p.wait("tensor", "s_dve", pv_ev)
                        p.op("tensor", lambda e: e.transpose(
                            ps_pv[0:1, 256:384], lA2[:], idf[:]))
                        c_lt = p.op("tensor", lambda e: e.transpose(
                            ps_pv[0:1, 384:512], lB2[:], idf[:]), "s_pe")
                        l_pe = c_lt
                        p.wait("vector", "s_pe", c_lt)
                        p.op("vector", lambda e: e.reciprocal(li_rA[:], ps_pv[0:1, 256:384]))
                        c_li = p.op("vector", lambda e: e.reciprocal(li_rB[:], ps_pv[0:1, 384:512]), "s_dve")
                        yield 0.2
                        c_pv = None
                        for hh in range(2):
                            pbuf = pbA if hh == 0 else pbB
                            p.wait("tensor", "s_dve", c_corr)
                            last_dve = 0
                            last_act = 0
                            ngrp = (ktiles + 3) // 4
                            for gi in range(ngrp):
                                kt0 = gi * 4
                                kt1 = min(ktiles, kt0 + 4)
                                tb = gi % 2
                                sem_n, cnt = tr_ev[tb]
                                p.wait("tensor", sem_n, cnt)
                                c_tr = None
                                for kt in range(kt0, kt1):
                                    c_tr = p.op("tensor", lambda e, kt=kt, kt0=kt0, tb=tb, pbuf=pbuf: e.transpose(
                                        ps_t[tb][:, (kt - kt0) * 128:(kt - kt0 + 1) * 128],
                                        pbuf[:, kt * 128:(kt + 1) * 128],
                                        idb[:]), "s_pe")
                                if kt1 == ktiles and hh == 1:
                                    pb_used = c_tr
                                eng = "vector" if tb == 0 else "scalar"
                                p.wait(eng, "s_pe", c_tr)
                                w = (kt1 - kt0) * 128
                                if eng == "vector":
                                    c_ev = p.op("vector", lambda e, kt0=kt0, w=w, tb=tb: e.tensor_copy(
                                        pT[:, kt0 * 128:kt0 * 128 + w], ps_t[tb][:, 0:w]), "s_dve")
                                    last_dve = c_ev
                                    tr_ev[tb] = ("s_dve", c_ev)
                                else:
                                    c_ev = p.op("scalar", lambda e, kt0=kt0, w=w, tb=tb: e.copy(
                                        pT[:, kt0 * 128:kt0 * 128 + w], ps_t[tb][:, 0:w]), "s_act")
                                    last_act = c_ev
                                    tr_ev[tb] = ("s_act", c_ev)
                                yield 0.8
                            p.wait("tensor", "s_dve", last_dve)
                            p.wait("tensor", "s_act", last_act)
                            if hh == 0:
                                p.wait("tensor", "s_dve", c_li)
                            off = 0 if hh == 0 else 64
                            tp = (0, 0) if hh == 0 else (0, 64)
                            for kt in range(ktiles):
                                if kt == ktiles - 1:
                                    c_pv = p.op("tensor", lambda e, kt=kt, off=off, tp=tp: e.matmul(
                                        ps_pv[off:off + 64, 0:128],
                                        vp[:, kt * 128 + off:kt * 128 + off + 64],
                                        pT[:, kt * 128:(kt + 1) * 128],
                                        start=(kt == 0), stop=True, tile_position=tp), "s_pe")
                                else:
                                    p.op("tensor", lambda e, kt=kt, off=off, tp=tp: e.matmul(
                                        ps_pv[off:off + 64, 0:128],
                                        vp[:, kt * 128 + off:kt * 128 + off + 64],
                                        pT[:, kt * 128:(kt + 1) * 128],
                                        start=(kt == 0), stop=False, tile_position=tp))
                            yield 0.2 + 0.12 * ktiles
                        p.op("tensor", lambda e: e.matmul(
                            ps_pv[0:64, 128:256], onef[0:1, 0:64], li_rA[:],
                            start=True, stop=True, tile_position=(0, 0)))
                        c_bc = p.op("tensor", lambda e: e.matmul(
                            ps_pv[64:128, 128:256], onef[0:1, 0:64], li_rB[:],
                            start=True, stop=True, tile_position=(0, 64)), "s_pe")
                        li_pe = c_bc
                        if i == NTT - 1:
                            pair_free = c_bc
                        cb = i % 2
                        p.wait("vector", "s_pe", c_bc)
                        c_bce = p.op("vector", lambda e: e.tensor_copy(
                            bc_sb[:], ps_pv[:, 128:256]), "s_dve")
                        p.wait("vector", "s_dve", c_bce)
                        p.wait("vector", "s_st", cat_st[cb])
                        c_cat = p.op("vector", lambda e, cb=cb: e.tensor_tensor(
                            cat_b[cb][:], ps_pv[:, 0:128], bc_sb[:], op=ALU.mult), "s_dve")
                        pv_ev = c_cat
                        p.wait("scalar", "s_dve", c_cat)
                        cat_st[cb] = p.op("scalar", lambda e, pr=pr, i=i, cb=cb: e.dma_start(
                            out=CATH[pr * 128:(pr + 1) * 128, i * 128:(i + 1) * 128],
                            in_=cat_b[cb][:]), "s_st", 16)
                        c_cat_stores[0] = cat_st[cb]
                        yield 0.3

            def gen_moe():
                h_ev = [0, 0]
                o_ev = 0
                i_ev = 0
                w1_ld = [0, 0]
                w1_free = [0, 0]
                w2_ld = 0
                w2_free = 0
                b1_ld = [0, 0]
                b1_free = [0, 0]
                xng_ld = 0
                xng_free = 0
                y_init = [0] * 8
                y_last = [0] * 8
                y_spill_st = 0

                def load_w2(e_):
                    c = None
                    for q4 in range(4):
                        c = p.op("sync", lambda e, e_=e_, q4=q4: e.dma_start(
                            out=w2[:, q4 * 8 * D:(q4 + 1) * 8 * D],
                            in_=W2C[e_, :, q4 * 8 * D:(q4 + 1) * 8 * D]), "s_ld", 16)
                    return c

                def load_w1(e_, qi):
                    rs = qi % 2
                    p.wait("sync", "s_pe", w1_free[rs])
                    w1_ld[rs] = p.op("sync", lambda e, e_=e_, qi=qi, rs=rs: e.dma_start(
                        out=w1blk[rs][:].rearrange("p (m c) -> p m c", m=4),
                        in_=W1C[e_, 4 * qi:4 * qi + 4].rearrange("m p c -> p m c")), "s_ld", 16)

                def emit_y_init(idx, g):
                    nonlocal i_ev
                    tt, n = idx // 2, idx % 2
                    p.wait("tensor", "s_dve", max(i_ev, o_ev))
                    p.wait("tensor", "s_cast", c_b2s)
                    c_mm = p.op("tensor", lambda e, g=g, tt=tt, n=n: e.matmul(
                        ps_oo[:], gatesT[0:NE, (g * 4 + tt) * 128:(g * 4 + tt + 1) * 128],
                        b2s[0:NE, n * 512:(n + 1) * 512], start=True, stop=True), "s_pe")
                    p.wait("vector", "s_pe", c_mm)
                    p.wait("vector", "s_st", y_spill_st)
                    i_ev = p.op("vector", lambda e, tt=tt, n=n: e.tensor_copy(
                        y_sb[:, tt * D + n * 512:tt * D + (n + 1) * 512], ps_oo[:]), "s_dve")
                    y_init[idx] = i_ev

                for g in range(NGRP):
                    p.wait("sync", "s_pe", xng_free)
                    xng_ld = p.op("sync", lambda e, g=g: e.dma_start(
                        out=xng[:].rearrange("p (c t) -> p c t", c=NC_),
                        in_=XNT.rearrange("(c p) t -> p c t", p=128)[:, :, g * GRP:(g + 1) * GRP]),
                        "s_ld", 16)
                    init_q = list(range(8))
                    for e_ in range(NE):
                        bb = e_ % 2
                        p.wait("sync", "s_pe", w2_free)
                        w2_ld = load_w2(e_)
                        p.wait("sync", "s_act", b1_free[bb])
                        b1_ld[bb] = p.op("sync", lambda e, e_=e_, bb=bb: e.dma_start(
                            out=b1f[bb][:], in_=B1F[e_]), "s_ld", 16)
                        load_w1(e_, 0)
                        load_w1(e_, 1)
                        for mi in range(32):
                            rs = (mi // 4) % 2
                            hb = mi % 2
                            p.wait("tensor", "s_ld", w1_ld[rs])
                            if mi == 0:
                                p.wait("tensor", "s_ld", xng_ld)
                                p.wait("tensor", "s_ld", b1_ld[bb])
                            p.wait("tensor", "s_act", h_ev[hb])
                            c_mm = None
                            for c_ in range(NC_):
                                off = (mi % 4) * (NC_ * 128)
                                if c_ == NC_ - 1:
                                    c_mm = p.op("tensor", lambda e, off=off, c_=c_, rs=rs, hb=hb: e.matmul(
                                        ps_h[hb][:], w1blk[rs][:, off + c_ * 128:off + (c_ + 1) * 128],
                                        xng[:, c_ * GRP:(c_ + 1) * GRP],
                                        start=False, stop=True), "s_pe")
                                else:
                                    p.op("tensor", lambda e, off=off, c_=c_, rs=rs, hb=hb: e.matmul(
                                        ps_h[hb][:], w1blk[rs][:, off + c_ * 128:off + (c_ + 1) * 128],
                                        xng[:, c_ * GRP:(c_ + 1) * GRP],
                                        start=(c_ == 0), stop=False))
                            p.wait("scalar", "s_pe", c_mm)
                            h_ev[hb] = p.op("scalar", lambda e, mi=mi, hb=hb, bb=bb: e.activation(
                                h_sb[:, mi * GRP:(mi + 1) * GRP],
                                ps_h[hb][:], AF.Gelu, bias=b1f[bb][:, mi:mi + 1]), "s_act")
                            if mi % 4 == 3:
                                w1_free[rs] = c_mm
                            xng_free = c_mm
                            if e_ == 0 and mi % 4 == 0 and init_q:
                                emit_y_init(init_q.pop(0), g)
                            if mi % 4 == 3 and mi // 4 + 2 < 8:
                                load_w1(e_, mi // 4 + 2)
                            yield 1.8
                        b1_free[bb] = h_ev[1]
                        c_h_all = p.c("s_act")
                        p.wait("tensor", "s_act", c_h_all)
                        p.wait("tensor", "s_ld", w2_ld)
                        for n2 in range(2):
                            for tt in range(4):
                                idx = tt * 2 + n2
                                p.wait("tensor", "s_dve", max(o_ev, i_ev))
                                c_mm = None
                                for ic in range(32):
                                    if ic == 31:
                                        c_mm = p.op("tensor", lambda e, n2=n2, tt=tt, ic=ic: e.matmul(
                                            ps_oo[:], h_sb[:, ic * GRP + tt * 128:ic * GRP + (tt + 1) * 128],
                                            w2[:, ic * D + n2 * 512:ic * D + (n2 + 1) * 512],
                                            start=False, stop=True), "s_pe")
                                    else:
                                        p.op("tensor", lambda e, n2=n2, tt=tt, ic=ic: e.matmul(
                                            ps_oo[:], h_sb[:, ic * GRP + tt * 128:ic * GRP + (tt + 1) * 128],
                                            w2[:, ic * D + n2 * 512:ic * D + (n2 + 1) * 512],
                                            start=(ic == 0), stop=False))
                                if n2 == 1 and tt == 3:
                                    w2_free = c_mm
                                p.wait("vector", "s_pe", c_mm)
                                if e_ == 0:
                                    p.wait("vector", "s_dve", y_init[idx])
                                else:
                                    p.wait("vector", "s_dve", y_last[idx])
                                gcol = (g * 4 + tt) * NE + e_
                                o_ev = y_last[idx] = p.op("vector", lambda e, n2=n2, tt=tt, gcol=gcol: e.scalar_tensor_tensor(
                                    y_sb[:, tt * D + n2 * 512:tt * D + (n2 + 1) * 512],
                                    ps_oo[:], gates[:, gcol:gcol + 1],
                                    y_sb[:, tt * D + n2 * 512:tt * D + (n2 + 1) * 512],
                                    op0=ALU.mult, op1=ALU.add), "s_dve")
                                yield 7.0
                    c_moe_done = p.c("s_dve")
                    p.wait("scalar", "s_dve", c_moe_done)
                    for tt in range(4):
                        y_spill_st = p.op("scalar", lambda e, g=g, tt=tt: e.dma_start(
                            out=YH[(g * 4 + tt) * 128:(g * 4 + tt + 1) * 128, :],
                            in_=y_sb[:, tt * D:(tt + 1) * D]), "s_st", 16)
                    c_y_stores[0] = y_spill_st
                    yield 0.5

            att = iter(()) if _DISABLE_ATT else gen_att()
            moe = iter(()) if _DISABLE_MOE else gen_moe()
            RATIO = 1.55
            cum_a = 0.0
            cum_m = 0.0
            a_alive = True
            m_alive = True
            while a_alive or m_alive:
                if a_alive and (not m_alive or cum_m >= cum_a * RATIO):
                    try:
                        cum_a += next(att)
                    except StopIteration:
                        a_alive = False
                else:
                    try:
                        cum_m += next(moe)
                    except StopIteration:
                        m_alive = False
            p.flush()

        with ExitStack() as ph:
            sb = lambda n, s, d: ph.enter_context(nc.sbuf_tensor(n, s, d))
            ps = lambda n, s, d: ph.enter_context(nc.psum_tensor(n, s, d))
            cat = sb("cat", [128, NC_ * T], BF16)
            wo = sb("wo", [128, NC_ * D], BF16)
            bor = sb("bor", [1, D], BF16)
            at_f = [sb(f"at_f{i}", [128, D], F32) for i in range(2)]
            x_t = [sb(f"x_t{i}", [128, D], F32) for i in range(2)]
            y_t = [sb(f"y_t{i}", [128, D], F32) for i in range(2)]
            o_t = [sb(f"o_t{i}", [128, D], F32) for i in range(2)]
            ps_o = [ps(f"ps_o{i}", [128, 512], F32) for i in range(2)]

            p.wait("sync", "s_st", p.c("s_st"))
            p.wait("gpsimd", "s_st", p.c("s_st"))
            p.op("gpsimd", lambda e: e.dma_start(
                out=wo[:].rearrange("p (c m) -> p c m", c=NC_),
                in_=WOT.rearrange("(c p) m -> p c m", p=128)), "s_cast", 16)
            c_bor = p.op("gpsimd", lambda e: e.dma_start(out=bor[:], in_=BO), "s_cast", 16)
            c_cat_ld = 0
            for half in range(2):
                c_cat_ld = p.op("sync", lambda e, half=half: e.dma_start(
                    out=cat[:].rearrange("p (c t) -> p c t", c=NC_)[:, half * 4:(half + 1) * 4, :],
                    in_=CATH.rearrange("(c p) t -> p c t", p=128)[:, half * 4:(half + 1) * 4, :]),
                    "s_ld", 16)
            at_ev = [0, 0]
            at_last = 0
            xy_free = [0, 0]
            out_st = [0, 0]
            p.wait("tensor", "s_ld", c_cat_ld)
            p.wait("tensor", "s_cast", c_bor)
            for tt in range(NTT):
                tb = tt % 2
                p.wait("sync", "s_dve", xy_free[tb])
                p.op("sync", lambda e, tt=tt, tb=tb: e.dma_start(
                    out=x_t[tb][:], in_=X[tt * 128:(tt + 1) * 128, :]), "s_ld", 16)
                c_xy = p.op("sync", lambda e, tt=tt, tb=tb: e.dma_start(
                    out=y_t[tb][:], in_=YH[tt * 128:(tt + 1) * 128, :]), "s_ld", 16)
                c_mm2 = [None, None]
                for n in range(2):
                    pb = n
                    p.wait("tensor", "s_dve", at_last)
                    p.op("tensor", lambda e, n=n, pb=pb: e.matmul(
                        ps_o[pb][:], oneb[:], bor[:, n * 512:(n + 1) * 512],
                        start=True, stop=False))
                    for c_ in range(NC_):
                        if c_ == NC_ - 1:
                            c_mm2[n] = p.op("tensor", lambda e, tt=tt, n=n, c_=c_, pb=pb: e.matmul(
                                ps_o[pb][:], cat[:, c_ * T + tt * 128:c_ * T + (tt + 1) * 128],
                                wo[:, c_ * D + n * 512:c_ * D + (n + 1) * 512],
                                start=False, stop=True), "s_pe")
                        else:
                            p.op("tensor", lambda e, tt=tt, n=n, c_=c_, pb=pb: e.matmul(
                                ps_o[pb][:], cat[:, c_ * T + tt * 128:c_ * T + (tt + 1) * 128],
                                wo[:, c_ * D + n * 512:c_ * D + (n + 1) * 512],
                                start=False, stop=False))
                p.wait("vector", "s_pe", c_mm2[1])
                p.op("vector", lambda e, tb=tb: e.tensor_copy(
                    at_f[tb][:, 0:512], ps_o[0][:]))
                c_ate = p.op("vector", lambda e, tb=tb: e.tensor_copy(
                    at_f[tb][:, 512:1024], ps_o[1][:]), "s_dve")
                at_ev[tb] = c_ate
                at_last = c_ate
                p.wait("vector", "s_ld", c_xy)
                p.wait("vector", "s_st", out_st[tb])
                c_s1 = p.op("vector", lambda e, tb=tb: e.tensor_tensor(
                    o_t[tb][:], x_t[tb][:], y_t[tb][:], op=ALU.add), "s_dve")
                p.wait("vector", "s_dve", c_s1)
                c_o = p.op("vector", lambda e, tb=tb: e.tensor_tensor(
                    o_t[tb][:], o_t[tb][:], at_f[tb][:], op=ALU.add), "s_dve")
                xy_free[tb] = c_o
                p.wait("scalar", "s_dve", c_o)
                out_st[tb] = p.op("scalar", lambda e, tt=tt, tb=tb: e.dma_start(
                    out=OUT[tt * 128:(tt + 1) * 128, :], in_=o_t[tb][:]), "s_st", 16)
            if _DEBUG_TAPS:
                p.wait("gpsimd", "s_st", p.c("s_st"))
                p.wait("gpsimd", "s_dve", p.c("s_dve"))
                for tname in _DEBUG_TAPS:
                    ap, shape = scratch_regs[tname]
                    dbg = nc.dram_tensor("dbg_" + tname, list(shape), F32,
                                         kind="ExternalOutput").ap()
                    p.op("gpsimd", lambda e, dbg=dbg, ap=ap: e.dma_start(
                        out=dbg, in_=ap), "s_cast", 16)
            p.wait("scalar", "s_st", p.c("s_st"))
            p.wait("gpsimd", "s_cast", p.c("s_cast"))
            p.flush()

    return nc


def _host_prep(L0_attn_W_q, L0_attn_b_q, L0_attn_W_k, L0_attn_b_k, L0_attn_W_v,
               L0_attn_W_o, L0_ffn_Wg, L0_ffn_W1, L0_ffn_b1, L0_ffn_W2):
    perm = []
    for h in range(NH):
        base = DH * h
        perm += [base + 2 * j for j in range(32)] + [base + 2 * j + 1 for j in range(32)]
    perm = np.array(perm)
    ac = np.ascontiguousarray
    Wq = np.asarray(L0_attn_W_q)
    Wk = np.asarray(L0_attn_W_k)
    common = {}
    common["wqt"] = ac(Wq.T[:, perm])
    common["wkt"] = ac(Wk.T[:, perm])
    common["wvt"] = ac(np.asarray(L0_attn_W_v).T)
    common["wot"] = ac(np.asarray(L0_attn_W_o).T)
    common["wgt"] = ac(np.asarray(L0_ffn_Wg).T)
    common["bq"] = ac(np.asarray(L0_attn_b_q)[perm].reshape(NC_, 128).T)
    common["bk"] = ac(np.asarray(L0_attn_b_k)[perm].reshape(NC_, 128).T)
    common["w1p"] = ac(np.asarray(L0_ffn_W1).reshape(NE, NC_, 128, 32, 128)
                       .transpose(0, 3, 2, 1, 4).reshape(NE, 32, 128, NC_ * 128))
    common["w2p"] = ac(np.asarray(L0_ffn_W2).reshape(NE, 32, 128, D)
                       .transpose(0, 2, 1, 3).reshape(NE, 128, 32 * D))
    common["b1f"] = ac(np.asarray(L0_ffn_b1).reshape(NE, 32, 128).transpose(0, 2, 1))
    j = np.arange(32, dtype=np.float64)
    theta = 10000.0 ** (-j / 32.0)
    t = np.arange(T, dtype=np.float64)
    ang = t[None, :] * theta[:, None]
    cosv = np.cos(ang)
    sinv = np.sin(ang)
    ccr = np.tile(cosv, (4, 1))
    ssr = np.concatenate([-sinv, sinv, -sinv, sinv], axis=0)
    common["cc"] = ccr.astype(BF)
    common["ss"] = ssr.astype(BF)
    psw = np.zeros((128, 128), dtype=BF)
    for pp in range(128):
        jj = pp // 32
        swp = pp + 32 if jj % 2 == 0 else pp - 32
        psw[swp, pp] = 1.0
    common["psw"] = psw
    common["idb"] = np.eye(128, dtype=BF)
    common["idf"] = np.eye(128, dtype=np.float32)
    common["oneb"] = np.ones((1, 128), dtype=BF)
    common["onef"] = np.ones((1, 128), dtype=np.float32)
    m = np.zeros((128, 4 * 512), dtype=np.float32)
    qq = np.arange(128)[:, None]
    kk = np.arange(512)[None, :]
    for r in range(4):
        m[:, r * 512:(r + 1) * 512] = np.where(kk <= 128 * r + qq, 0.0, NEG)
    common["mask"] = m.astype(BF)
    return common


def kernel(L0_attn_x, L0_attn_W_q, L0_attn_b_q, L0_attn_W_k, L0_attn_b_k,
           L0_attn_W_v, L0_attn_b_v, L0_attn_W_o, L0_attn_b_o,
           L0_ffn_Wg, L0_ffn_W1, L0_ffn_b1, L0_ffn_W2, L0_ffn_b2):
    common = _host_prep(L0_attn_W_q, L0_attn_b_q, L0_attn_W_k, L0_attn_b_k,
                        L0_attn_W_v, L0_attn_W_o, L0_ffn_Wg, L0_ffn_W1,
                        L0_ffn_b1, L0_ffn_W2)
    common["bv"] = np.ascontiguousarray(np.asarray(L0_attn_b_v, dtype=np.float32)[None, :])
    common["bo"] = np.ascontiguousarray(np.asarray(L0_attn_b_o, dtype=np.float32)[None, :])
    common["b2"] = np.ascontiguousarray(np.asarray(L0_ffn_b2, dtype=np.float32))
    common = {k: (np.asarray(v, dtype=np.float32) if np.asarray(v).dtype == np.float64 else np.asarray(v))
              for k, v in common.items()}
    x = np.asarray(L0_attn_x, dtype=np.float32)
    in_maps = []
    for c in range(8):
        m = dict(common)
        m["x"] = np.ascontiguousarray(x[c])
        in_maps.append(m)
    nc = build_program()
    res = run_bass_kernel_spmd(nc, in_maps, core_ids=list(range(8)))
    out = np.stack([np.asarray(res.results[c]["out"]) for c in range(8)], axis=0)
    return out


# version 17
# speedup vs baseline: 1.0789x; 1.0759x over previous
import numpy as np
import ml_dtypes
from contextlib import ExitStack
import concourse.bass as bass
import concourse.mybir as mybir
from concourse.bass_utils import run_bass_kernel_spmd

F32 = mybir.dt.float32
F16 = mybir.dt.float16
BF16 = mybir.dt.bfloat16
AF = mybir.ActivationFunctionType
ALU = mybir.AluOpType
AX = mybir.AxisListType
BF = ml_dtypes.bfloat16

D = 1024
T = 4096
NH = 16
DH = 64
NE = 8
INTER = 4096
NC_ = 8
NTT = 32
NGRP = 8
GRP = 512
NEG = -30000.0

_DEBUG_TAPS = []
_VERBOSE = False
_DISABLE_ATT = False
_DISABLE_MOE = False


class Prog:
    ENGINES = ("sync", "tensor", "vector", "scalar", "gpsimd")

    def __init__(self, nc):
        self.nc = nc
        self.streams = {e: [] for e in self.ENGINES}
        self.sems = {}
        self.cnt = {}
        self.last_wait = {}

    def sem(self, name, stack):
        h = stack.enter_context(self.nc.semaphore(name))
        self.sems[name] = h
        self.cnt[name] = 0

    def wait(self, eng, sem, val=None):
        v = self.cnt[sem] if val is None else val
        if v <= 0:
            return
        key = (eng, sem)
        if self.last_wait.get(key, -1) >= v:
            return
        self.last_wait[key] = v
        h = self.sems[sem]
        self.streams[eng].append(lambda e, h=h, v=v: e.wait_ge(h, v))

    def op(self, eng, build, inc=None, by=1):
        if inc is None:
            self.streams[eng].append(lambda e, b=build: b(e))
            return None
        h = self.sems[inc]
        self.cnt[inc] += by
        self.streams[eng].append(lambda e, b=build, h=h, by=by: b(e).then_inc(h, by))
        return self.cnt[inc]

    def c(self, sem):
        return self.cnt[sem]

    def flush(self):
        nc = self.nc
        streams = self.streams
        self.streams = {e: [] for e in self.ENGINES}
        with nc.Block() as block:
            if streams["sync"]:
                @block.sync
                def _(e, ops=streams["sync"]):
                    for f in ops:
                        f(e)
            if streams["tensor"]:
                @block.tensor
                def _(e, ops=streams["tensor"]):
                    for f in ops:
                        f(e)
            if streams["vector"]:
                @block.vector
                def _(e, ops=streams["vector"]):
                    for f in ops:
                        f(e)
            if streams["scalar"]:
                @block.scalar
                def _(e, ops=streams["scalar"]):
                    for f in ops:
                        f(e)
            if streams["gpsimd"]:
                @block.gpsimd
                def _(e, ops=streams["gpsimd"]):
                    for f in ops:
                        f(e)


def build_program():
    nc = bass.Bass("TRN2", target_bir_lowering=False)

    scratch_regs = {}

    def dram(name, shape, dt, kind):
        ap = nc.dram_tensor(name, shape, dt, kind=kind).ap()
        if kind == "Internal":
            scratch_regs[name] = (ap, shape)
        return ap

    X = dram("x", [T, D], F32, "ExternalInput")
    WQT = dram("wqt", [D, D], F32, "ExternalInput")
    WKT = dram("wkt", [D, D], F32, "ExternalInput")
    WVT = dram("wvt", [D, D], F32, "ExternalInput")
    WOT = dram("wot", [D, D], F32, "ExternalInput")
    WGT = dram("wgt", [D, NE], F32, "ExternalInput")
    BQ = dram("bq", [128, NC_], F32, "ExternalInput")
    BK = dram("bk", [128, NC_], F32, "ExternalInput")
    BV = dram("bv", [1, D], F32, "ExternalInput")
    BO = dram("bo", [1, D], F32, "ExternalInput")
    W1P = dram("w1p", [NE, 32, 128, NC_ * 128], F32, "ExternalInput")
    W2P = dram("w2p", [NE, 128, 32 * D], F32, "ExternalInput")
    B1F = dram("b1f", [NE, 128, 32], F32, "ExternalInput")
    B2 = dram("b2", [NE, D], F32, "ExternalInput")
    CC = dram("cc", [128, T], BF16, "ExternalInput")
    SS = dram("ss", [128, T], BF16, "ExternalInput")
    PSW = dram("psw", [128, 128], BF16, "ExternalInput")
    IDB = dram("idb", [128, 128], BF16, "ExternalInput")
    IDF = dram("idf", [128, 128], F32, "ExternalInput")
    ONEB = dram("oneb", [1, 128], BF16, "ExternalInput")
    ONEF = dram("onef", [1, 128], F32, "ExternalInput")
    MASK = dram("mask", [128, 896], BF16, "ExternalInput")

    OUT = dram("out", [T, D], F32, "ExternalOutput")

    XNT = dram("xnt_h", [D, T], BF16, "Internal")
    QT = dram("qt_h", [D, T], BF16, "Internal")
    KT = dram("kt_h", [D, T], BF16, "Internal")
    VH = dram("v_h", [T, D], BF16, "Internal")
    CATH = dram("cat_h", [D, T], BF16, "Internal")
    ATTH = dram("att_h", [T, D], F32, "Internal")
    YH = dram("y_h", [T, D], F32, "Internal")
    W1C = dram("w1c", [NE, 32, 128, NC_ * 128], BF16, "Internal")
    W2C = dram("w2c", [NE, 128, 32 * D], BF16, "Internal")

    with ExitStack() as outer:
        p = Prog(nc)
        for s in ["s_ld", "s_cast", "s_st", "s_pe", "s_act", "s_dve", "s_pl"]:
            p.sem(s, outer)

        idb = outer.enter_context(nc.sbuf_tensor("idb_sb", [128, 128], BF16))
        idf = outer.enter_context(nc.sbuf_tensor("idf_sb", [128, 128], F32))
        oneb = outer.enter_context(nc.sbuf_tensor("oneb_sb", [1, 128], BF16))
        gates = outer.enter_context(nc.sbuf_tensor("gates_sb", [128, NTT * NE], F32))
        gatesT = outer.enter_context(nc.sbuf_tensor("gatesT_sb", [NE, T], BF16))

        if _VERBOSE:
            print("P01 sbuf_remaining", nc.sbuf_bytes_remaining)
        with ExitStack() as ph:
            sb = lambda n, s, d: ph.enter_context(nc.sbuf_tensor(n, s, d))
            ps = lambda n, s, d: ph.enter_context(nc.psum_tensor(n, s, d))
            x_b = [sb(f"x_b{i}", [128, D], F32) for i in range(2)]
            sq = [sb(f"sq{i}", [128, D], F32) for i in range(2)]
            s2 = [sb(f"s2_{i}", [128, 1], F32) for i in range(2)]
            nsum = [sb(f"nsum{i}", [128, 1], F32) for i in range(2)]
            m_neg = [sb(f"mneg{i}", [128, 1], F32) for i in range(2)]
            m2t = [sb(f"m2t{i}", [128, 1], F32) for i in range(2)]
            var = [sb(f"var{i}", [128, 1], F32) for i in range(2)]
            rstd = [sb(f"rstd{i}", [128, 1], F32) for i in range(2)]
            xn_b = [sb(f"xn_b{i}", [128, D], BF16) for i in range(2)]
            xnT = sb("xnT", [128, NC_ * T], BF16)
            wq = sb("wq", [128, NC_ * D], BF16)
            wk = sb("wk", [128, NC_ * D], BF16)
            wv = sb("wv", [128, NC_ * D], BF16)
            wg = sb("wg", [128, NC_ * NE], BF16)
            bqf = sb("bqf", [128, NC_], F32)
            bkf = sb("bkf", [128, NC_], F32)
            bvr = sb("bvr", [1, D], BF16)
            cc = sb("cc_sb", [128, T], BF16)
            ss = sb("ss_sb", [128, T], BF16)
            psw = sb("psw_sb", [128, 128], BF16)
            q_sb = [sb(f"q_sb{i}", [128, 512], BF16) for i in range(2)]
            t1b = [sb(f"t1b{i}", [128, 512], BF16) for i in range(2)]
            t2b = [sb(f"t2b{i}", [128, 512], BF16) for i in range(2)]
            rope_b = [sb(f"rope{i}", [128, 512], BF16) for i in range(2)]
            v_b = [sb(f"v_b{i}", [128, 512], BF16) for i in range(2)]
            logT = sb("logT", [NE, T], F32)
            lt_b = sb("lt_b", [128, NE], F32)
            g_m1 = sb("g_m1", [128, 1], F32)
            g_m2 = sb("g_m2", [128, 1], F32)
            g_d12 = sb("g_d12", [128, 1], F32)
            g_w1 = sb("g_w1", [128, 1], F32)
            g_w2 = sb("g_w2", [128, 1], F32)
            g_is1 = sb("g_is1", [128, NE], F32)
            g_is2 = sb("g_is2", [128, NE], F32)
            g_msk = sb("g_msk", [128, NE], F32)
            g_g1 = sb("g_g1", [128, NE], F32)
            gt_b = sb("gt_b", [128, NE], F32)
            ps_tr = [ps(f"ps_tr{i}", [128, 1024], BF16) for i in range(2)]
            ps_pr = [ps(f"ps_pr{i}", [128, 512], F32) for i in range(2)]
            ps_sw = [ps(f"ps_sw{i}", [128, 512], F32) for i in range(2)]
            ps_g = ps("ps_g", [128, 512], F32)

            p.op("sync", lambda e: e.dma_start(out=idb[:], in_=IDB), "s_ld", 16)
            p.op("sync", lambda e: e.dma_start(out=idf[:], in_=IDF), "s_ld", 16)
            p.op("sync", lambda e: e.dma_start(out=oneb[:], in_=ONEB), "s_ld", 16)
            p.op("sync", lambda e: e.dma_start(out=cc[:], in_=CC), "s_ld", 16)
            p.op("sync", lambda e: e.dma_start(out=ss[:], in_=SS), "s_ld", 16)
            p.op("sync", lambda e: e.dma_start(out=psw[:], in_=PSW), "s_ld", 16)
            p.op("sync", lambda e: e.dma_start(out=bqf[:], in_=BQ), "s_ld", 16)
            c_tabs = p.op("sync", lambda e: e.dma_start(out=bkf[:], in_=BK), "s_ld", 16)
            for w_sb_, W_ in [(wq, WQT), (wk, WKT), (wv, WVT)]:
                p.op("gpsimd", lambda e, o=w_sb_, i=W_: e.dma_start(
                    out=o[:].rearrange("p (c m) -> p c m", c=NC_),
                    in_=i.rearrange("(c p) m -> p c m", p=128)), "s_cast", 16)
            p.op("gpsimd", lambda e: e.dma_start(
                out=wg[:].rearrange("p (c m) -> p c m", c=NC_),
                in_=WGT.rearrange("(c p) m -> p c m", p=128)), "s_cast", 16)
            c_wproj = p.op("gpsimd", lambda e: e.dma_start(out=bvr[:], in_=BV), "s_cast", 16)
            for e_ in range(NE):
                for q4 in range(4):
                    sl = slice(q4 * 8, q4 * 8 + 8)
                    p.op("gpsimd", lambda e, e_=e_, sl=sl: e.dma_start(
                        out=W1C[e_, sl], in_=W1P[e_, sl]), "s_cast", 16)
                    sl2 = slice(q4 * 32, q4 * 32 + 32)
                    p.op("gpsimd", lambda e, e_=e_, sl2=sl2: e.dma_start(
                        out=W2C[e_, sl2], in_=W2P[e_, sl2]), "s_cast", 16)
            c_wcache = p.c("s_cast")

            ln_done = [0, 0]
            tr_done = [0, 0]
            xn_done = [0, 0]
            for t in range(NTT):
                b = t % 2
                p.wait("sync", "s_dve", ln_done[b])
                c_x = p.op("sync", lambda e, t=t, b=b: e.dma_start(
                    out=x_b[b][:], in_=X[t * 128:(t + 1) * 128, :]), "s_ld", 16)
                p.wait("scalar", "s_ld", c_x)
                p.wait("scalar", "s_dve", ln_done[b])
                c_sq = p.op("scalar", lambda e, b=b: e.activation(
                    sq[b][:], x_b[b][:], AF.Square, accum_out=s2[b][:]), "s_act")
                p.wait("vector", "s_ld", c_x)
                c_ns = p.op("vector", lambda e, b=b: e.tensor_reduce(
                    nsum[b][:], x_b[b][:], axis=AX.X, op=ALU.add, negate=True), "s_dve")
                p.wait("vector", "s_dve", c_ns)
                p.op("vector", lambda e, b=b: e.tensor_scalar(
                    m_neg[b][:], nsum[b][:], 1.0 / D, None, op0=ALU.mult))
                c_mn = p.op("vector", lambda e, b=b: e.scalar_tensor_tensor(
                    m2t[b][:], nsum[b][:], 1.0 / (D * D), nsum[b][:],
                    op0=ALU.mult, op1=ALU.mult), "s_dve")
                p.wait("vector", "s_act", c_sq)
                p.wait("vector", "s_dve", c_mn)
                c_va = p.op("vector", lambda e, b=b: e.tensor_scalar(
                    var[b][:], s2[b][:], 1.0 / D, 1e-5, op0=ALU.mult, op1=ALU.add), "s_dve")
                p.wait("vector", "s_dve", c_va)
                c_var = p.op("vector", lambda e, b=b: e.tensor_tensor(
                    var[b][:], var[b][:], m2t[b][:], op=ALU.subtract), "s_dve")
                p.wait("scalar", "s_dve", c_var)
                c_sqr = p.op("scalar", lambda e, b=b: e.activation(
                    rstd[b][:], var[b][:], AF.Sqrt), "s_act")
                p.wait("vector", "s_act", c_sqr)
                c_rs = p.op("vector", lambda e, b=b: e.reciprocal(
                    rstd[b][:], rstd[b][:]), "s_dve")
                p.wait("vector", "s_dve", c_rs)
                p.wait("vector", "s_pe", xn_done[b])
                c_xn = p.op("vector", lambda e, b=b: e.tensor_scalar(
                    xn_b[b][:], x_b[b][:], m_neg[b][:], rstd[b][:],
                    op0=ALU.add, op1=ALU.mult), "s_dve")
                ln_done[b] = c_xn
                p.wait("tensor", "s_dve", c_xn)
                p.wait("tensor", "s_ld", c_tabs)
                for c_ in range(NC_):
                    tb = c_ % 2
                    p.wait("tensor", "s_dve", tr_done[tb])
                    c_tr = p.op("tensor", lambda e, b=b, c_=c_, tb=tb: e.transpose(
                        ps_tr[tb][:, 0:128], xn_b[b][:, c_ * 128:(c_ + 1) * 128],
                        idb[:]), "s_pe")
                    if c_ == NC_ - 1:
                        xn_done[b] = c_tr
                    p.wait("vector", "s_pe", c_tr)
                    tr_done[tb] = p.op("vector", lambda e, t=t, c_=c_, tb=tb: e.tensor_copy(
                        xnT[:, c_ * T + t * 128: c_ * T + (t + 1) * 128],
                        ps_tr[tb][:, 0:128]), "s_dve")
            c_xnT_all = p.c("s_dve")

            pr_done = [0, 0]
            sw_done = [0, 0]
            qs_done = [0, 0]
            rope_st = [0, 0]
            p.wait("tensor", "s_cast", c_wproj)
            p.wait("tensor", "s_dve", c_xnT_all)
            for (w_sb, b_fm, OUT_T) in [(wq, bqf, QT), (wk, bkf, KT)]:
                for m in range(NC_):
                    for g in range(NC_):
                        pb = (m * NC_ + g) % 2
                        p.wait("tensor", "s_dve", pr_done[pb])
                        c_mm = None
                        for c_ in range(NC_):
                            if c_ == NC_ - 1:
                                c_mm = p.op("tensor", lambda e, w_sb=w_sb, m=m, g=g, c_=c_, pb=pb: e.matmul(
                                    ps_pr[pb][:], w_sb[:, c_ * D + m * 128:c_ * D + (m + 1) * 128],
                                    xnT[:, c_ * T + g * 512:c_ * T + (g + 1) * 512],
                                    start=False, stop=True), "s_pe")
                            else:
                                p.op("tensor", lambda e, w_sb=w_sb, m=m, g=g, c_=c_, pb=pb: e.matmul(
                                    ps_pr[pb][:], w_sb[:, c_ * D + m * 128:c_ * D + (m + 1) * 128],
                                    xnT[:, c_ * T + g * 512:c_ * T + (g + 1) * 512],
                                    start=(c_ == 0), stop=False))
                        p.wait("vector", "s_pe", c_mm)
                        p.wait("vector", "s_pe", qs_done[pb])
                        p.wait("vector", "s_dve", sw_done[pb])
                        c_ev = p.op("vector", lambda e, b_fm=b_fm, m=m, pb=pb: e.tensor_scalar(
                            q_sb[pb][:], ps_pr[pb][:], b_fm[:, m:m + 1], None,
                            op0=ALU.add), "s_dve")
                        pr_done[pb] = c_ev
                        p.wait("tensor", "s_dve", c_ev)
                        p.wait("tensor", "s_dve", sw_done[pb])
                        c_sw = p.op("tensor", lambda e, pb=pb: e.matmul(
                            ps_sw[pb][:], psw[:], q_sb[pb][:], start=True, stop=True), "s_pe")
                        qs_done[pb] = c_sw
                        p.wait("vector", "s_dve", c_ev)
                        p.op("vector", lambda e, g=g, pb=pb: e.tensor_tensor(
                            t1b[pb][:], q_sb[pb][:], cc[:, g * 512:(g + 1) * 512], op=ALU.mult))
                        p.wait("vector", "s_pe", c_sw)
                        c_t2 = p.op("vector", lambda e, g=g, pb=pb: e.tensor_tensor(
                            t2b[pb][:], ps_sw[pb][:], ss[:, g * 512:(g + 1) * 512], op=ALU.mult), "s_dve")
                        sw_done[pb] = c_t2
                        p.wait("vector", "s_dve", c_t2)
                        p.wait("vector", "s_st", rope_st[pb])
                        c_rp = p.op("vector", lambda e, pb=pb: e.tensor_tensor(
                            rope_b[pb][:], t1b[pb][:], t2b[pb][:], op=ALU.add), "s_dve")
                        p.wait("scalar", "s_dve", c_rp)
                        rope_st[pb] = p.op("scalar", lambda e, OUT_T=OUT_T, m=m, g=g, pb=pb: e.dma_start(
                            out=OUT_T[m * 128:(m + 1) * 128, g * 512:(g + 1) * 512],
                            in_=rope_b[pb][:]), "s_st", 16)

            v_ev_done = [0, 0]
            v_st = [0, 0]
            for tt in range(NTT):
                for n in range(2):
                    pb = (tt * 2 + n) % 2
                    p.wait("tensor", "s_dve", v_ev_done[pb])
                    p.wait("tensor", "s_dve", pr_done[pb])
                    p.op("tensor", lambda e, n=n, pb=pb: e.matmul(
                        ps_pr[pb][:], oneb[:], bvr[:, n * 512:(n + 1) * 512],
                        start=True, stop=False))
                    for c_ in range(NC_):
                        if c_ == NC_ - 1:
                            c_mm = p.op("tensor", lambda e, tt=tt, n=n, c_=c_, pb=pb: e.matmul(
                                ps_pr[pb][:], xnT[:, c_ * T + tt * 128:c_ * T + (tt + 1) * 128],
                                wv[:, c_ * D + n * 512:c_ * D + (n + 1) * 512],
                                start=False, stop=True), "s_pe")
                        else:
                            p.op("tensor", lambda e, tt=tt, n=n, c_=c_, pb=pb: e.matmul(
                                ps_pr[pb][:], xnT[:, c_ * T + tt * 128:c_ * T + (tt + 1) * 128],
                                wv[:, c_ * D + n * 512:c_ * D + (n + 1) * 512],
                                start=False, stop=False))
                    p.wait("vector", "s_pe", c_mm)
                    p.wait("vector", "s_st", v_st[pb])
                    c_ev = p.op("vector", lambda e, pb=pb: e.tensor_copy(
                        v_b[pb][:], ps_pr[pb][:]), "s_dve")
                    v_ev_done[pb] = c_ev
                    p.wait("scalar", "s_dve", c_ev)
                    v_st[pb] = p.op("scalar", lambda e, tt=tt, n=n, pb=pb: e.dma_start(
                        out=VH[tt * 128:(tt + 1) * 128, n * 512:(n + 1) * 512],
                        in_=v_b[pb][:]), "s_st", 16)

            c_lg = 0
            for g in range(NC_):
                p.wait("tensor", "s_dve", c_lg)
                for c_ in range(NC_):
                    if c_ == NC_ - 1:
                        c_mm = p.op("tensor", lambda e, g=g, c_=c_: e.matmul(
                            ps_g[0:NE, 0:512], wg[:, c_ * NE:(c_ + 1) * NE],
                            xnT[:, c_ * T + g * 512:c_ * T + (g + 1) * 512],
                            start=False, stop=True), "s_pe")
                    else:
                        p.op("tensor", lambda e, g=g, c_=c_: e.matmul(
                            ps_g[0:NE, 0:512], wg[:, c_ * NE:(c_ + 1) * NE],
                            xnT[:, c_ * T + g * 512:c_ * T + (g + 1) * 512],
                            start=(c_ == 0), stop=False))
                p.wait("vector", "s_pe", c_mm)
                c_lg = p.op("vector", lambda e, g=g: e.tensor_copy(
                    logT[0:NE, g * 512:(g + 1) * 512], ps_g[0:NE, 0:512]), "s_dve")
            for tt in range(NTT):
                p.wait("tensor", "s_dve", c_lg)
                c_tr = p.op("tensor", lambda e, tt=tt: e.transpose(
                    ps_g[:, 0:NE], logT[0:NE, tt * 128:(tt + 1) * 128],
                    idf[0:NE, 0:NE]), "s_pe")
                p.wait("vector", "s_pe", c_tr)
                c_lt = p.op("vector", lambda e: e.tensor_copy(
                    lt_b[:], ps_g[:, 0:NE]), "s_dve")
                p.wait("vector", "s_dve", c_lt)
                c_m1 = p.op("vector", lambda e: e.tensor_reduce(
                    g_m1[:], lt_b[:], axis=AX.X, op=ALU.max), "s_dve")
                p.wait("vector", "s_dve", c_m1)
                c_is1 = p.op("vector", lambda e: e.tensor_scalar(
                    g_is1[:], lt_b[:], g_m1[:], None, op0=ALU.is_ge), "s_dve")
                p.wait("vector", "s_dve", c_is1)
                c_msk = p.op("vector", lambda e: e.scalar_tensor_tensor(
                    g_msk[:], g_is1[:], NEG, lt_b[:], op0=ALU.mult, op1=ALU.add), "s_dve")
                p.wait("vector", "s_dve", c_msk)
                c_m2 = p.op("vector", lambda e: e.tensor_reduce(
                    g_m2[:], g_msk[:], axis=AX.X, op=ALU.max), "s_dve")
                p.wait("vector", "s_dve", c_m2)
                c_d = p.op("vector", lambda e: e.tensor_tensor(
                    g_d12[:], g_m1[:], g_m2[:], op=ALU.subtract), "s_dve")
                p.wait("scalar", "s_dve", c_d)
                p.op("scalar", lambda e: e.activation(g_w1[:], g_d12[:], AF.Sigmoid))
                c_w2 = p.op("scalar", lambda e: e.activation(
                    g_w2[:], g_d12[:], AF.Sigmoid, scale=-1.0), "s_act")
                c_is2 = p.op("vector", lambda e: e.tensor_scalar(
                    g_is2[:], g_msk[:], g_m2[:], None, op0=ALU.is_ge), "s_dve")
                p.wait("vector", "s_act", c_w2)
                p.wait("vector", "s_dve", c_is2)
                c_g1 = p.op("vector", lambda e: e.tensor_scalar(
                    g_g1[:], g_is1[:], g_w1[:], None, op0=ALU.mult), "s_dve")
                p.wait("vector", "s_dve", c_g1)
                c_gt = p.op("vector", lambda e, tt=tt: e.scalar_tensor_tensor(
                    gates[:, tt * NE:(tt + 1) * NE], g_is2[:], g_w2[:], g_g1[:],
                    op0=ALU.mult, op1=ALU.add), "s_dve")
                p.wait("vector", "s_dve", c_gt)
                c_gtb = p.op("vector", lambda e, tt=tt: e.tensor_copy(
                    gt_b[:], gates[:, tt * NE:(tt + 1) * NE]), "s_dve")
                p.wait("tensor", "s_dve", c_gtb)
                c_tr2 = p.op("tensor", lambda e: e.transpose(
                    ps_g[0:NE, 0:128], gt_b[:], idf[:]), "s_pe")
                p.wait("vector", "s_pe", c_tr2)
                c_lg = p.op("vector", lambda e, tt=tt: e.tensor_copy(
                    gatesT[0:NE, tt * 128:(tt + 1) * 128], ps_g[0:NE, 0:128]), "s_dve")

            p.wait("scalar", "s_dve", c_xnT_all)
            c_xnt_sp = 0
            for half in range(2):
                c_xnt_sp = p.op("scalar", lambda e, half=half: e.dma_start(
                    out=XNT.rearrange("(c p) t -> p c t", p=128)[:, half * 4:(half + 1) * 4, :],
                    in_=xnT[:].rearrange("p (c t) -> p c t", c=NC_)[:, half * 4:(half + 1) * 4, :]),
                    "s_st", 16)
            p.flush()

        if _VERBOSE:
            print("P2 sbuf_remaining", nc.sbuf_bytes_remaining)
        with ExitStack() as ph:
            sb = lambda n, s, d: ph.enter_context(nc.sbuf_tensor(n, s, d))
            ps = lambda n, s, d: ph.enter_context(nc.psum_tensor(n, s, d))
            qp = sb("qp", [128, T], BF16)
            kp = sb("kp", [128, T], BF16)
            vp = sb("vp", [128, NTT * 128], BF16)
            pbA = sb("pbA", [128, T], BF16)
            pbB = sb("pbB", [128, T], BF16)
            pT = sb("pT", [128, NTT * 128], BF16)
            mask = sb("mask_sb", [128, 896], BF16)
            s_A = sb("s_A", [128, T], F16)
            s_B = sb("s_B", [128, T], F16)
            nmpA = sb("nmpA", [128, 4], F32)
            nmpB = sb("nmpB", [128, 4], F32)
            nmA = sb("nmA", [128, 1], F32)
            nmB = sb("nmB", [128, 1], F32)
            lpA = sb("lpA", [128, 4], F32)
            lpB = sb("lpB", [128, 4], F32)
            lA2 = sb("lA2", [128, 1], F32)
            lB2 = sb("lB2", [128, 1], F32)
            li_rA = sb("li_rA", [1, 128], BF16)
            li_rB = sb("li_rB", [1, 128], BF16)
            bc_sb = sb("bc_sb", [128, 128], BF16)
            cat_b = [sb("cat_b0", [128, 128], BF16)]
            w2 = sb("w2", [128, 32 * D], BF16)
            w1blk = [sb(f"w1blk{i}", [128, 2 * NC_ * 128], BF16) for i in range(2)]
            xng = sb("xng", [128, NC_ * GRP], BF16)
            h_sb = sb("h_sb", [128, 32 * GRP], BF16)
            y_sb = sb("y_sb", [128, 4 * D], F32)
            b1f = [sb(f"b1f{i}", [128, 32], F32) for i in range(2)]
            b2s = sb("b2s", [NE, D], BF16)
            ps_sA = ps("ps_sA", [128, 512], F32)
            ps_sB = ps("ps_sB", [128, 512], F32)
            ps_t = [ps(f"ps_t{i}", [128, 1024], BF16) for i in range(2)]
            ps_pv = ps("ps_pv", [128, 512], F32)
            ps_h = [ps(f"ps_h{i}", [128, 512], F32) for i in range(2)]
            ps_oo = ps("ps_oo", [128, 512], F32)

            p.wait("sync", "s_st", p.c("s_st"))
            p.wait("gpsimd", "s_st", p.c("s_st"))
            p.wait("sync", "s_cast", c_wcache)
            c_mask = p.op("sync", lambda e: e.dma_start(out=mask[:], in_=MASK), "s_ld", 16)
            c_b2s = p.op("gpsimd", lambda e: e.dma_start(out=b2s[:], in_=B2), "s_cast", 16)
            c_cat_stores = [0]
            c_y_stores = [0]

            def gen_att():
                relA = ("s_act", 0)
                relB = ("s_dve", 0)
                exp_last = 0
                pb_used = 0
                l_pe = 0
                tr_ev = [("s_dve", 0), ("s_act", 0)]
                pv_ev = 0
                pair_free = 0
                cat_st = [0, 0]
                lw_done = 0
                for pr in range(NH // 2):
                    p.wait("sync", "s_pe", pair_free)
                    p.op("sync", lambda e, pr=pr: e.dma_start(
                        out=qp[:], in_=QT[pr * 128:(pr + 1) * 128, :]), "s_ld", 16)
                    p.op("sync", lambda e, pr=pr: e.dma_start(
                        out=kp[:], in_=KT[pr * 128:(pr + 1) * 128, :]), "s_ld", 16)
                    c_pld = None
                    for half in range(2):
                        c_pld = p.op("sync", lambda e, pr=pr, half=half: e.dma_start(
                            out=vp[:].rearrange("p (tt d) -> p tt d", tt=NTT)[:, half * 16:(half + 1) * 16, :],
                            in_=VH.rearrange("(tt p) d -> p tt d", p=128)[:, half * 16:(half + 1) * 16,
                                pr * 128:(pr + 1) * 128]), "s_ld", 16)
                    for i in range(NTT):
                        jd = i // 4
                        r = i % 4
                        ktiles = i + 1
                        nch = jd + 1
                        ktot = nch * 512
                        for j in range(nch):
                            diag = (j == jd)
                            p.wait("tensor", "s_ld", c_pld)
                            semA, cntA = relA
                            p.wait("tensor", semA, cntA)
                            semB, cntB = relB
                            p.wait("tensor", semB, cntB)
                            p.op("tensor", lambda e, i=i, j=j: e.matmul(
                                ps_sA[:], qp[0:64, i * 128:(i + 1) * 128],
                                kp[0:64, j * 512:(j + 1) * 512],
                                start=True, stop=True, tile_position=(0, 0)))
                            c_s = p.op("tensor", lambda e, i=i, j=j: e.matmul(
                                ps_sB[:], qp[64:128, i * 128:(i + 1) * 128],
                                kp[64:128, j * 512:(j + 1) * 512],
                                start=True, stop=True, tile_position=(64, 0)), "s_pe")
                            if j == 0:
                                p.wait("scalar", "s_act", exp_last)
                                p.wait("vector", "s_act", exp_last)
                            if diag:
                                p.wait("vector", "s_pe", c_s)
                                p.wait("vector", "s_ld", c_mask)
                                m0 = 384 - 128 * r
                                p.op("vector", lambda e, j=j, m0=m0: e.tensor_tensor(
                                    s_A[:, j * 512:(j + 1) * 512], ps_sA[:],
                                    mask[:, m0:m0 + 512], op=ALU.add))
                                c_ev = p.op("vector", lambda e, j=j, m0=m0: e.tensor_tensor(
                                    s_B[:, j * 512:(j + 1) * 512], ps_sB[:],
                                    mask[:, m0:m0 + 512], op=ALU.add), "s_dve")
                                relA = ("s_dve", c_ev)
                                relB = ("s_dve", c_ev)
                            else:
                                p.wait("scalar", "s_pe", c_s)
                                c_evA = p.op("scalar", lambda e, j=j: e.copy(
                                    s_A[:, j * 512:(j + 1) * 512], ps_sA[:]), "s_act")
                                relA = ("s_act", c_evA)
                                p.wait("vector", "s_pe", c_s)
                                c_ev = p.op("vector", lambda e, j=j: e.tensor_copy(
                                    s_B[:, j * 512:(j + 1) * 512], ps_sB[:]), "s_dve")
                                relB = ("s_dve", c_ev)
                            yield 0.5
                        npc = (ktot + 1023) // 1024
                        p.wait("vector", "s_act", p.c("s_act"))
                        p.wait("vector", "s_dve", p.c("s_dve"))
                        c_nmp = None
                        for pc in range(npc):
                            w0 = pc * 1024
                            w1 = min(ktot, w0 + 1024)
                            p.op("vector", lambda e, w0=w0, w1=w1, pc=pc: e.tensor_reduce(
                                nmpA[:, pc:pc + 1], s_A[:, w0:w1], axis=AX.X, op=ALU.max))
                            c_nmp = p.op("vector", lambda e, w0=w0, w1=w1, pc=pc: e.tensor_reduce(
                                nmpB[:, pc:pc + 1], s_B[:, w0:w1], axis=AX.X, op=ALU.max), "s_dve")
                        p.wait("vector", "s_dve", c_nmp)
                        p.op("vector", lambda e, npc=npc: e.tensor_reduce(
                            nmA[:], nmpA[:, 0:npc], axis=AX.X, op=ALU.max))
                        c_nm = p.op("vector", lambda e, npc=npc: e.tensor_reduce(
                            nmB[:], nmpB[:, 0:npc], axis=AX.X, op=ALU.max), "s_dve")
                        p.wait("vector", "s_dve", c_nm)
                        p.op("vector", lambda e: e.tensor_scalar(
                            nmA[:], nmA[:], -0.125, None, op0=ALU.mult))
                        c_nm8 = p.op("vector", lambda e: e.tensor_scalar(
                            nmB[:], nmB[:], -0.125, None, op0=ALU.mult), "s_dve")
                        yield 0.3
                        p.wait("scalar", "s_dve", c_nm8)
                        p.wait("scalar", "s_pe", pb_used)
                        p.wait("scalar", "s_dve", lw_done)
                        c_eB = None
                        for pc in range(npc):
                            w0 = pc * 1024
                            w1 = min(ktot, w0 + 1024)
                            p.op("scalar", lambda e, w0=w0, w1=w1, pc=pc: e.activation(
                                pbA[:, w0:w1], s_A[:, w0:w1], AF.Exp,
                                bias=nmA[:], scale=0.125, accum_out=lpA[:, pc:pc + 1]))
                            c_eB = p.op("scalar", lambda e, w0=w0, w1=w1, pc=pc: e.activation(
                                pbB[:, w0:w1], s_B[:, w0:w1], AF.Exp,
                                bias=nmB[:], scale=0.125, accum_out=lpB[:, pc:pc + 1]), "s_act")
                        exp_last = c_eB
                        p.wait("vector", "s_act", c_eB)
                        p.wait("vector", "s_pe", l_pe)
                        p.op("vector", lambda e, npc=npc: e.reduce_sum(
                            lA2[:], lpA[:, 0:npc], axis=AX.X))
                        c_l = p.op("vector", lambda e, npc=npc: e.reduce_sum(
                            lB2[:], lpB[:, 0:npc], axis=AX.X), "s_dve")
                        lw_done = c_l
                        yield 0.3
                        p.wait("tensor", "s_dve", c_l)
                        p.wait("tensor", "s_dve", pv_ev)
                        p.op("tensor", lambda e: e.transpose(
                            ps_pv[0:1, 256:384], lA2[:], idf[:]))
                        c_lt = p.op("tensor", lambda e: e.transpose(
                            ps_pv[0:1, 384:512], lB2[:], idf[:]), "s_pe")
                        l_pe = c_lt
                        p.wait("vector", "s_pe", c_lt)

                        def _recipA(e):
                            with nc.allow_low_precision(reason="softmax 1/l in bf16"):
                                return e.reciprocal(li_rA[:], ps_pv[0:1, 256:384])

                        def _recipB(e):
                            with nc.allow_low_precision(reason="softmax 1/l in bf16"):
                                return e.reciprocal(li_rB[:], ps_pv[0:1, 384:512])
                        p.op("vector", _recipA)
                        c_li = p.op("vector", _recipB, "s_dve")
                        yield 0.2
                        c_pv = None
                        for hh in range(2):
                            pbuf = pbA if hh == 0 else pbB
                            p.wait("tensor", "s_act", exp_last)
                            last_dve = 0
                            last_act = 0
                            ngrp = (ktiles + 3) // 4
                            for gi in range(ngrp):
                                kt0 = gi * 4
                                kt1 = min(ktiles, kt0 + 4)
                                tb = gi % 2
                                sem_n, cnt = tr_ev[tb]
                                p.wait("tensor", sem_n, cnt)
                                c_tr = None
                                for kt in range(kt0, kt1):
                                    c_tr = p.op("tensor", lambda e, kt=kt, kt0=kt0, tb=tb, pbuf=pbuf: e.transpose(
                                        ps_t[tb][:, (kt - kt0) * 128:(kt - kt0 + 1) * 128],
                                        pbuf[:, kt * 128:(kt + 1) * 128],
                                        idb[:]), "s_pe")
                                if kt1 == ktiles and hh == 1:
                                    pb_used = c_tr
                                eng = "vector" if tb == 0 else "scalar"
                                p.wait(eng, "s_pe", c_tr)
                                w = (kt1 - kt0) * 128
                                if eng == "vector":
                                    c_ev = p.op("vector", lambda e, kt0=kt0, w=w, tb=tb: e.tensor_copy(
                                        pT[:, kt0 * 128:kt0 * 128 + w], ps_t[tb][:, 0:w]), "s_dve")
                                    last_dve = c_ev
                                    tr_ev[tb] = ("s_dve", c_ev)
                                else:
                                    c_ev = p.op("scalar", lambda e, kt0=kt0, w=w, tb=tb: e.copy(
                                        pT[:, kt0 * 128:kt0 * 128 + w], ps_t[tb][:, 0:w]), "s_act")
                                    last_act = c_ev
                                    tr_ev[tb] = ("s_act", c_ev)
                                yield 0.8
                            p.wait("tensor", "s_dve", last_dve)
                            p.wait("tensor", "s_act", last_act)
                            if hh == 0:
                                p.wait("tensor", "s_dve", c_li)
                            off = 0 if hh == 0 else 64
                            tp = (0, 0) if hh == 0 else (0, 64)
                            for kt in range(ktiles):
                                if kt == ktiles - 1:
                                    c_pv = p.op("tensor", lambda e, kt=kt, off=off, tp=tp: e.matmul(
                                        ps_pv[off:off + 64, 0:128],
                                        vp[:, kt * 128 + off:kt * 128 + off + 64],
                                        pT[:, kt * 128:(kt + 1) * 128],
                                        start=(kt == 0), stop=True, tile_position=tp), "s_pe")
                                else:
                                    p.op("tensor", lambda e, kt=kt, off=off, tp=tp: e.matmul(
                                        ps_pv[off:off + 64, 0:128],
                                        vp[:, kt * 128 + off:kt * 128 + off + 64],
                                        pT[:, kt * 128:(kt + 1) * 128],
                                        start=(kt == 0), stop=False, tile_position=tp))
                            yield 0.2 + 0.12 * ktiles
                        p.op("tensor", lambda e: e.matmul(
                            ps_pv[0:64, 128:256], oneb[0:1, 0:64], li_rA[:],
                            start=True, stop=True, tile_position=(0, 0)))
                        c_bc = p.op("tensor", lambda e: e.matmul(
                            ps_pv[64:128, 128:256], oneb[0:1, 0:64], li_rB[:],
                            start=True, stop=True, tile_position=(0, 64)), "s_pe")
                        li_pe = c_bc
                        if i == NTT - 1:
                            pair_free = c_bc
                        cb = 0
                        p.wait("vector", "s_pe", c_bc)
                        c_bce = p.op("vector", lambda e: e.tensor_copy(
                            bc_sb[:], ps_pv[:, 128:256]), "s_dve")
                        p.wait("vector", "s_dve", c_bce)
                        p.wait("vector", "s_st", cat_st[cb])
                        c_cat = p.op("vector", lambda e, cb=cb: e.tensor_tensor(
                            cat_b[cb][:], ps_pv[:, 0:128], bc_sb[:], op=ALU.mult), "s_dve")
                        pv_ev = c_cat
                        p.wait("scalar", "s_dve", c_cat)
                        cat_st[cb] = p.op("scalar", lambda e, pr=pr, i=i, cb=cb: e.dma_start(
                            out=CATH[pr * 128:(pr + 1) * 128, i * 128:(i + 1) * 128],
                            in_=cat_b[cb][:]), "s_st", 16)
                        c_cat_stores[0] = cat_st[cb]
                        yield 0.3

            def gen_moe():
                h_ev = [0, 0]
                o_ev = 0
                i_ev = 0
                w1_ld = [0, 0]
                w1_free = [0, 0]
                w2_ld = 0
                w2_free = 0
                b1_ld = [0, 0]
                b1_free = [0, 0]
                xng_ld = 0
                xng_free = 0
                y_init = [0] * 8
                y_last = [0] * 8
                y_spill_st = 0

                def load_w2(e_):
                    c = None
                    for q4 in range(4):
                        c = p.op("scalar", lambda e, e_=e_, q4=q4: e.dma_start(
                            out=w2[:, q4 * 8 * D:(q4 + 1) * 8 * D],
                            in_=W2C[e_, :, q4 * 8 * D:(q4 + 1) * 8 * D]), "s_st", 16)
                    return c

                def load_w1(e_, qi):
                    rs = qi % 2
                    p.wait("sync", "s_pe", w1_free[rs])
                    w1_ld[rs] = p.op("sync", lambda e, e_=e_, qi=qi, rs=rs: e.dma_start(
                        out=w1blk[rs][:].rearrange("p (m c) -> p m c", m=2),
                        in_=W1C[e_, 2 * qi:2 * qi + 2].rearrange("m p c -> p m c")), "s_ld", 16)

                def emit_y_init(idx, g):
                    nonlocal i_ev
                    tt, n = idx // 2, idx % 2
                    p.wait("tensor", "s_dve", max(i_ev, o_ev))
                    p.wait("tensor", "s_cast", c_b2s)
                    c_mm = p.op("tensor", lambda e, g=g, tt=tt, n=n: e.matmul(
                        ps_oo[:], gatesT[0:NE, (g * 4 + tt) * 128:(g * 4 + tt + 1) * 128],
                        b2s[0:NE, n * 512:(n + 1) * 512], start=True, stop=True), "s_pe")
                    p.wait("vector", "s_pe", c_mm)
                    p.wait("vector", "s_st", y_spill_st)
                    i_ev = p.op("vector", lambda e, tt=tt, n=n: e.tensor_copy(
                        y_sb[:, tt * D + n * 512:tt * D + (n + 1) * 512], ps_oo[:]), "s_dve")
                    y_init[idx] = i_ev

                for g in range(NGRP):
                    p.wait("sync", "s_pe", xng_free)
                    xng_ld = p.op("sync", lambda e, g=g: e.dma_start(
                        out=xng[:].rearrange("p (c t) -> p c t", c=NC_),
                        in_=XNT.rearrange("(c p) t -> p c t", p=128)[:, :, g * GRP:(g + 1) * GRP]),
                        "s_ld", 16)
                    init_q = list(range(8))
                    for e_ in range(NE):
                        bb = e_ % 2
                        p.wait("scalar", "s_pe", w2_free)
                        w2_ld = load_w2(e_)
                        p.wait("sync", "s_act", b1_free[bb])
                        b1_ld[bb] = p.op("sync", lambda e, e_=e_, bb=bb: e.dma_start(
                            out=b1f[bb][:], in_=B1F[e_]), "s_ld", 16)
                        load_w1(e_, 0)
                        load_w1(e_, 1)
                        for mi in range(32):
                            rs = (mi // 2) % 2
                            hb = mi % 2
                            p.wait("tensor", "s_ld", w1_ld[rs])
                            if mi == 0:
                                p.wait("tensor", "s_ld", xng_ld)
                                p.wait("tensor", "s_ld", b1_ld[bb])
                            p.wait("tensor", "s_act", h_ev[hb])
                            c_mm = None
                            for c_ in range(NC_):
                                off = (mi % 2) * (NC_ * 128)
                                if c_ == NC_ - 1:
                                    c_mm = p.op("tensor", lambda e, off=off, c_=c_, rs=rs, hb=hb: e.matmul(
                                        ps_h[hb][:], w1blk[rs][:, off + c_ * 128:off + (c_ + 1) * 128],
                                        xng[:, c_ * GRP:(c_ + 1) * GRP],
                                        start=False, stop=True), "s_pe")
                                else:
                                    p.op("tensor", lambda e, off=off, c_=c_, rs=rs, hb=hb: e.matmul(
                                        ps_h[hb][:], w1blk[rs][:, off + c_ * 128:off + (c_ + 1) * 128],
                                        xng[:, c_ * GRP:(c_ + 1) * GRP],
                                        start=(c_ == 0), stop=False))
                            p.wait("scalar", "s_pe", c_mm)
                            h_ev[hb] = p.op("scalar", lambda e, mi=mi, hb=hb, bb=bb: e.activation(
                                h_sb[:, mi * GRP:(mi + 1) * GRP],
                                ps_h[hb][:], AF.Gelu, bias=b1f[bb][:, mi:mi + 1]), "s_act")
                            if mi % 2 == 1:
                                w1_free[rs] = c_mm
                            xng_free = c_mm
                            if e_ == 0 and mi % 4 == 0 and init_q:
                                emit_y_init(init_q.pop(0), g)
                            if mi % 2 == 1 and mi // 2 + 2 < 16:
                                load_w1(e_, mi // 2 + 2)
                            yield 1.8
                        b1_free[bb] = h_ev[1]
                        c_h_all = p.c("s_act")
                        p.wait("tensor", "s_act", c_h_all)
                        p.wait("tensor", "s_st", w2_ld)
                        for n2 in range(2):
                            for tt in range(4):
                                idx = tt * 2 + n2
                                p.wait("tensor", "s_dve", max(o_ev, i_ev))
                                c_mm = None
                                for ic in range(32):
                                    if ic == 31:
                                        c_mm = p.op("tensor", lambda e, n2=n2, tt=tt, ic=ic: e.matmul(
                                            ps_oo[:], h_sb[:, ic * GRP + tt * 128:ic * GRP + (tt + 1) * 128],
                                            w2[:, ic * D + n2 * 512:ic * D + (n2 + 1) * 512],
                                            start=False, stop=True), "s_pe")
                                    else:
                                        p.op("tensor", lambda e, n2=n2, tt=tt, ic=ic: e.matmul(
                                            ps_oo[:], h_sb[:, ic * GRP + tt * 128:ic * GRP + (tt + 1) * 128],
                                            w2[:, ic * D + n2 * 512:ic * D + (n2 + 1) * 512],
                                            start=(ic == 0), stop=False))
                                if n2 == 1 and tt == 3:
                                    w2_free = c_mm
                                p.wait("vector", "s_pe", c_mm)
                                if e_ == 0:
                                    p.wait("vector", "s_dve", y_init[idx])
                                else:
                                    p.wait("vector", "s_dve", y_last[idx])
                                gcol = (g * 4 + tt) * NE + e_
                                o_ev = y_last[idx] = p.op("vector", lambda e, n2=n2, tt=tt, gcol=gcol: e.scalar_tensor_tensor(
                                    y_sb[:, tt * D + n2 * 512:tt * D + (n2 + 1) * 512],
                                    ps_oo[:], gates[:, gcol:gcol + 1],
                                    y_sb[:, tt * D + n2 * 512:tt * D + (n2 + 1) * 512],
                                    op0=ALU.mult, op1=ALU.add), "s_dve")
                                yield 7.0
                    c_moe_done = p.c("s_dve")
                    p.wait("scalar", "s_dve", c_moe_done)
                    for tt in range(4):
                        y_spill_st = p.op("scalar", lambda e, g=g, tt=tt: e.dma_start(
                            out=YH[(g * 4 + tt) * 128:(g * 4 + tt + 1) * 128, :],
                            in_=y_sb[:, tt * D:(tt + 1) * D]), "s_st", 16)
                    c_y_stores[0] = y_spill_st
                    yield 0.5

            att = iter(()) if _DISABLE_ATT else gen_att()
            moe = iter(()) if _DISABLE_MOE else gen_moe()
            RATIO = 1.55
            cum_a = 0.0
            cum_m = 0.0
            a_alive = True
            m_alive = True
            while a_alive or m_alive:
                if a_alive and (not m_alive or cum_m >= cum_a * RATIO):
                    try:
                        cum_a += next(att)
                    except StopIteration:
                        a_alive = False
                else:
                    try:
                        cum_m += next(moe)
                    except StopIteration:
                        m_alive = False
            p.flush()

        with ExitStack() as ph:
            sb = lambda n, s, d: ph.enter_context(nc.sbuf_tensor(n, s, d))
            ps = lambda n, s, d: ph.enter_context(nc.psum_tensor(n, s, d))
            cat = sb("cat", [128, NC_ * T], BF16)
            wo = sb("wo", [128, NC_ * D], BF16)
            bor = sb("bor", [1, D], BF16)
            at_f = [sb(f"at_f{i}", [128, D], F32) for i in range(2)]
            x_t = [sb(f"x_t{i}", [128, D], F32) for i in range(2)]
            y_t = [sb(f"y_t{i}", [128, D], F32) for i in range(2)]
            o_t = [sb(f"o_t{i}", [128, D], F32) for i in range(2)]
            ps_o = [ps(f"ps_o{i}", [128, 512], F32) for i in range(2)]

            p.wait("sync", "s_st", p.c("s_st"))
            p.wait("gpsimd", "s_st", p.c("s_st"))
            p.op("gpsimd", lambda e: e.dma_start(
                out=wo[:].rearrange("p (c m) -> p c m", c=NC_),
                in_=WOT.rearrange("(c p) m -> p c m", p=128)), "s_cast", 16)
            c_bor = p.op("gpsimd", lambda e: e.dma_start(out=bor[:], in_=BO), "s_cast", 16)
            c_cat_ld = 0
            for half in range(2):
                c_cat_ld = p.op("sync", lambda e, half=half: e.dma_start(
                    out=cat[:].rearrange("p (c t) -> p c t", c=NC_)[:, half * 4:(half + 1) * 4, :],
                    in_=CATH.rearrange("(c p) t -> p c t", p=128)[:, half * 4:(half + 1) * 4, :]),
                    "s_ld", 16)
            at_ev = [0, 0]
            at_last = 0
            xy_free = [0, 0]
            out_st = [0, 0]
            p.wait("tensor", "s_ld", c_cat_ld)
            p.wait("tensor", "s_cast", c_bor)
            for tt in range(NTT):
                tb = tt % 2
                p.wait("sync", "s_dve", xy_free[tb])
                p.op("sync", lambda e, tt=tt, tb=tb: e.dma_start(
                    out=x_t[tb][:], in_=X[tt * 128:(tt + 1) * 128, :]), "s_ld", 16)
                c_xy = p.op("sync", lambda e, tt=tt, tb=tb: e.dma_start(
                    out=y_t[tb][:], in_=YH[tt * 128:(tt + 1) * 128, :]), "s_ld", 16)
                c_mm2 = [None, None]
                for n in range(2):
                    pb = n
                    p.wait("tensor", "s_dve", at_last)
                    p.op("tensor", lambda e, n=n, pb=pb: e.matmul(
                        ps_o[pb][:], oneb[:], bor[:, n * 512:(n + 1) * 512],
                        start=True, stop=False))
                    for c_ in range(NC_):
                        if c_ == NC_ - 1:
                            c_mm2[n] = p.op("tensor", lambda e, tt=tt, n=n, c_=c_, pb=pb: e.matmul(
                                ps_o[pb][:], cat[:, c_ * T + tt * 128:c_ * T + (tt + 1) * 128],
                                wo[:, c_ * D + n * 512:c_ * D + (n + 1) * 512],
                                start=False, stop=True), "s_pe")
                        else:
                            p.op("tensor", lambda e, tt=tt, n=n, c_=c_, pb=pb: e.matmul(
                                ps_o[pb][:], cat[:, c_ * T + tt * 128:c_ * T + (tt + 1) * 128],
                                wo[:, c_ * D + n * 512:c_ * D + (n + 1) * 512],
                                start=False, stop=False))
                p.wait("vector", "s_pe", c_mm2[1])
                p.op("vector", lambda e, tb=tb: e.tensor_copy(
                    at_f[tb][:, 0:512], ps_o[0][:]))
                c_ate = p.op("vector", lambda e, tb=tb: e.tensor_copy(
                    at_f[tb][:, 512:1024], ps_o[1][:]), "s_dve")
                at_ev[tb] = c_ate
                at_last = c_ate
                p.wait("vector", "s_ld", c_xy)
                p.wait("vector", "s_st", out_st[tb])
                c_s1 = p.op("vector", lambda e, tb=tb: e.tensor_tensor(
                    o_t[tb][:], x_t[tb][:], y_t[tb][:], op=ALU.add), "s_dve")
                p.wait("vector", "s_dve", c_s1)
                c_o = p.op("vector", lambda e, tb=tb: e.tensor_tensor(
                    o_t[tb][:], o_t[tb][:], at_f[tb][:], op=ALU.add), "s_dve")
                xy_free[tb] = c_o
                p.wait("scalar", "s_dve", c_o)
                out_st[tb] = p.op("scalar", lambda e, tt=tt, tb=tb: e.dma_start(
                    out=OUT[tt * 128:(tt + 1) * 128, :], in_=o_t[tb][:]), "s_st", 16)
            if _DEBUG_TAPS:
                p.wait("gpsimd", "s_st", p.c("s_st"))
                p.wait("gpsimd", "s_dve", p.c("s_dve"))
                for tname in _DEBUG_TAPS:
                    ap, shape = scratch_regs[tname]
                    dbg = nc.dram_tensor("dbg_" + tname, list(shape), F32,
                                         kind="ExternalOutput").ap()
                    p.op("gpsimd", lambda e, dbg=dbg, ap=ap: e.dma_start(
                        out=dbg, in_=ap), "s_cast", 16)
            p.wait("scalar", "s_st", p.c("s_st"))
            p.wait("gpsimd", "s_cast", p.c("s_cast"))
            p.flush()

    return nc


def _host_prep(L0_attn_W_q, L0_attn_b_q, L0_attn_W_k, L0_attn_b_k, L0_attn_W_v,
               L0_attn_W_o, L0_ffn_Wg, L0_ffn_W1, L0_ffn_b1, L0_ffn_W2):
    perm = []
    for h in range(NH):
        base = DH * h
        perm += [base + 2 * j for j in range(32)] + [base + 2 * j + 1 for j in range(32)]
    perm = np.array(perm)
    ac = np.ascontiguousarray
    Wq = np.asarray(L0_attn_W_q)
    Wk = np.asarray(L0_attn_W_k)
    common = {}
    common["wqt"] = ac(Wq.T[:, perm])
    common["wkt"] = ac(Wk.T[:, perm])
    common["wvt"] = ac(np.asarray(L0_attn_W_v).T)
    common["wot"] = ac(np.asarray(L0_attn_W_o).T)
    common["wgt"] = ac(np.asarray(L0_ffn_Wg).T)
    common["bq"] = ac(np.asarray(L0_attn_b_q)[perm].reshape(NC_, 128).T)
    common["bk"] = ac(np.asarray(L0_attn_b_k)[perm].reshape(NC_, 128).T)
    common["w1p"] = ac(np.asarray(L0_ffn_W1).reshape(NE, NC_, 128, 32, 128)
                       .transpose(0, 3, 2, 1, 4).reshape(NE, 32, 128, NC_ * 128))
    common["w2p"] = ac(np.asarray(L0_ffn_W2).reshape(NE, 32, 128, D)
                       .transpose(0, 2, 1, 3).reshape(NE, 128, 32 * D))
    common["b1f"] = ac(np.asarray(L0_ffn_b1).reshape(NE, 32, 128).transpose(0, 2, 1))
    j = np.arange(32, dtype=np.float64)
    theta = 10000.0 ** (-j / 32.0)
    t = np.arange(T, dtype=np.float64)
    ang = t[None, :] * theta[:, None]
    cosv = np.cos(ang)
    sinv = np.sin(ang)
    ccr = np.tile(cosv, (4, 1))
    ssr = np.concatenate([-sinv, sinv, -sinv, sinv], axis=0)
    common["cc"] = ccr.astype(BF)
    common["ss"] = ssr.astype(BF)
    psw = np.zeros((128, 128), dtype=BF)
    for pp in range(128):
        jj = pp // 32
        swp = pp + 32 if jj % 2 == 0 else pp - 32
        psw[swp, pp] = 1.0
    common["psw"] = psw
    common["idb"] = np.eye(128, dtype=BF)
    common["idf"] = np.eye(128, dtype=np.float32)
    common["oneb"] = np.ones((1, 128), dtype=BF)
    common["onef"] = np.ones((1, 128), dtype=np.float32)
    qq = np.arange(128)[:, None]
    cc2 = np.arange(896)[None, :]
    m = np.where(cc2 <= qq + 384, 0.0, NEG).astype(np.float32)
    common["mask"] = m.astype(BF)
    return common


def kernel(L0_attn_x, L0_attn_W_q, L0_attn_b_q, L0_attn_W_k, L0_attn_b_k,
           L0_attn_W_v, L0_attn_b_v, L0_attn_W_o, L0_attn_b_o,
           L0_ffn_Wg, L0_ffn_W1, L0_ffn_b1, L0_ffn_W2, L0_ffn_b2):
    common = _host_prep(L0_attn_W_q, L0_attn_b_q, L0_attn_W_k, L0_attn_b_k,
                        L0_attn_W_v, L0_attn_W_o, L0_ffn_Wg, L0_ffn_W1,
                        L0_ffn_b1, L0_ffn_W2)
    common["bv"] = np.ascontiguousarray(np.asarray(L0_attn_b_v, dtype=np.float32)[None, :])
    common["bo"] = np.ascontiguousarray(np.asarray(L0_attn_b_o, dtype=np.float32)[None, :])
    common["b2"] = np.ascontiguousarray(np.asarray(L0_ffn_b2, dtype=np.float32))
    common = {k: (np.asarray(v, dtype=np.float32) if np.asarray(v).dtype == np.float64 else np.asarray(v))
              for k, v in common.items()}
    x = np.asarray(L0_attn_x, dtype=np.float32)
    in_maps = []
    for c in range(8):
        m = dict(common)
        m["x"] = np.ascontiguousarray(x[c])
        in_maps.append(m)
    nc = build_program()
    res = run_bass_kernel_spmd(nc, in_maps, core_ids=list(range(8)))
    out = np.stack([np.asarray(res.results[c]["out"]) for c in range(8)], axis=0)
    return out


# version 19
# speedup vs baseline: 1.2235x; 1.1340x over previous
import numpy as np
import ml_dtypes
from contextlib import ExitStack
import concourse.bass as bass
import concourse.mybir as mybir
from concourse.bass_utils import run_bass_kernel_spmd

F32 = mybir.dt.float32
F16 = mybir.dt.float16
BF16 = mybir.dt.bfloat16
AF = mybir.ActivationFunctionType
ALU = mybir.AluOpType
AX = mybir.AxisListType
BF = ml_dtypes.bfloat16

D = 1024
T = 4096
NH = 16
DH = 64
NE = 8
INTER = 4096
NC_ = 8
NTT = 32
NGRP = 4
GRP = 1024
NEG = -1.0e9

_DEBUG_TAPS = []
_VERBOSE = False


class Prog:
    ENGINES = ("sync", "tensor", "vector", "scalar", "gpsimd")

    def __init__(self, nc):
        self.nc = nc
        self.streams = {e: [] for e in self.ENGINES}
        self.sems = {}
        self.cnt = {}
        self.last_wait = {}

    def sem(self, name, stack):
        h = stack.enter_context(self.nc.semaphore(name))
        self.sems[name] = h
        self.cnt[name] = 0

    def wait(self, eng, sem, val=None):
        v = self.cnt[sem] if val is None else val
        if v <= 0:
            return
        key = (eng, sem)
        if self.last_wait.get(key, -1) >= v:
            return
        self.last_wait[key] = v
        h = self.sems[sem]
        self.streams[eng].append(lambda e, h=h, v=v: e.wait_ge(h, v))

    def op(self, eng, build, inc=None, by=1):
        if inc is None:
            self.streams[eng].append(lambda e, b=build: b(e))
            return None
        h = self.sems[inc]
        self.cnt[inc] += by
        self.streams[eng].append(lambda e, b=build, h=h, by=by: b(e).then_inc(h, by))
        return self.cnt[inc]

    def c(self, sem):
        return self.cnt[sem]

    def flush(self):
        nc = self.nc
        streams = self.streams
        self.streams = {e: [] for e in self.ENGINES}
        with nc.Block() as block:
            if streams["sync"]:
                @block.sync
                def _(e, ops=streams["sync"]):
                    for f in ops:
                        f(e)
            if streams["tensor"]:
                @block.tensor
                def _(e, ops=streams["tensor"]):
                    for f in ops:
                        f(e)
            if streams["vector"]:
                @block.vector
                def _(e, ops=streams["vector"]):
                    for f in ops:
                        f(e)
            if streams["scalar"]:
                @block.scalar
                def _(e, ops=streams["scalar"]):
                    for f in ops:
                        f(e)
            if streams["gpsimd"]:
                @block.gpsimd
                def _(e, ops=streams["gpsimd"]):
                    for f in ops:
                        f(e)


def build_program():
    nc = bass.Bass("TRN2", target_bir_lowering=False)

    scratch_regs = {}

    def dram(name, shape, dt, kind):
        ap = nc.dram_tensor(name, shape, dt, kind=kind).ap()
        if kind == "Internal":
            scratch_regs[name] = (ap, shape)
        return ap

    X = dram("x", [T, D], F32, "ExternalInput")
    WQT = dram("wqt", [D, D], F32, "ExternalInput")
    WKT = dram("wkt", [D, D], F32, "ExternalInput")
    WVT = dram("wvt", [D, D], F32, "ExternalInput")
    WOT = dram("wot", [D, D], F32, "ExternalInput")
    WGT = dram("wgt", [D, NE], F32, "ExternalInput")
    BQ = dram("bq", [128, NC_], F32, "ExternalInput")
    BK = dram("bk", [128, NC_], F32, "ExternalInput")
    BV = dram("bv", [1, D], F32, "ExternalInput")
    BO = dram("bo", [1, D], F32, "ExternalInput")
    W1P = dram("w1p", [NE, 32, 128, NC_ * 128], F32, "ExternalInput")
    W2P = dram("w2p", [NE, 128, 32 * D], F32, "ExternalInput")
    B1F = dram("b1f", [NE, 128, 32], F32, "ExternalInput")
    B2 = dram("b2", [NE, D], F32, "ExternalInput")
    CC = dram("cc", [128, T], BF16, "ExternalInput")
    SS = dram("ss", [128, T], BF16, "ExternalInput")
    PSW = dram("psw", [128, 128], BF16, "ExternalInput")
    IDB = dram("idb", [128, 128], BF16, "ExternalInput")
    IDF = dram("idf", [128, 128], F32, "ExternalInput")
    ONEB = dram("oneb", [1, 128], BF16, "ExternalInput")
    ONEF = dram("onef", [1, 128], F32, "ExternalInput")
    MASK = dram("mask", [128, 4 * 512], F32, "ExternalInput")

    OUT = dram("out", [T, D], F32, "ExternalOutput")

    XNT = dram("xnt_h", [D, T], BF16, "Internal")
    QT = dram("qt_h", [D, T], BF16, "Internal")
    KT = dram("kt_h", [D, T], BF16, "Internal")
    VH = dram("v_h", [T, D], BF16, "Internal")
    CATH = dram("cat_h", [D, T], BF16, "Internal")
    ATTH = dram("att_h", [T, D], F32, "Internal")
    W1C = dram("w1c", [NE, 32, 128, NC_ * 128], BF16, "Internal")
    W2C = dram("w2c", [NE, 128, 32 * D], BF16, "Internal")

    with ExitStack() as outer:
        p = Prog(nc)
        for s in ["s_ld", "s_cast", "s_st", "s_pe", "s_act", "s_dve", "s_pl"]:
            p.sem(s, outer)

        idb = outer.enter_context(nc.sbuf_tensor("idb_sb", [128, 128], BF16))
        idf = outer.enter_context(nc.sbuf_tensor("idf_sb", [128, 128], F32))
        oneb = outer.enter_context(nc.sbuf_tensor("oneb_sb", [1, 128], BF16))
        onef = outer.enter_context(nc.sbuf_tensor("onef_sb", [1, 128], F32))
        gates = outer.enter_context(nc.sbuf_tensor("gates_sb", [128, NTT * NE], F32))
        gatesT = outer.enter_context(nc.sbuf_tensor("gatesT_sb", [NE, T], BF16))

        if _VERBOSE:
            print("P01 sbuf_remaining", nc.sbuf_bytes_remaining)
        with ExitStack() as ph:
            sb = lambda n, s, d: ph.enter_context(nc.sbuf_tensor(n, s, d))
            ps = lambda n, s, d: ph.enter_context(nc.psum_tensor(n, s, d))
            x_b = [sb(f"x_b{i}", [128, D], F32) for i in range(2)]
            sq = [sb(f"sq{i}", [128, D], F32) for i in range(2)]
            s2 = [sb(f"s2_{i}", [128, 1], F32) for i in range(2)]
            nsum = [sb(f"nsum{i}", [128, 1], F32) for i in range(2)]
            m_neg = [sb(f"mneg{i}", [128, 1], F32) for i in range(2)]
            m2t = [sb(f"m2t{i}", [128, 1], F32) for i in range(2)]
            var = [sb(f"var{i}", [128, 1], F32) for i in range(2)]
            rstd = [sb(f"rstd{i}", [128, 1], F32) for i in range(2)]
            xn_b = [sb(f"xn_b{i}", [128, D], BF16) for i in range(2)]
            xnT = sb("xnT", [128, NC_ * T], BF16)
            wq = sb("wq", [128, NC_ * D], BF16)
            wk = sb("wk", [128, NC_ * D], BF16)
            wv = sb("wv", [128, NC_ * D], BF16)
            wg = sb("wg", [128, NC_ * NE], BF16)
            bqf = sb("bqf", [128, NC_], F32)
            bkf = sb("bkf", [128, NC_], F32)
            bvr = sb("bvr", [1, D], BF16)
            cc = sb("cc_sb", [128, T], BF16)
            ss = sb("ss_sb", [128, T], BF16)
            psw = sb("psw_sb", [128, 128], BF16)
            q_sb = [sb(f"q_sb{i}", [128, 512], BF16) for i in range(2)]
            t1b = [sb(f"t1b{i}", [128, 512], BF16) for i in range(2)]
            t2b = [sb(f"t2b{i}", [128, 512], BF16) for i in range(2)]
            rope_b = [sb(f"rope{i}", [128, 512], BF16) for i in range(2)]
            v_b = [sb(f"v_b{i}", [128, 512], BF16) for i in range(2)]
            logT = sb("logT", [NE, T], F32)
            lt_b = sb("lt_b", [128, NE], F32)
            g_m1 = sb("g_m1", [128, 1], F32)
            g_m2 = sb("g_m2", [128, 1], F32)
            g_d12 = sb("g_d12", [128, 1], F32)
            g_w1 = sb("g_w1", [128, 1], F32)
            g_w2 = sb("g_w2", [128, 1], F32)
            g_is1 = sb("g_is1", [128, NE], F32)
            g_is2 = sb("g_is2", [128, NE], F32)
            g_msk = sb("g_msk", [128, NE], F32)
            g_g1 = sb("g_g1", [128, NE], F32)
            gt_b = sb("gt_b", [128, NE], F32)
            ps_tr = [ps(f"ps_tr{i}", [128, 1024], BF16) for i in range(2)]
            ps_pr = [ps(f"ps_pr{i}", [128, 512], F32) for i in range(2)]
            ps_sw = [ps(f"ps_sw{i}", [128, 512], F32) for i in range(2)]
            ps_g = ps("ps_g", [128, 512], F32)

            p.op("sync", lambda e: e.dma_start(out=idb[:], in_=IDB), "s_ld", 16)
            p.op("sync", lambda e: e.dma_start(out=idf[:], in_=IDF), "s_ld", 16)
            p.op("sync", lambda e: e.dma_start(out=oneb[:], in_=ONEB), "s_ld", 16)
            p.op("sync", lambda e: e.dma_start(out=onef[:], in_=ONEF), "s_ld", 16)
            p.op("sync", lambda e: e.dma_start(out=cc[:], in_=CC), "s_ld", 16)
            p.op("sync", lambda e: e.dma_start(out=ss[:], in_=SS), "s_ld", 16)
            p.op("sync", lambda e: e.dma_start(out=psw[:], in_=PSW), "s_ld", 16)
            p.op("sync", lambda e: e.dma_start(out=bqf[:], in_=BQ), "s_ld", 16)
            c_tabs = p.op("sync", lambda e: e.dma_start(out=bkf[:], in_=BK), "s_ld", 16)
            for w_sb_, W_ in [(wq, WQT), (wk, WKT), (wv, WVT)]:
                p.op("gpsimd", lambda e, o=w_sb_, i=W_: e.dma_start(
                    out=o[:].rearrange("p (c m) -> p c m", c=NC_),
                    in_=i.rearrange("(c p) m -> p c m", p=128)), "s_cast", 16)
            p.op("gpsimd", lambda e: e.dma_start(
                out=wg[:].rearrange("p (c m) -> p c m", c=NC_),
                in_=WGT.rearrange("(c p) m -> p c m", p=128)), "s_cast", 16)
            c_wproj = p.op("gpsimd", lambda e: e.dma_start(out=bvr[:], in_=BV), "s_cast", 16)
            for e_ in range(NE):
                for q4 in range(4):
                    sl = slice(q4 * 8, q4 * 8 + 8)
                    p.op("gpsimd", lambda e, e_=e_, sl=sl: e.dma_start(
                        out=W1C[e_, sl], in_=W1P[e_, sl]), "s_cast", 16)
                    sl2 = slice(q4 * 32, q4 * 32 + 32)
                    p.op("gpsimd", lambda e, e_=e_, sl2=sl2: e.dma_start(
                        out=W2C[e_, sl2], in_=W2P[e_, sl2]), "s_cast", 16)
            c_wcache = p.c("s_cast")

            ln_done = [0, 0]
            tr_done = [0, 0]
            xn_done = [0, 0]
            for t in range(NTT):
                b = t % 2
                p.wait("sync", "s_dve", ln_done[b])
                c_x = p.op("sync", lambda e, t=t, b=b: e.dma_start(
                    out=x_b[b][:], in_=X[t * 128:(t + 1) * 128, :]), "s_ld", 16)
                p.wait("scalar", "s_ld", c_x)
                p.wait("scalar", "s_dve", ln_done[b])
                c_sq = p.op("scalar", lambda e, b=b: e.activation(
                    sq[b][:], x_b[b][:], AF.Square, accum_out=s2[b][:]), "s_act")
                p.wait("vector", "s_ld", c_x)
                c_ns = p.op("vector", lambda e, b=b: e.tensor_reduce(
                    nsum[b][:], x_b[b][:], axis=AX.X, op=ALU.add, negate=True), "s_dve")
                p.wait("vector", "s_dve", c_ns)
                p.op("vector", lambda e, b=b: e.tensor_scalar(
                    m_neg[b][:], nsum[b][:], 1.0 / D, None, op0=ALU.mult))
                c_mn = p.op("vector", lambda e, b=b: e.scalar_tensor_tensor(
                    m2t[b][:], nsum[b][:], 1.0 / (D * D), nsum[b][:],
                    op0=ALU.mult, op1=ALU.mult), "s_dve")
                p.wait("vector", "s_act", c_sq)
                p.wait("vector", "s_dve", c_mn)
                c_va = p.op("vector", lambda e, b=b: e.tensor_scalar(
                    var[b][:], s2[b][:], 1.0 / D, 1e-5, op0=ALU.mult, op1=ALU.add), "s_dve")
                p.wait("vector", "s_dve", c_va)
                c_var = p.op("vector", lambda e, b=b: e.tensor_tensor(
                    var[b][:], var[b][:], m2t[b][:], op=ALU.subtract), "s_dve")
                p.wait("scalar", "s_dve", c_var)
                c_sqr = p.op("scalar", lambda e, b=b: e.activation(
                    rstd[b][:], var[b][:], AF.Sqrt), "s_act")
                p.wait("vector", "s_act", c_sqr)
                c_rs = p.op("vector", lambda e, b=b: e.reciprocal(
                    rstd[b][:], rstd[b][:]), "s_dve")
                p.wait("vector", "s_dve", c_rs)
                p.wait("vector", "s_pe", xn_done[b])
                c_xn = p.op("vector", lambda e, b=b: e.tensor_scalar(
                    xn_b[b][:], x_b[b][:], m_neg[b][:], rstd[b][:],
                    op0=ALU.add, op1=ALU.mult), "s_dve")
                ln_done[b] = c_xn
                p.wait("tensor", "s_dve", c_xn)
                p.wait("tensor", "s_ld", c_tabs)
                for c_ in range(NC_):
                    tb = c_ % 2
                    p.wait("tensor", "s_dve", tr_done[tb])
                    c_tr = p.op("tensor", lambda e, b=b, c_=c_, tb=tb: e.transpose(
                        ps_tr[tb][:, 0:128], xn_b[b][:, c_ * 128:(c_ + 1) * 128],
                        idb[:]), "s_pe")
                    if c_ == NC_ - 1:
                        xn_done[b] = c_tr
                    p.wait("vector", "s_pe", c_tr)
                    tr_done[tb] = p.op("vector", lambda e, t=t, c_=c_, tb=tb: e.tensor_copy(
                        xnT[:, c_ * T + t * 128: c_ * T + (t + 1) * 128],
                        ps_tr[tb][:, 0:128]), "s_dve")
            c_xnT_all = p.c("s_dve")

            pr_done = [0, 0]
            sw_done = [0, 0]
            qs_done = [0, 0]
            rope_st = [0, 0]
            p.wait("tensor", "s_cast", c_wproj)
            p.wait("tensor", "s_dve", c_xnT_all)
            for (w_sb, b_fm, OUT_T) in [(wq, bqf, QT), (wk, bkf, KT)]:
                for m in range(NC_):
                    for g in range(NC_):
                        pb = (m * NC_ + g) % 2
                        p.wait("tensor", "s_dve", pr_done[pb])
                        c_mm = None
                        for c_ in range(NC_):
                            if c_ == NC_ - 1:
                                c_mm = p.op("tensor", lambda e, w_sb=w_sb, m=m, g=g, c_=c_, pb=pb: e.matmul(
                                    ps_pr[pb][:], w_sb[:, c_ * D + m * 128:c_ * D + (m + 1) * 128],
                                    xnT[:, c_ * T + g * 512:c_ * T + (g + 1) * 512],
                                    start=False, stop=True), "s_pe")
                            else:
                                p.op("tensor", lambda e, w_sb=w_sb, m=m, g=g, c_=c_, pb=pb: e.matmul(
                                    ps_pr[pb][:], w_sb[:, c_ * D + m * 128:c_ * D + (m + 1) * 128],
                                    xnT[:, c_ * T + g * 512:c_ * T + (g + 1) * 512],
                                    start=(c_ == 0), stop=False))
                        p.wait("vector", "s_pe", c_mm)
                        p.wait("vector", "s_pe", qs_done[pb])
                        p.wait("vector", "s_dve", sw_done[pb])
                        c_ev = p.op("vector", lambda e, b_fm=b_fm, m=m, pb=pb: e.tensor_scalar(
                            q_sb[pb][:], ps_pr[pb][:], b_fm[:, m:m + 1], None,
                            op0=ALU.add), "s_dve")
                        pr_done[pb] = c_ev
                        p.wait("tensor", "s_dve", c_ev)
                        p.wait("tensor", "s_dve", sw_done[pb])
                        c_sw = p.op("tensor", lambda e, pb=pb: e.matmul(
                            ps_sw[pb][:], psw[:], q_sb[pb][:], start=True, stop=True), "s_pe")
                        qs_done[pb] = c_sw
                        p.wait("vector", "s_dve", c_ev)
                        p.op("vector", lambda e, g=g, pb=pb: e.tensor_tensor(
                            t1b[pb][:], q_sb[pb][:], cc[:, g * 512:(g + 1) * 512], op=ALU.mult))
                        p.wait("vector", "s_pe", c_sw)
                        c_t2 = p.op("vector", lambda e, g=g, pb=pb: e.tensor_tensor(
                            t2b[pb][:], ps_sw[pb][:], ss[:, g * 512:(g + 1) * 512], op=ALU.mult), "s_dve")
                        sw_done[pb] = c_t2
                        p.wait("vector", "s_dve", c_t2)
                        p.wait("vector", "s_st", rope_st[pb])
                        c_rp = p.op("vector", lambda e, pb=pb: e.tensor_tensor(
                            rope_b[pb][:], t1b[pb][:], t2b[pb][:], op=ALU.add), "s_dve")
                        p.wait("scalar", "s_dve", c_rp)
                        rope_st[pb] = p.op("scalar", lambda e, OUT_T=OUT_T, m=m, g=g, pb=pb: e.dma_start(
                            out=OUT_T[m * 128:(m + 1) * 128, g * 512:(g + 1) * 512],
                            in_=rope_b[pb][:]), "s_st", 16)

            v_ev_done = [0, 0]
            v_st = [0, 0]
            for tt in range(NTT):
                for n in range(2):
                    pb = (tt * 2 + n) % 2
                    p.wait("tensor", "s_dve", v_ev_done[pb])
                    p.wait("tensor", "s_dve", pr_done[pb])
                    p.op("tensor", lambda e, n=n, pb=pb: e.matmul(
                        ps_pr[pb][:], oneb[:], bvr[:, n * 512:(n + 1) * 512],
                        start=True, stop=False))
                    for c_ in range(NC_):
                        if c_ == NC_ - 1:
                            c_mm = p.op("tensor", lambda e, tt=tt, n=n, c_=c_, pb=pb: e.matmul(
                                ps_pr[pb][:], xnT[:, c_ * T + tt * 128:c_ * T + (tt + 1) * 128],
                                wv[:, c_ * D + n * 512:c_ * D + (n + 1) * 512],
                                start=False, stop=True), "s_pe")
                        else:
                            p.op("tensor", lambda e, tt=tt, n=n, c_=c_, pb=pb: e.matmul(
                                ps_pr[pb][:], xnT[:, c_ * T + tt * 128:c_ * T + (tt + 1) * 128],
                                wv[:, c_ * D + n * 512:c_ * D + (n + 1) * 512],
                                start=False, stop=False))
                    p.wait("vector", "s_pe", c_mm)
                    p.wait("vector", "s_st", v_st[pb])
                    c_ev = p.op("vector", lambda e, pb=pb: e.tensor_copy(
                        v_b[pb][:], ps_pr[pb][:]), "s_dve")
                    v_ev_done[pb] = c_ev
                    p.wait("scalar", "s_dve", c_ev)
                    v_st[pb] = p.op("scalar", lambda e, tt=tt, n=n, pb=pb: e.dma_start(
                        out=VH[tt * 128:(tt + 1) * 128, n * 512:(n + 1) * 512],
                        in_=v_b[pb][:]), "s_st", 16)

            c_lg = 0
            for g in range(NC_):
                p.wait("tensor", "s_dve", c_lg)
                for c_ in range(NC_):
                    if c_ == NC_ - 1:
                        c_mm = p.op("tensor", lambda e, g=g, c_=c_: e.matmul(
                            ps_g[0:NE, 0:512], wg[:, c_ * NE:(c_ + 1) * NE],
                            xnT[:, c_ * T + g * 512:c_ * T + (g + 1) * 512],
                            start=False, stop=True), "s_pe")
                    else:
                        p.op("tensor", lambda e, g=g, c_=c_: e.matmul(
                            ps_g[0:NE, 0:512], wg[:, c_ * NE:(c_ + 1) * NE],
                            xnT[:, c_ * T + g * 512:c_ * T + (g + 1) * 512],
                            start=(c_ == 0), stop=False))
                p.wait("vector", "s_pe", c_mm)
                c_lg = p.op("vector", lambda e, g=g: e.tensor_copy(
                    logT[0:NE, g * 512:(g + 1) * 512], ps_g[0:NE, 0:512]), "s_dve")
            for tt in range(NTT):
                p.wait("tensor", "s_dve", c_lg)
                c_tr = p.op("tensor", lambda e, tt=tt: e.transpose(
                    ps_g[:, 0:NE], logT[0:NE, tt * 128:(tt + 1) * 128],
                    idf[0:NE, 0:NE]), "s_pe")
                p.wait("vector", "s_pe", c_tr)
                c_lt = p.op("vector", lambda e: e.tensor_copy(
                    lt_b[:], ps_g[:, 0:NE]), "s_dve")
                p.wait("vector", "s_dve", c_lt)
                c_m1 = p.op("vector", lambda e: e.tensor_reduce(
                    g_m1[:], lt_b[:], axis=AX.X, op=ALU.max), "s_dve")
                p.wait("vector", "s_dve", c_m1)
                c_is1 = p.op("vector", lambda e: e.tensor_scalar(
                    g_is1[:], lt_b[:], g_m1[:], None, op0=ALU.is_ge), "s_dve")
                p.wait("vector", "s_dve", c_is1)
                c_msk = p.op("vector", lambda e: e.scalar_tensor_tensor(
                    g_msk[:], g_is1[:], NEG, lt_b[:], op0=ALU.mult, op1=ALU.add), "s_dve")
                p.wait("vector", "s_dve", c_msk)
                c_m2 = p.op("vector", lambda e: e.tensor_reduce(
                    g_m2[:], g_msk[:], axis=AX.X, op=ALU.max), "s_dve")
                p.wait("vector", "s_dve", c_m2)
                c_d = p.op("vector", lambda e: e.tensor_tensor(
                    g_d12[:], g_m1[:], g_m2[:], op=ALU.subtract), "s_dve")
                p.wait("scalar", "s_dve", c_d)
                p.op("scalar", lambda e: e.activation(g_w1[:], g_d12[:], AF.Sigmoid))
                c_w2 = p.op("scalar", lambda e: e.activation(
                    g_w2[:], g_d12[:], AF.Sigmoid, scale=-1.0), "s_act")
                c_is2 = p.op("vector", lambda e: e.tensor_scalar(
                    g_is2[:], g_msk[:], g_m2[:], None, op0=ALU.is_ge), "s_dve")
                p.wait("vector", "s_act", c_w2)
                p.wait("vector", "s_dve", c_is2)
                c_g1 = p.op("vector", lambda e: e.tensor_scalar(
                    g_g1[:], g_is1[:], g_w1[:], None, op0=ALU.mult), "s_dve")
                p.wait("vector", "s_dve", c_g1)
                c_gt = p.op("vector", lambda e, tt=tt: e.scalar_tensor_tensor(
                    gates[:, tt * NE:(tt + 1) * NE], g_is2[:], g_w2[:], g_g1[:],
                    op0=ALU.mult, op1=ALU.add), "s_dve")
                p.wait("vector", "s_dve", c_gt)
                c_gtb = p.op("vector", lambda e, tt=tt: e.tensor_copy(
                    gt_b[:], gates[:, tt * NE:(tt + 1) * NE]), "s_dve")
                p.wait("tensor", "s_dve", c_gtb)
                c_tr2 = p.op("tensor", lambda e: e.transpose(
                    ps_g[0:NE, 0:128], gt_b[:], idf[:]), "s_pe")
                p.wait("vector", "s_pe", c_tr2)
                c_lg = p.op("vector", lambda e, tt=tt: e.tensor_copy(
                    gatesT[0:NE, tt * 128:(tt + 1) * 128], ps_g[0:NE, 0:128]), "s_dve")

            p.wait("scalar", "s_dve", c_xnT_all)
            c_xnt_sp = 0
            for half in range(2):
                c_xnt_sp = p.op("scalar", lambda e, half=half: e.dma_start(
                    out=XNT.rearrange("(c p) t -> p c t", p=128)[:, half * 4:(half + 1) * 4, :],
                    in_=xnT[:].rearrange("p (c t) -> p c t", c=NC_)[:, half * 4:(half + 1) * 4, :]),
                    "s_st", 16)
            p.flush()

        if _VERBOSE:
            print("P2 sbuf_remaining", nc.sbuf_bytes_remaining)
        with ExitStack() as ph:
            sb = lambda n, s, d: ph.enter_context(nc.sbuf_tensor(n, s, d))
            ps = lambda n, s, d: ph.enter_context(nc.psum_tensor(n, s, d))
            qp = [sb(f"qp{i}", [128, T], BF16) for i in range(2)]
            kp = [sb(f"kp{i}", [128, T], BF16) for i in range(2)]
            vp = [sb(f"vp{i}", [128, NTT * 128], BF16) for i in range(2)]
            s_A = [sb(f"s_A{i}", [128, T], F16) for i in range(3)]
            s_B = [sb(f"s_B{i}", [128, T], F16) for i in range(3)]
            pbA = [sb(f"pbA{i}", [128, T], BF16) for i in range(2)]
            pbB = [sb(f"pbB{i}", [128, T], BF16) for i in range(2)]
            li_rA = sb("li_rA", [1, 128], F32)
            li_rB = sb("li_rB", [1, 128], F32)
            bc_sb = sb("bc_sb", [128, 128], F32)
            pT = sb("pT", [128, NTT * 128], BF16)
            nmA = [sb(f"nmA{i}", [128, 1], F32) for i in range(2)]
            nmB = [sb(f"nmB{i}", [128, 1], F32) for i in range(2)]
            lA2 = [sb(f"lA2{i}", [128, 1], F32) for i in range(2)]
            lB2 = [sb(f"lB2{i}", [128, 1], F32) for i in range(2)]
            lpA = [sb(f"lpA{i}", [128, 4], F32) for i in range(2)]
            lpB = [sb(f"lpB{i}", [128, 4], F32) for i in range(2)]
            nmpA = [sb(f"nmpA{i}", [128, 4], F32) for i in range(2)]
            nmpB = [sb(f"nmpB{i}", [128, 4], F32) for i in range(2)]
            mask = sb("mask_sb", [128, 4 * 512], F32)
            cat_b = [sb(f"cat_b{i}", [128, 128], BF16) for i in range(2)]
            ps_sA = [ps(f"ps_sA{i}", [128, 512], F32) for i in range(2)]
            ps_sB = [ps(f"ps_sB{i}", [128, 512], F32) for i in range(2)]
            ps_t = [ps(f"ps_t{i}", [128, 1024], BF16) for i in range(3)]
            ps_pv = ps("ps_pv", [128, 512], F32)

            c_mask = p.op("sync", lambda e: e.dma_start(out=mask[:], in_=MASK), "s_ld", 16)
            pair_ld = [0, 0]
            pair_free = [0, 0]
            cat_st = [0, 0]

            def load_pair(pr, buf):
                p.op("sync", lambda e, pr=pr, buf=buf: e.dma_start(
                    out=qp[buf][:], in_=QT[pr * 128:(pr + 1) * 128, :]), "s_ld", 16)
                p.op("sync", lambda e, pr=pr, buf=buf: e.dma_start(
                    out=kp[buf][:], in_=KT[pr * 128:(pr + 1) * 128, :]), "s_ld", 16)
                c = None
                for half in range(2):
                    c = p.op("sync", lambda e, pr=pr, buf=buf, half=half: e.dma_start(
                        out=vp[buf][:].rearrange("p (tt d) -> p tt d", tt=NTT)[:, half * 16:(half + 1) * 16, :],
                        in_=VH.rearrange("(tt p) d -> p tt d", p=128)[:, half * 16:(half + 1) * 16,
                            pr * 128:(pr + 1) * 128]), "s_ld", 16)
                return c

            p.wait("sync", "s_st", p.c("s_st"))
            pair_ld[0] = load_pair(0, 0)

            sc_evA = [0, 0]
            sc_evAd = [0, 0]
            sc_evB = [0, 0]
            evA_t = [0, 0, 0]
            evB_t = [0, 0, 0]
            exp_done3 = [0, 0, 0]
            lt_pe = [0, 0]
            tr_ev = [("s_dve", 0), ("s_act", 0), ("s_dve", 0)]
            pv_ev = 0
            exp_done = [0, 0]
            nm_done = [0, 0]
            l_done = [0, 0]
            pbf_used = [0, 0]
            for pr in range(NH // 2):
                buf = pr % 2
                if pr + 1 < NH // 2:
                    nbuf = (pr + 1) % 2
                    p.wait("sync", "s_pe", pair_free[nbuf])
                    pair_ld[nbuf] = load_pair(pr + 1, nbuf)
                p.wait("tensor", "s_ld", pair_ld[buf])
                p.wait("vector", "s_ld", c_mask)
                for ii in range(NTT + 3):
                    if ii < NTT:
                        i = ii
                        p3 = i % 3
                        jd = i // 4
                        r = i % 4
                        for j in range(jd + 1):
                            pbk = j % 2
                            p.wait("tensor", "s_act", sc_evA[pbk])
                            p.wait("tensor", "s_dve", max(sc_evB[pbk], sc_evAd[pbk]))
                            diag = (j == jd)
                            p.op("tensor", lambda e, i=i, j=j, pbk=pbk, buf=buf: e.matmul(
                                ps_sA[pbk][:], qp[buf][0:64, i * 128:(i + 1) * 128],
                                kp[buf][0:64, j * 512:(j + 1) * 512],
                                start=True, stop=True, tile_position=(0, 0)))
                            c_s = p.op("tensor", lambda e, i=i, j=j, pbk=pbk, buf=buf: e.matmul(
                                ps_sB[pbk][:], qp[buf][64:128, i * 128:(i + 1) * 128],
                                kp[buf][64:128, j * 512:(j + 1) * 512],
                                start=True, stop=True, tile_position=(64, 0)), "s_pe")
                            if not diag:
                                p.wait("scalar", "s_pe", c_s)
                                p.wait("scalar", "s_act", exp_done3[p3])
                                sc_evA[pbk] = p.op("scalar", lambda e, j=j, pbk=pbk, p3=p3: e.copy(
                                    s_A[p3][:, j * 512:(j + 1) * 512], ps_sA[pbk][:]), "s_act")
                                p.wait("vector", "s_pe", c_s)
                                p.wait("vector", "s_act", exp_done3[p3])
                                sc_evB[pbk] = p.op("vector", lambda e, j=j, pbk=pbk, p3=p3: e.tensor_copy(
                                    s_B[p3][:, j * 512:(j + 1) * 512], ps_sB[pbk][:]), "s_dve")
                            else:
                                p.wait("vector", "s_pe", c_s)
                                p.wait("vector", "s_act", exp_done3[p3])
                                sc_evAd[pbk] = p.op("vector", lambda e, j=j, r=r, pbk=pbk, p3=p3: e.tensor_tensor(
                                    s_A[p3][:, j * 512:(j + 1) * 512], ps_sA[pbk][:],
                                    mask[:, r * 512:(r + 1) * 512], op=ALU.add), "s_dve")
                                sc_evB[pbk] = p.op("vector", lambda e, j=j, r=r, pbk=pbk, p3=p3: e.tensor_tensor(
                                    s_B[p3][:, j * 512:(j + 1) * 512], ps_sB[pbk][:],
                                    mask[:, r * 512:(r + 1) * 512], op=ALU.add), "s_dve")
                        evA_t[p3] = max(sc_evA)
                        evB_t[p3] = max(max(sc_evB), max(sc_evAd))
                    if 3 <= ii:
                        i2 = ii - 3
                        par2 = i2 % 2
                        ktiles = i2 + 1
                        c_pv = None
                        p.wait("tensor", "s_dve", l_done[par2])
                        p.wait("tensor", "s_dve", pv_ev)
                        p.op("tensor", lambda e, par2=par2: e.transpose(
                            ps_pv[0:1, 256:384], lA2[par2][:], idf[:]))
                        c_lt = p.op("tensor", lambda e, par2=par2: e.transpose(
                            ps_pv[0:1, 384:512], lB2[par2][:], idf[:]), "s_pe")
                        lt_pe[par2] = c_lt
                        p.wait("vector", "s_pe", c_lt)
                        p.op("vector", lambda e: e.reciprocal(li_rA[:], ps_pv[0:1, 256:384]))
                        c_li = p.op("vector", lambda e: e.reciprocal(li_rB[:], ps_pv[0:1, 384:512]), "s_dve")
                        for hh in range(2):
                            pbuf = (pbA if hh == 0 else pbB)[par2]
                            last_dve = 0
                            last_act = 0
                            ngrp = (ktiles + 3) // 4
                            for gi in range(ngrp):
                                kt0 = gi * 4
                                kt1 = min(ktiles, kt0 + 4)
                                tb = gi % 3
                                sem_n, cnt = tr_ev[tb]
                                p.wait("tensor", sem_n, cnt)
                                c_tr = None
                                for kt in range(kt0, kt1):
                                    c_tr = p.op("tensor", lambda e, kt=kt, kt0=kt0, tb=tb, pbuf=pbuf: e.transpose(
                                        ps_t[tb][:, (kt - kt0) * 128:(kt - kt0 + 1) * 128],
                                        pbuf[:, kt * 128:(kt + 1) * 128],
                                        idb[:]), "s_pe")
                                if kt1 == ktiles and hh == 1:
                                    pbf_used[par2] = c_tr
                                eng = "vector" if gi % 2 == 0 else "scalar"
                                p.wait(eng, "s_pe", c_tr)
                                w = (kt1 - kt0) * 128
                                if eng == "vector":
                                    c_ev = p.op("vector", lambda e, kt0=kt0, w=w, tb=tb: e.tensor_copy(
                                        pT[:, kt0 * 128:kt0 * 128 + w], ps_t[tb][:, 0:w]), "s_dve")
                                    last_dve = c_ev
                                    tr_ev[tb] = ("s_dve", c_ev)
                                else:
                                    c_ev = p.op("scalar", lambda e, kt0=kt0, w=w, tb=tb: e.copy(
                                        pT[:, kt0 * 128:kt0 * 128 + w], ps_t[tb][:, 0:w]), "s_act")
                                    last_act = c_ev
                                    tr_ev[tb] = ("s_act", c_ev)
                            p.wait("tensor", "s_dve", last_dve)
                            p.wait("tensor", "s_act", last_act)
                            off = 0 if hh == 0 else 64
                            tp = (0, 0) if hh == 0 else (0, 64)
                            for kt in range(ktiles):
                                if kt == ktiles - 1:
                                    c_pv = p.op("tensor", lambda e, kt=kt, off=off, tp=tp, buf=buf: e.matmul(
                                        ps_pv[off:off + 64, 0:128],
                                        vp[buf][:, kt * 128 + off:kt * 128 + off + 64],
                                        pT[:, kt * 128:(kt + 1) * 128],
                                        start=(kt == 0), stop=True, tile_position=tp), "s_pe")
                                else:
                                    p.op("tensor", lambda e, kt=kt, off=off, tp=tp, buf=buf: e.matmul(
                                        ps_pv[off:off + 64, 0:128],
                                        vp[buf][:, kt * 128 + off:kt * 128 + off + 64],
                                        pT[:, kt * 128:(kt + 1) * 128],
                                        start=(kt == 0), stop=False, tile_position=tp))
                        p.wait("tensor", "s_dve", c_li)
                        p.op("tensor", lambda e: e.matmul(
                            ps_pv[0:64, 128:256], onef[0:1, 0:64], li_rA[:],
                            start=True, stop=True, tile_position=(0, 0)))
                        c_bc = p.op("tensor", lambda e: e.matmul(
                            ps_pv[64:128, 128:256], onef[0:1, 0:64], li_rB[:],
                            start=True, stop=True, tile_position=(0, 64)), "s_pe")
                        if i2 == NTT - 1:
                            pair_free[buf] = c_bc
                        cb = i2 % 2
                        p.wait("vector", "s_pe", c_bc)
                        c_bce = p.op("vector", lambda e: e.tensor_copy(
                            bc_sb[:], ps_pv[:, 128:256]), "s_dve")
                        p.wait("vector", "s_dve", c_bce)
                        p.wait("vector", "s_st", cat_st[cb])
                        c_cat = p.op("vector", lambda e, cb=cb: e.tensor_tensor(
                            cat_b[cb][:], ps_pv[:, 0:128], bc_sb[:], op=ALU.mult), "s_dve")
                        pv_ev = c_cat
                        p.wait("scalar", "s_dve", c_cat)
                        cat_st[cb] = p.op("scalar", lambda e, pr=pr, i2=i2, cb=cb: e.dma_start(
                            out=CATH[pr * 128:(pr + 1) * 128, i2 * 128:(i2 + 1) * 128],
                            in_=cat_b[cb][:]), "s_st", 16)
                    if 1 <= ii <= NTT:
                        x = ii - 1
                        p3 = x % 3
                        par = x % 2
                        ktot = (x // 4 + 1) * 512
                        npc = (ktot + 1023) // 1024
                        p.wait("vector", "s_act", evA_t[p3])
                        p.wait("vector", "s_dve", evB_t[p3])
                        p.wait("vector", "s_act", exp_done[par])
                        c_nmp = None
                        for pc in range(npc):
                            w0 = pc * 1024
                            w1 = min(ktot, w0 + 1024)
                            p.op("vector", lambda e, par=par, p3=p3, w0=w0, w1=w1, pc=pc: e.tensor_reduce(
                                nmpA[par][:, pc:pc + 1], s_A[p3][:, w0:w1], axis=AX.X, op=ALU.max))
                            c_nmp = p.op("vector", lambda e, par=par, p3=p3, w0=w0, w1=w1, pc=pc: e.tensor_reduce(
                                nmpB[par][:, pc:pc + 1], s_B[p3][:, w0:w1], axis=AX.X, op=ALU.max), "s_dve")
                        p.wait("vector", "s_dve", c_nmp)
                        p.op("vector", lambda e, par=par, npc=npc: e.tensor_reduce(
                            nmA[par][:], nmpA[par][:, 0:npc], axis=AX.X, op=ALU.max, negate=True))
                        c_nmB = p.op("vector", lambda e, par=par, npc=npc: e.tensor_reduce(
                            nmB[par][:], nmpB[par][:, 0:npc], axis=AX.X, op=ALU.max, negate=True), "s_dve")
                        p.wait("vector", "s_dve", c_nmB)
                        p.op("vector", lambda e, par=par: e.tensor_scalar(
                            nmA[par][:], nmA[par][:], 0.125, None, op0=ALU.mult))
                        nm_done[par] = p.op("vector", lambda e, par=par: e.tensor_scalar(
                            nmB[par][:], nmB[par][:], 0.125, None, op0=ALU.mult), "s_dve")
                    if 2 <= ii <= NTT + 1:
                        x = ii - 2
                        p3 = x % 3
                        par = x % 2
                        ktot1 = (x // 4 + 1) * 512
                        npc1 = (ktot1 + 1023) // 1024
                        p.wait("scalar", "s_dve", nm_done[par])
                        p.wait("scalar", "s_pe", pbf_used[par])
                        p.wait("scalar", "s_dve", l_done[par])
                        c_eB = None
                        for pc in range(npc1):
                            w0 = pc * 1024
                            w1 = min(ktot1, w0 + 1024)
                            p.op("scalar", lambda e, par=par, p3=p3, w0=w0, w1=w1, pc=pc: e.activation(
                                pbA[par][:, w0:w1], s_A[p3][:, w0:w1], AF.Exp,
                                bias=nmA[par][:], scale=0.125, accum_out=lpA[par][:, pc:pc + 1]))
                            c_eB = p.op("scalar", lambda e, par=par, p3=p3, w0=w0, w1=w1, pc=pc: e.activation(
                                pbB[par][:, w0:w1], s_B[p3][:, w0:w1], AF.Exp,
                                bias=nmB[par][:], scale=0.125, accum_out=lpB[par][:, pc:pc + 1]), "s_act")
                        exp_done[par] = c_eB
                        exp_done3[p3] = c_eB
                        p.wait("vector", "s_act", c_eB)
                        p.wait("vector", "s_pe", lt_pe[par])
                        p.op("vector", lambda e, par=par, npc=npc1: e.reduce_sum(
                            lA2[par][:], lpA[par][:, 0:npc], axis=AX.X))
                        l_done[par] = p.op("vector", lambda e, par=par, npc=npc1: e.reduce_sum(
                            lB2[par][:], lpB[par][:, 0:npc], axis=AX.X), "s_dve")
            p.flush()

        with ExitStack() as ph:
            sb = lambda n, s, d: ph.enter_context(nc.sbuf_tensor(n, s, d))
            ps = lambda n, s, d: ph.enter_context(nc.psum_tensor(n, s, d))
            cat = sb("cat", [128, NC_ * T], BF16)
            wo = sb("wo", [128, NC_ * D], BF16)
            bor = sb("bor", [1, D], BF16)
            at_b = [sb(f"at_b{i}", [128, 512], F32) for i in range(2)]
            ps_o = [ps(f"ps_o{i}", [128, 512], F32) for i in range(2)]

            p.op("gpsimd", lambda e: e.dma_start(
                out=wo[:].rearrange("p (c m) -> p c m", c=NC_),
                in_=WOT.rearrange("(c p) m -> p c m", p=128)), "s_cast", 16)
            c_bor = p.op("gpsimd", lambda e: e.dma_start(out=bor[:], in_=BO), "s_cast", 16)
            p.wait("sync", "s_st", p.c("s_st"))
            c_cat_ld = 0
            for half in range(2):
                c_cat_ld = p.op("sync", lambda e, half=half: e.dma_start(
                    out=cat[:].rearrange("p (c t) -> p c t", c=NC_)[:, half * 4:(half + 1) * 4, :],
                    in_=CATH.rearrange("(c p) t -> p c t", p=128)[:, half * 4:(half + 1) * 4, :]),
                    "s_ld", 16)
            at_ev = [0, 0]
            at_st = [0, 0]
            p.wait("tensor", "s_ld", c_cat_ld)
            p.wait("tensor", "s_cast", c_bor)
            for tt in range(NTT):
                for n in range(2):
                    pb = (tt * 2 + n) % 2
                    p.wait("tensor", "s_dve", at_ev[pb])
                    p.op("tensor", lambda e, n=n, pb=pb: e.matmul(
                        ps_o[pb][:], oneb[:], bor[:, n * 512:(n + 1) * 512],
                        start=True, stop=False))
                    for c_ in range(NC_):
                        if c_ == NC_ - 1:
                            c_mm = p.op("tensor", lambda e, tt=tt, n=n, c_=c_, pb=pb: e.matmul(
                                ps_o[pb][:], cat[:, c_ * T + tt * 128:c_ * T + (tt + 1) * 128],
                                wo[:, c_ * D + n * 512:c_ * D + (n + 1) * 512],
                                start=False, stop=True), "s_pe")
                        else:
                            p.op("tensor", lambda e, tt=tt, n=n, c_=c_, pb=pb: e.matmul(
                                ps_o[pb][:], cat[:, c_ * T + tt * 128:c_ * T + (tt + 1) * 128],
                                wo[:, c_ * D + n * 512:c_ * D + (n + 1) * 512],
                                start=False, stop=False))
                    p.wait("vector", "s_pe", c_mm)
                    p.wait("vector", "s_st", at_st[pb])
                    c_ev = p.op("vector", lambda e, pb=pb: e.tensor_copy(
                        at_b[pb][:], ps_o[pb][:]), "s_dve")
                    at_ev[pb] = c_ev
                    p.wait("scalar", "s_dve", c_ev)
                    at_st[pb] = p.op("scalar", lambda e, tt=tt, n=n, pb=pb: e.dma_start(
                        out=ATTH[tt * 128:(tt + 1) * 128, n * 512:(n + 1) * 512],
                        in_=at_b[pb][:]), "s_st", 16)
            p.flush()

        if _VERBOSE:
            print("P4 sbuf_remaining", nc.sbuf_bytes_remaining)
        with ExitStack() as ph:
            sb = lambda n, s, d: ph.enter_context(nc.sbuf_tensor(n, s, d))
            ps = lambda n, s, d: ph.enter_context(nc.psum_tensor(n, s, d))
            w2 = sb("w2", [128, 32 * D], BF16)
            w1blk = [sb(f"w1blk{i}", [128, NC_ * 128], BF16) for i in range(2)]
            xng = sb("xng", [128, NC_ * GRP], BF16)
            h_sb = sb("h_sb", [128, 32 * GRP], BF16)
            y_sb = sb("y_sb", [128, NC_ * GRP], F32)
            b1f = [sb(f"b1f{i}", [128, 32], F32) for i in range(2)]
            b2s = sb("b2s", [NE, D], BF16)
            x_t = sb("x_t", [128, D], F32)
            a_t = sb("a_t", [128, D], F32)
            o_t = [sb("o_t0", [128, D], F32)]
            ps_h = [ps(f"ps_h{i}", [128, 1024], F32) for i in range(2)]
            ps_oo = [ps(f"ps_oo{i}", [128, 512], F32) for i in range(2)]
            ps_i = ps("ps_i", [128, 512], F32)

            c_b2s = p.op("gpsimd", lambda e: e.dma_start(out=b2s[:], in_=B2), "s_cast", 16)
            p.wait("sync", "s_cast", c_wcache)
            p.wait("sync", "s_st", c_xnt_sp)

            h_ev = [0, 0]
            o_ev = [0, 0]
            i_ev = 0
            w1_ld = [0, 0]
            w1_free = [0, 0]
            w2_ld = 0
            w2_free = 0
            b1_ld = [0, 0]
            b1_free = [0, 0]
            xng_ld = 0
            xng_free = 0
            y_init = [0] * 16
            y_last = [0] * 16
            comb_dve = 0
            out_st = [0, 0]
            x_ld = [0, 0]
            att_ok = p.c("s_st")

            def load_w2(e_):
                c = None
                for q4 in range(4):
                    c = p.op("sync", lambda e, e_=e_, q4=q4: e.dma_start(
                        out=w2[:, q4 * 8 * D:(q4 + 1) * 8 * D],
                        in_=W2C[e_, :, q4 * 8 * D:(q4 + 1) * 8 * D]), "s_ld", 16)
                return c

            def load_w1(e_, mi):
                rs = mi % 2
                p.wait("sync", "s_pe", w1_free[rs])
                w1_ld[rs] = p.op("sync", lambda e, e_=e_, mi=mi, rs=rs: e.dma_start(
                    out=w1blk[rs][:], in_=W1C[e_, mi]), "s_ld", 16)

            for g in range(NGRP):
                p.wait("sync", "s_pe", xng_free)
                xng_ld = p.op("sync", lambda e, g=g: e.dma_start(
                    out=xng[:].rearrange("p (c t) -> p c t", c=NC_),
                    in_=XNT.rearrange("(c p) t -> p c t", p=128)[:, :, g * GRP:(g + 1) * GRP]),
                    "s_ld", 16)
                def emit_y_init(idx, g=g):
                    nonlocal i_ev
                    tt, n = idx // 2, idx % 2
                    p.wait("tensor", "s_dve", i_ev)
                    if g == 0 and idx == 0:
                        p.wait("tensor", "s_cast", c_b2s)
                    c_mm = p.op("tensor", lambda e, g=g, tt=tt, n=n: e.matmul(
                        ps_i[:], gatesT[0:NE, (g * NC_ + tt) * 128:(g * NC_ + tt + 1) * 128],
                        b2s[0:NE, n * 512:(n + 1) * 512], start=True, stop=True), "s_pe")
                    p.wait("vector", "s_pe", c_mm)
                    p.wait("vector", "s_dve", comb_dve)
                    i_ev = p.op("vector", lambda e, tt=tt, n=n: e.tensor_copy(
                        y_sb[:, tt * D + n * 512:tt * D + (n + 1) * 512], ps_i[:]), "s_dve")
                    y_init[idx] = i_ev
                for e_ in range(NE):
                    bb = e_ % 2
                    p.wait("sync", "s_pe", w2_free)
                    w2_ld = load_w2(e_)
                    p.wait("sync", "s_act", b1_free[bb])
                    b1_ld[bb] = p.op("sync", lambda e, e_=e_, bb=bb: e.dma_start(
                        out=b1f[bb][:], in_=B1F[e_]), "s_ld", 16)
                    for mi in range(2):
                        load_w1(e_, mi)
                    for mi in range(32):
                        rs = mi % 2
                        hb = mi % 2
                        p.wait("tensor", "s_ld", w1_ld[rs])
                        if mi == 0:
                            p.wait("tensor", "s_ld", xng_ld)
                            p.wait("tensor", "s_ld", b1_ld[bb])
                        p.wait("tensor", "s_act", h_ev[hb])
                        c_mm = None
                        for n in range(2):
                            for c_ in range(NC_):
                                if c_ == NC_ - 1:
                                    c_mm = p.op("tensor", lambda e, mi=mi, n=n, c_=c_, rs=rs, hb=hb: e.matmul(
                                        ps_h[hb][:, n * 512:(n + 1) * 512], w1blk[rs][:, c_ * 128:(c_ + 1) * 128],
                                        xng[:, c_ * GRP + n * 512:c_ * GRP + (n + 1) * 512],
                                        start=False, stop=True), "s_pe")
                                else:
                                    p.op("tensor", lambda e, mi=mi, n=n, c_=c_, rs=rs, hb=hb: e.matmul(
                                        ps_h[hb][:, n * 512:(n + 1) * 512], w1blk[rs][:, c_ * 128:(c_ + 1) * 128],
                                        xng[:, c_ * GRP + n * 512:c_ * GRP + (n + 1) * 512],
                                        start=(c_ == 0), stop=False))
                        p.wait("scalar", "s_pe", c_mm)
                        h_ev[hb] = p.op("scalar", lambda e, mi=mi, hb=hb, bb=bb: e.activation(
                            h_sb[:, mi * GRP:(mi + 1) * GRP],
                            ps_h[hb][:], AF.Gelu, bias=b1f[bb][:, mi:mi + 1]), "s_act")
                        w1_free[rs] = c_mm
                        if mi == 31:
                            xng_free = c_mm
                        if e_ == 0 and mi < 16:
                            emit_y_init(mi)
                        if mi + 2 < 32:
                            load_w1(e_, mi + 2)
                    b1_free[bb] = h_ev[1]
                    c_h_all = p.c("s_act")
                    p.wait("tensor", "s_act", c_h_all)
                    p.wait("tensor", "s_ld", w2_ld)
                    for n2 in range(2):
                        for tt in range(NC_):
                            ob = (n2 * NC_ + tt) % 2
                            p.wait("tensor", "s_dve", o_ev[ob])
                            for ic in range(32):
                                if ic == 31:
                                    c_mm = p.op("tensor", lambda e, n2=n2, tt=tt, ic=ic, ob=ob: e.matmul(
                                        ps_oo[ob][:], h_sb[:, ic * GRP + tt * 128:ic * GRP + (tt + 1) * 128],
                                        w2[:, ic * D + n2 * 512:ic * D + (n2 + 1) * 512],
                                        start=False, stop=True), "s_pe")
                                else:
                                    p.op("tensor", lambda e, n2=n2, tt=tt, ic=ic, ob=ob: e.matmul(
                                        ps_oo[ob][:], h_sb[:, ic * GRP + tt * 128:ic * GRP + (tt + 1) * 128],
                                        w2[:, ic * D + n2 * 512:ic * D + (n2 + 1) * 512],
                                        start=(ic == 0), stop=False))
                            if n2 == 1 and tt == NC_ - 1:
                                w2_free = c_mm
                            p.wait("vector", "s_pe", c_mm)
                            if e_ == 0:
                                p.wait("vector", "s_dve", y_init[tt * 2 + n2])
                            else:
                                p.wait("vector", "s_dve", y_last[tt * 2 + n2])
                            gcol = (g * NC_ + tt) * NE + e_
                            o_ev[ob] = y_last[tt * 2 + n2] = p.op("vector", lambda e, n2=n2, tt=tt, ob=ob, gcol=gcol: e.scalar_tensor_tensor(
                                y_sb[:, tt * D + n2 * 512:tt * D + (n2 + 1) * 512],
                                ps_oo[ob][:], gates[:, gcol:gcol + 1],
                                y_sb[:, tt * D + n2 * 512:tt * D + (n2 + 1) * 512],
                                op0=ALU.mult, op1=ALU.add), "s_dve")
                c_moe_done = p.c("s_dve")
                for tt in range(NC_):
                    tb = 0
                    gt = g * NC_ + tt
                    p.wait("sync", "s_st", att_ok)
                    p.wait("sync", "s_dve", x_ld[0])
                    p.op("sync", lambda e, gt=gt: e.dma_start(
                        out=x_t[:], in_=X[gt * 128:(gt + 1) * 128, :]), "s_ld", 16)
                    c_xa = p.op("sync", lambda e, gt=gt: e.dma_start(
                        out=a_t[:], in_=ATTH[gt * 128:(gt + 1) * 128, :]), "s_ld", 16)
                    p.wait("vector", "s_ld", c_xa)
                    p.wait("vector", "s_dve", c_moe_done)
                    p.wait("vector", "s_st", out_st[tb])
                    c_s1 = p.op("vector", lambda e, tb=tb: e.tensor_tensor(
                        o_t[tb][:], x_t[:], a_t[:], op=ALU.add), "s_dve")
                    p.wait("vector", "s_dve", c_s1)
                    c_o = p.op("vector", lambda e, tt=tt, tb=tb: e.tensor_tensor(
                        o_t[tb][:], o_t[tb][:], y_sb[:, tt * D:(tt + 1) * D], op=ALU.add), "s_dve")
                    x_ld[0] = c_o
                    comb_dve = c_o
                    p.wait("scalar", "s_dve", c_o)
                    out_st[tb] = p.op("scalar", lambda e, gt=gt, tb=tb: e.dma_start(
                        out=OUT[gt * 128:(gt + 1) * 128, :], in_=o_t[tb][:]), "s_st", 16)
            if _DEBUG_TAPS:
                p.wait("gpsimd", "s_st", p.c("s_st"))
                p.wait("gpsimd", "s_dve", p.c("s_dve"))
                for tname in _DEBUG_TAPS:
                    ap, shape = scratch_regs[tname]
                    dbg = nc.dram_tensor("dbg_" + tname, list(shape), F32,
                                         kind="ExternalOutput").ap()
                    p.op("gpsimd", lambda e, dbg=dbg, ap=ap: e.dma_start(
                        out=dbg, in_=ap), "s_cast", 16)
            p.wait("scalar", "s_st", p.c("s_st"))
            p.wait("gpsimd", "s_cast", p.c("s_cast"))
            p.flush()

    return nc


def _host_prep(L0_attn_W_q, L0_attn_b_q, L0_attn_W_k, L0_attn_b_k, L0_attn_W_v,
               L0_attn_W_o, L0_ffn_Wg, L0_ffn_W1, L0_ffn_b1, L0_ffn_W2):
    perm = []
    for h in range(NH):
        base = DH * h
        perm += [base + 2 * j for j in range(32)] + [base + 2 * j + 1 for j in range(32)]
    perm = np.array(perm)
    ac = np.ascontiguousarray
    Wq = np.asarray(L0_attn_W_q)
    Wk = np.asarray(L0_attn_W_k)
    common = {}
    common["wqt"] = ac(Wq.T[:, perm])
    common["wkt"] = ac(Wk.T[:, perm])
    common["wvt"] = ac(np.asarray(L0_attn_W_v).T)
    common["wot"] = ac(np.asarray(L0_attn_W_o).T)
    common["wgt"] = ac(np.asarray(L0_ffn_Wg).T)
    common["bq"] = ac(np.asarray(L0_attn_b_q)[perm].reshape(NC_, 128).T)
    common["bk"] = ac(np.asarray(L0_attn_b_k)[perm].reshape(NC_, 128).T)
    common["w1p"] = ac(np.asarray(L0_ffn_W1).reshape(NE, NC_, 128, 32, 128)
                       .transpose(0, 3, 2, 1, 4).reshape(NE, 32, 128, NC_ * 128))
    common["w2p"] = ac(np.asarray(L0_ffn_W2).reshape(NE, 32, 128, D)
                       .transpose(0, 2, 1, 3).reshape(NE, 128, 32 * D))
    common["b1f"] = ac(np.asarray(L0_ffn_b1).reshape(NE, 32, 128).transpose(0, 2, 1))
    j = np.arange(32, dtype=np.float64)
    theta = 10000.0 ** (-j / 32.0)
    t = np.arange(T, dtype=np.float64)
    ang = t[None, :] * theta[:, None]
    cosv = np.cos(ang)
    sinv = np.sin(ang)
    ccr = np.tile(cosv, (4, 1))
    ssr = np.concatenate([-sinv, sinv, -sinv, sinv], axis=0)
    common["cc"] = ccr.astype(BF)
    common["ss"] = ssr.astype(BF)
    psw = np.zeros((128, 128), dtype=BF)
    for pp in range(128):
        jj = pp // 32
        swp = pp + 32 if jj % 2 == 0 else pp - 32
        psw[swp, pp] = 1.0
    common["psw"] = psw
    common["idb"] = np.eye(128, dtype=BF)
    common["idf"] = np.eye(128, dtype=np.float32)
    common["oneb"] = np.ones((1, 128), dtype=BF)
    common["onef"] = np.ones((1, 128), dtype=np.float32)
    m = np.zeros((128, 4 * 512), dtype=np.float32)
    qq = np.arange(128)[:, None]
    kk = np.arange(512)[None, :]
    for r in range(4):
        m[:, r * 512:(r + 1) * 512] = np.where(kk <= 128 * r + qq, 0.0, NEG)
    common["mask"] = m
    return common


def kernel(L0_attn_x, L0_attn_W_q, L0_attn_b_q, L0_attn_W_k, L0_attn_b_k,
           L0_attn_W_v, L0_attn_b_v, L0_attn_W_o, L0_attn_b_o,
           L0_ffn_Wg, L0_ffn_W1, L0_ffn_b1, L0_ffn_W2, L0_ffn_b2):
    common = _host_prep(L0_attn_W_q, L0_attn_b_q, L0_attn_W_k, L0_attn_b_k,
                        L0_attn_W_v, L0_attn_W_o, L0_ffn_Wg, L0_ffn_W1,
                        L0_ffn_b1, L0_ffn_W2)
    common["bv"] = np.ascontiguousarray(np.asarray(L0_attn_b_v, dtype=np.float32)[None, :])
    common["bo"] = np.ascontiguousarray(np.asarray(L0_attn_b_o, dtype=np.float32)[None, :])
    common["b2"] = np.ascontiguousarray(np.asarray(L0_ffn_b2, dtype=np.float32))
    common = {k: (np.asarray(v, dtype=np.float32) if np.asarray(v).dtype == np.float64 else np.asarray(v))
              for k, v in common.items()}
    x = np.asarray(L0_attn_x, dtype=np.float32)
    in_maps = []
    for c in range(8):
        m = dict(common)
        m["x"] = np.ascontiguousarray(x[c])
        in_maps.append(m)
    nc = build_program()
    res = run_bass_kernel_spmd(nc, in_maps, core_ids=list(range(8)))
    out = np.stack([np.asarray(res.results[c]["out"]) for c in range(8)], axis=0)
    return out


# version 20 (final)
# speedup vs baseline: 1.2241x; 1.0005x over previous
import numpy as np
import ml_dtypes
from contextlib import ExitStack
import concourse.bass as bass
import concourse.mybir as mybir
from concourse.bass_utils import run_bass_kernel_spmd

F32 = mybir.dt.float32
F16 = mybir.dt.float16
BF16 = mybir.dt.bfloat16
AF = mybir.ActivationFunctionType
ALU = mybir.AluOpType
AX = mybir.AxisListType
BF = ml_dtypes.bfloat16

D = 1024
T = 4096
NH = 16
DH = 64
NE = 8
INTER = 4096
NC_ = 8
NTT = 32
NGRP = 4
GRP = 1024
NEG = -1.0e9

_DEBUG_TAPS = []
_VERBOSE = False


class Prog:
    ENGINES = ("sync", "tensor", "vector", "scalar", "gpsimd")

    def __init__(self, nc):
        self.nc = nc
        self.streams = {e: [] for e in self.ENGINES}
        self.sems = {}
        self.cnt = {}
        self.last_wait = {}

    def sem(self, name, stack):
        h = stack.enter_context(self.nc.semaphore(name))
        self.sems[name] = h
        self.cnt[name] = 0

    def wait(self, eng, sem, val=None):
        v = self.cnt[sem] if val is None else val
        if v <= 0:
            return
        key = (eng, sem)
        if self.last_wait.get(key, -1) >= v:
            return
        self.last_wait[key] = v
        h = self.sems[sem]
        self.streams[eng].append(lambda e, h=h, v=v: e.wait_ge(h, v))

    def op(self, eng, build, inc=None, by=1):
        if inc is None:
            self.streams[eng].append(lambda e, b=build: b(e))
            return None
        h = self.sems[inc]
        self.cnt[inc] += by
        self.streams[eng].append(lambda e, b=build, h=h, by=by: b(e).then_inc(h, by))
        return self.cnt[inc]

    def c(self, sem):
        return self.cnt[sem]

    def flush(self):
        nc = self.nc
        streams = self.streams
        self.streams = {e: [] for e in self.ENGINES}
        with nc.Block() as block:
            if streams["sync"]:
                @block.sync
                def _(e, ops=streams["sync"]):
                    for f in ops:
                        f(e)
            if streams["tensor"]:
                @block.tensor
                def _(e, ops=streams["tensor"]):
                    for f in ops:
                        f(e)
            if streams["vector"]:
                @block.vector
                def _(e, ops=streams["vector"]):
                    for f in ops:
                        f(e)
            if streams["scalar"]:
                @block.scalar
                def _(e, ops=streams["scalar"]):
                    for f in ops:
                        f(e)
            if streams["gpsimd"]:
                @block.gpsimd
                def _(e, ops=streams["gpsimd"]):
                    for f in ops:
                        f(e)


def build_program():
    nc = bass.Bass("TRN2", target_bir_lowering=False)

    scratch_regs = {}

    def dram(name, shape, dt, kind):
        ap = nc.dram_tensor(name, shape, dt, kind=kind).ap()
        if kind == "Internal":
            scratch_regs[name] = (ap, shape)
        return ap

    X = dram("x", [T, D], F32, "ExternalInput")
    WQT = dram("wqt", [D, D], F32, "ExternalInput")
    WKT = dram("wkt", [D, D], F32, "ExternalInput")
    WVT = dram("wvt", [D, D], F32, "ExternalInput")
    WOT = dram("wot", [D, D], F32, "ExternalInput")
    WGT = dram("wgt", [D, NE], F32, "ExternalInput")
    BQ = dram("bq", [128, NC_], F32, "ExternalInput")
    BK = dram("bk", [128, NC_], F32, "ExternalInput")
    BV = dram("bv", [1, D], F32, "ExternalInput")
    BO = dram("bo", [1, D], F32, "ExternalInput")
    W1P = dram("w1p", [NE, 32, 128, NC_ * 128], F32, "ExternalInput")
    W2P = dram("w2p", [NE, 128, 32 * D], F32, "ExternalInput")
    B1F = dram("b1f", [NE, 128, 32], F32, "ExternalInput")
    B2 = dram("b2", [NE, D], F32, "ExternalInput")
    CC = dram("cc", [128, T], BF16, "ExternalInput")
    SS = dram("ss", [128, T], BF16, "ExternalInput")
    PSW = dram("psw", [128, 128], BF16, "ExternalInput")
    IDB = dram("idb", [128, 128], BF16, "ExternalInput")
    IDF = dram("idf", [128, 128], F32, "ExternalInput")
    ONEB = dram("oneb", [1, 128], BF16, "ExternalInput")
    ONEF = dram("onef", [1, 128], F32, "ExternalInput")
    MASK = dram("mask", [128, 4 * 512], F32, "ExternalInput")

    OUT = dram("out", [T, D], F32, "ExternalOutput")

    XNT = dram("xnt_h", [D, T], BF16, "Internal")
    QT = dram("qt_h", [D, T], BF16, "Internal")
    KT = dram("kt_h", [D, T], BF16, "Internal")
    VH = dram("v_h", [T, D], BF16, "Internal")
    CATH = dram("cat_h", [D, T], BF16, "Internal")
    ATTH = dram("att_h", [T, D], F32, "Internal")
    W1C = dram("w1c", [NE, 32, 128, NC_ * 128], BF16, "Internal")
    W2C = dram("w2c", [NE, 128, 32 * D], BF16, "Internal")

    with ExitStack() as outer:
        p = Prog(nc)
        for s in ["s_ld", "s_cast", "s_st", "s_pe", "s_act", "s_dve", "s_pl"]:
            p.sem(s, outer)

        idb = outer.enter_context(nc.sbuf_tensor("idb_sb", [128, 128], BF16))
        idf = outer.enter_context(nc.sbuf_tensor("idf_sb", [128, 128], F32))
        oneb = outer.enter_context(nc.sbuf_tensor("oneb_sb", [1, 128], BF16))
        onef = outer.enter_context(nc.sbuf_tensor("onef_sb", [1, 128], F32))
        gates = outer.enter_context(nc.sbuf_tensor("gates_sb", [128, NTT * NE], F32))
        gatesT = outer.enter_context(nc.sbuf_tensor("gatesT_sb", [NE, T], BF16))

        if _VERBOSE:
            print("P01 sbuf_remaining", nc.sbuf_bytes_remaining)
        with ExitStack() as ph:
            sb = lambda n, s, d: ph.enter_context(nc.sbuf_tensor(n, s, d))
            ps = lambda n, s, d: ph.enter_context(nc.psum_tensor(n, s, d))
            x_b = [sb(f"x_b{i}", [128, D], F32) for i in range(3)]
            sq = [sb(f"sq{i}", [128, D], F32) for i in range(3)]
            s2 = [sb(f"s2_{i}", [128, 1], F32) for i in range(3)]
            nsum = [sb(f"nsum{i}", [128, 1], F32) for i in range(3)]
            m_neg = [sb(f"mneg{i}", [128, 1], F32) for i in range(3)]
            m2t = [sb(f"m2t{i}", [128, 1], F32) for i in range(3)]
            var = [sb(f"var{i}", [128, 1], F32) for i in range(3)]
            rstd = [sb(f"rstd{i}", [128, 1], F32) for i in range(3)]
            xn_b = [sb(f"xn_b{i}", [128, D], BF16) for i in range(3)]
            xnT = sb("xnT", [128, NC_ * T], BF16)
            wq = sb("wq", [128, NC_ * D], BF16)
            wk = sb("wk", [128, NC_ * D], BF16)
            wv = sb("wv", [128, NC_ * D], BF16)
            wg = sb("wg", [128, NC_ * NE], BF16)
            bqf = sb("bqf", [128, NC_], F32)
            bkf = sb("bkf", [128, NC_], F32)
            bvr = sb("bvr", [1, D], BF16)
            cc = sb("cc_sb", [128, T], BF16)
            ss = sb("ss_sb", [128, T], BF16)
            psw = sb("psw_sb", [128, 128], BF16)
            q_sb = [sb(f"q_sb{i}", [128, 512], BF16) for i in range(3)]
            t1b = [sb(f"t1b{i}", [128, 512], BF16) for i in range(3)]
            t2b = [sb(f"t2b{i}", [128, 512], BF16) for i in range(3)]
            rope_b = [sb(f"rope{i}", [128, 512], BF16) for i in range(3)]
            v_b = [sb(f"v_b{i}", [128, 512], BF16) for i in range(3)]
            logT = sb("logT", [NE, T], F32)
            lt_b = sb("lt_b", [128, NE], F32)
            g_m1 = sb("g_m1", [128, 1], F32)
            g_m2 = sb("g_m2", [128, 1], F32)
            g_d12 = sb("g_d12", [128, 1], F32)
            g_w1 = sb("g_w1", [128, 1], F32)
            g_w2 = sb("g_w2", [128, 1], F32)
            g_is1 = sb("g_is1", [128, NE], F32)
            g_is2 = sb("g_is2", [128, NE], F32)
            g_msk = sb("g_msk", [128, NE], F32)
            g_g1 = sb("g_g1", [128, NE], F32)
            gt_b = sb("gt_b", [128, NE], F32)
            ps_tr = [ps(f"ps_tr{i}", [128, 1024], BF16) for i in range(2)]
            ps_pr = [ps(f"ps_pr{i}", [128, 512], F32) for i in range(3)]
            ps_sw = [ps(f"ps_sw{i}", [128, 512], F32) for i in range(2)]
            ps_g = ps("ps_g", [128, 512], F32)

            p.op("sync", lambda e: e.dma_start(out=idb[:], in_=IDB), "s_ld", 16)
            p.op("sync", lambda e: e.dma_start(out=idf[:], in_=IDF), "s_ld", 16)
            p.op("sync", lambda e: e.dma_start(out=oneb[:], in_=ONEB), "s_ld", 16)
            p.op("sync", lambda e: e.dma_start(out=onef[:], in_=ONEF), "s_ld", 16)
            p.op("sync", lambda e: e.dma_start(out=cc[:], in_=CC), "s_ld", 16)
            p.op("sync", lambda e: e.dma_start(out=ss[:], in_=SS), "s_ld", 16)
            p.op("sync", lambda e: e.dma_start(out=psw[:], in_=PSW), "s_ld", 16)
            p.op("sync", lambda e: e.dma_start(out=bqf[:], in_=BQ), "s_ld", 16)
            c_tabs = p.op("sync", lambda e: e.dma_start(out=bkf[:], in_=BK), "s_ld", 16)
            for w_sb_, W_ in [(wq, WQT), (wk, WKT), (wv, WVT)]:
                p.op("gpsimd", lambda e, o=w_sb_, i=W_: e.dma_start(
                    out=o[:].rearrange("p (c m) -> p c m", c=NC_),
                    in_=i.rearrange("(c p) m -> p c m", p=128)), "s_cast", 16)
            p.op("gpsimd", lambda e: e.dma_start(
                out=wg[:].rearrange("p (c m) -> p c m", c=NC_),
                in_=WGT.rearrange("(c p) m -> p c m", p=128)), "s_cast", 16)
            c_wproj = p.op("gpsimd", lambda e: e.dma_start(out=bvr[:], in_=BV), "s_cast", 16)
            for e_ in range(NE):
                for q4 in range(4):
                    sl = slice(q4 * 8, q4 * 8 + 8)
                    p.op("gpsimd", lambda e, e_=e_, sl=sl: e.dma_start(
                        out=W1C[e_, sl], in_=W1P[e_, sl]), "s_cast", 16)
                    sl2 = slice(q4 * 32, q4 * 32 + 32)
                    p.op("gpsimd", lambda e, e_=e_, sl2=sl2: e.dma_start(
                        out=W2C[e_, sl2], in_=W2P[e_, sl2]), "s_cast", 16)
            c_wcache = p.c("s_cast")

            ln_done = [0, 0, 0]
            tr_done = [0, 0]
            xn_done = [0, 0, 0]
            for t in range(NTT):
                b = t % 3
                p.wait("sync", "s_dve", ln_done[b])
                c_x = p.op("sync", lambda e, t=t, b=b: e.dma_start(
                    out=x_b[b][:], in_=X[t * 128:(t + 1) * 128, :]), "s_ld", 16)
                p.wait("scalar", "s_ld", c_x)
                p.wait("scalar", "s_dve", ln_done[b])
                c_sq = p.op("scalar", lambda e, b=b: e.activation(
                    sq[b][:], x_b[b][:], AF.Square, accum_out=s2[b][:]), "s_act")
                p.wait("vector", "s_ld", c_x)
                c_ns = p.op("vector", lambda e, b=b: e.tensor_reduce(
                    nsum[b][:], x_b[b][:], axis=AX.X, op=ALU.add, negate=True), "s_dve")
                p.wait("vector", "s_dve", c_ns)
                p.op("vector", lambda e, b=b: e.tensor_scalar(
                    m_neg[b][:], nsum[b][:], 1.0 / D, None, op0=ALU.mult))
                c_mn = p.op("vector", lambda e, b=b: e.scalar_tensor_tensor(
                    m2t[b][:], nsum[b][:], 1.0 / (D * D), nsum[b][:],
                    op0=ALU.mult, op1=ALU.mult), "s_dve")
                p.wait("vector", "s_act", c_sq)
                p.wait("vector", "s_dve", c_mn)
                c_va = p.op("vector", lambda e, b=b: e.tensor_scalar(
                    var[b][:], s2[b][:], 1.0 / D, 1e-5, op0=ALU.mult, op1=ALU.add), "s_dve")
                p.wait("vector", "s_dve", c_va)
                c_var = p.op("vector", lambda e, b=b: e.tensor_tensor(
                    var[b][:], var[b][:], m2t[b][:], op=ALU.subtract), "s_dve")
                p.wait("scalar", "s_dve", c_var)
                c_sqr = p.op("scalar", lambda e, b=b: e.activation(
                    rstd[b][:], var[b][:], AF.Sqrt), "s_act")
                p.wait("vector", "s_act", c_sqr)
                c_rs = p.op("vector", lambda e, b=b: e.reciprocal(
                    rstd[b][:], rstd[b][:]), "s_dve")
                p.wait("vector", "s_dve", c_rs)
                p.wait("vector", "s_pe", xn_done[b])
                c_xn = p.op("vector", lambda e, b=b: e.tensor_scalar(
                    xn_b[b][:], x_b[b][:], m_neg[b][:], rstd[b][:],
                    op0=ALU.add, op1=ALU.mult), "s_dve")
                ln_done[b] = c_xn
                p.wait("tensor", "s_dve", c_xn)
                p.wait("tensor", "s_ld", c_tabs)
                for c_ in range(NC_):
                    tb = c_ % 2
                    p.wait("tensor", "s_dve", tr_done[tb])
                    c_tr = p.op("tensor", lambda e, b=b, c_=c_, tb=tb: e.transpose(
                        ps_tr[tb][:, 0:128], xn_b[b][:, c_ * 128:(c_ + 1) * 128],
                        idb[:]), "s_pe")
                    if c_ == NC_ - 1:
                        xn_done[b] = c_tr
                    p.wait("vector", "s_pe", c_tr)
                    tr_done[tb] = p.op("vector", lambda e, t=t, c_=c_, tb=tb: e.tensor_copy(
                        xnT[:, c_ * T + t * 128: c_ * T + (t + 1) * 128],
                        ps_tr[tb][:, 0:128]), "s_dve")
            c_xnT_all = p.c("s_dve")

            pr_done = [0, 0, 0]
            sw_done = [0, 0, 0]
            swb_done = [0, 0]
            qs_done = [0, 0, 0]
            rope_st = [0, 0, 0]
            p.wait("tensor", "s_cast", c_wproj)
            p.wait("tensor", "s_dve", c_xnT_all)
            for (w_sb, b_fm, OUT_T) in [(wq, bqf, QT), (wk, bkf, KT)]:
                for m in range(NC_):
                    for g in range(NC_):
                        pb = (m * NC_ + g) % 3
                        p.wait("tensor", "s_dve", pr_done[pb])
                        c_mm = None
                        for c_ in range(NC_):
                            if c_ == NC_ - 1:
                                c_mm = p.op("tensor", lambda e, w_sb=w_sb, m=m, g=g, c_=c_, pb=pb: e.matmul(
                                    ps_pr[pb][:], w_sb[:, c_ * D + m * 128:c_ * D + (m + 1) * 128],
                                    xnT[:, c_ * T + g * 512:c_ * T + (g + 1) * 512],
                                    start=False, stop=True), "s_pe")
                            else:
                                p.op("tensor", lambda e, w_sb=w_sb, m=m, g=g, c_=c_, pb=pb: e.matmul(
                                    ps_pr[pb][:], w_sb[:, c_ * D + m * 128:c_ * D + (m + 1) * 128],
                                    xnT[:, c_ * T + g * 512:c_ * T + (g + 1) * 512],
                                    start=(c_ == 0), stop=False))
                        p.wait("vector", "s_pe", c_mm)
                        p.wait("vector", "s_pe", qs_done[pb])
                        p.wait("vector", "s_dve", sw_done[pb])
                        c_ev = p.op("vector", lambda e, b_fm=b_fm, m=m, pb=pb: e.tensor_scalar(
                            q_sb[pb][:], ps_pr[pb][:], b_fm[:, m:m + 1], None,
                            op0=ALU.add), "s_dve")
                        pr_done[pb] = c_ev
                        p.wait("tensor", "s_dve", c_ev)
                        p.wait("tensor", "s_dve", swb_done[pb % 2])
                        c_sw = p.op("tensor", lambda e, pb=pb: e.matmul(
                            ps_sw[pb % 2][:], psw[:], q_sb[pb][:], start=True, stop=True), "s_pe")
                        qs_done[pb] = c_sw
                        p.wait("vector", "s_dve", c_ev)
                        p.op("vector", lambda e, g=g, pb=pb: e.tensor_tensor(
                            t1b[pb][:], q_sb[pb][:], cc[:, g * 512:(g + 1) * 512], op=ALU.mult))
                        p.wait("vector", "s_pe", c_sw)
                        c_t2 = p.op("vector", lambda e, g=g, pb=pb: e.tensor_tensor(
                            t2b[pb][:], ps_sw[pb % 2][:], ss[:, g * 512:(g + 1) * 512], op=ALU.mult), "s_dve")
                        sw_done[pb] = c_t2
                        swb_done[pb % 2] = c_t2
                        p.wait("vector", "s_dve", c_t2)
                        p.wait("vector", "s_st", rope_st[pb])
                        c_rp = p.op("vector", lambda e, pb=pb: e.tensor_tensor(
                            rope_b[pb][:], t1b[pb][:], t2b[pb][:], op=ALU.add), "s_dve")
                        p.wait("scalar", "s_dve", c_rp)
                        rope_st[pb] = p.op("scalar", lambda e, OUT_T=OUT_T, m=m, g=g, pb=pb: e.dma_start(
                            out=OUT_T[m * 128:(m + 1) * 128, g * 512:(g + 1) * 512],
                            in_=rope_b[pb][:]), "s_st", 16)

            v_ev_done = [0, 0, 0]
            v_st = [0, 0, 0]
            for tt in range(NTT):
                for n in range(2):
                    pb = (tt * 2 + n) % 3
                    p.wait("tensor", "s_dve", v_ev_done[pb])
                    p.wait("tensor", "s_dve", pr_done[pb])
                    p.op("tensor", lambda e, n=n, pb=pb: e.matmul(
                        ps_pr[pb][:], oneb[:], bvr[:, n * 512:(n + 1) * 512],
                        start=True, stop=False))
                    for c_ in range(NC_):
                        if c_ == NC_ - 1:
                            c_mm = p.op("tensor", lambda e, tt=tt, n=n, c_=c_, pb=pb: e.matmul(
                                ps_pr[pb][:], xnT[:, c_ * T + tt * 128:c_ * T + (tt + 1) * 128],
                                wv[:, c_ * D + n * 512:c_ * D + (n + 1) * 512],
                                start=False, stop=True), "s_pe")
                        else:
                            p.op("tensor", lambda e, tt=tt, n=n, c_=c_, pb=pb: e.matmul(
                                ps_pr[pb][:], xnT[:, c_ * T + tt * 128:c_ * T + (tt + 1) * 128],
                                wv[:, c_ * D + n * 512:c_ * D + (n + 1) * 512],
                                start=False, stop=False))
                    p.wait("vector", "s_pe", c_mm)
                    p.wait("vector", "s_st", v_st[pb])
                    c_ev = p.op("vector", lambda e, pb=pb: e.tensor_copy(
                        v_b[pb][:], ps_pr[pb][:]), "s_dve")
                    v_ev_done[pb] = c_ev
                    p.wait("scalar", "s_dve", c_ev)
                    v_st[pb] = p.op("scalar", lambda e, tt=tt, n=n, pb=pb: e.dma_start(
                        out=VH[tt * 128:(tt + 1) * 128, n * 512:(n + 1) * 512],
                        in_=v_b[pb][:]), "s_st", 16)

            c_lg = 0
            for g in range(NC_):
                p.wait("tensor", "s_dve", c_lg)
                for c_ in range(NC_):
                    if c_ == NC_ - 1:
                        c_mm = p.op("tensor", lambda e, g=g, c_=c_: e.matmul(
                            ps_g[0:NE, 0:512], wg[:, c_ * NE:(c_ + 1) * NE],
                            xnT[:, c_ * T + g * 512:c_ * T + (g + 1) * 512],
                            start=False, stop=True), "s_pe")
                    else:
                        p.op("tensor", lambda e, g=g, c_=c_: e.matmul(
                            ps_g[0:NE, 0:512], wg[:, c_ * NE:(c_ + 1) * NE],
                            xnT[:, c_ * T + g * 512:c_ * T + (g + 1) * 512],
                            start=(c_ == 0), stop=False))
                p.wait("vector", "s_pe", c_mm)
                c_lg = p.op("vector", lambda e, g=g: e.tensor_copy(
                    logT[0:NE, g * 512:(g + 1) * 512], ps_g[0:NE, 0:512]), "s_dve")
            for tt in range(NTT):
                p.wait("tensor", "s_dve", c_lg)
                c_tr = p.op("tensor", lambda e, tt=tt: e.transpose(
                    ps_g[:, 0:NE], logT[0:NE, tt * 128:(tt + 1) * 128],
                    idf[0:NE, 0:NE]), "s_pe")
                p.wait("vector", "s_pe", c_tr)
                c_lt = p.op("vector", lambda e: e.tensor_copy(
                    lt_b[:], ps_g[:, 0:NE]), "s_dve")
                p.wait("vector", "s_dve", c_lt)
                c_m1 = p.op("vector", lambda e: e.tensor_reduce(
                    g_m1[:], lt_b[:], axis=AX.X, op=ALU.max), "s_dve")
                p.wait("vector", "s_dve", c_m1)
                c_is1 = p.op("vector", lambda e: e.tensor_scalar(
                    g_is1[:], lt_b[:], g_m1[:], None, op0=ALU.is_ge), "s_dve")
                p.wait("vector", "s_dve", c_is1)
                c_msk = p.op("vector", lambda e: e.scalar_tensor_tensor(
                    g_msk[:], g_is1[:], NEG, lt_b[:], op0=ALU.mult, op1=ALU.add), "s_dve")
                p.wait("vector", "s_dve", c_msk)
                c_m2 = p.op("vector", lambda e: e.tensor_reduce(
                    g_m2[:], g_msk[:], axis=AX.X, op=ALU.max), "s_dve")
                p.wait("vector", "s_dve", c_m2)
                c_d = p.op("vector", lambda e: e.tensor_tensor(
                    g_d12[:], g_m1[:], g_m2[:], op=ALU.subtract), "s_dve")
                p.wait("scalar", "s_dve", c_d)
                p.op("scalar", lambda e: e.activation(g_w1[:], g_d12[:], AF.Sigmoid))
                c_w2 = p.op("scalar", lambda e: e.activation(
                    g_w2[:], g_d12[:], AF.Sigmoid, scale=-1.0), "s_act")
                c_is2 = p.op("vector", lambda e: e.tensor_scalar(
                    g_is2[:], g_msk[:], g_m2[:], None, op0=ALU.is_ge), "s_dve")
                p.wait("vector", "s_act", c_w2)
                p.wait("vector", "s_dve", c_is2)
                c_g1 = p.op("vector", lambda e: e.tensor_scalar(
                    g_g1[:], g_is1[:], g_w1[:], None, op0=ALU.mult), "s_dve")
                p.wait("vector", "s_dve", c_g1)
                c_gt = p.op("vector", lambda e, tt=tt: e.scalar_tensor_tensor(
                    gates[:, tt * NE:(tt + 1) * NE], g_is2[:], g_w2[:], g_g1[:],
                    op0=ALU.mult, op1=ALU.add), "s_dve")
                p.wait("vector", "s_dve", c_gt)
                c_gtb = p.op("vector", lambda e, tt=tt: e.tensor_copy(
                    gt_b[:], gates[:, tt * NE:(tt + 1) * NE]), "s_dve")
                p.wait("tensor", "s_dve", c_gtb)
                c_tr2 = p.op("tensor", lambda e: e.transpose(
                    ps_g[0:NE, 0:128], gt_b[:], idf[:]), "s_pe")
                p.wait("vector", "s_pe", c_tr2)
                c_lg = p.op("vector", lambda e, tt=tt: e.tensor_copy(
                    gatesT[0:NE, tt * 128:(tt + 1) * 128], ps_g[0:NE, 0:128]), "s_dve")

            p.wait("scalar", "s_dve", c_xnT_all)
            c_xnt_sp = 0
            for half in range(2):
                c_xnt_sp = p.op("scalar", lambda e, half=half: e.dma_start(
                    out=XNT.rearrange("(c p) t -> p c t", p=128)[:, half * 4:(half + 1) * 4, :],
                    in_=xnT[:].rearrange("p (c t) -> p c t", c=NC_)[:, half * 4:(half + 1) * 4, :]),
                    "s_st", 16)
            p.flush()

        if _VERBOSE:
            print("P2 sbuf_remaining", nc.sbuf_bytes_remaining)
        with ExitStack() as ph:
            sb = lambda n, s, d: ph.enter_context(nc.sbuf_tensor(n, s, d))
            ps = lambda n, s, d: ph.enter_context(nc.psum_tensor(n, s, d))
            qp = [sb(f"qp{i}", [128, T], BF16) for i in range(2)]
            kp = [sb(f"kp{i}", [128, T], BF16) for i in range(2)]
            vp = [sb(f"vp{i}", [128, NTT * 128], BF16) for i in range(2)]
            s_A = [sb(f"s_A{i}", [128, T], F16) for i in range(3)]
            s_B = [sb(f"s_B{i}", [128, T], F16) for i in range(3)]
            pbA = [sb(f"pbA{i}", [128, T], BF16) for i in range(2)]
            pbB = [sb(f"pbB{i}", [128, T], BF16) for i in range(2)]
            li_rA = sb("li_rA", [1, 128], F32)
            li_rB = sb("li_rB", [1, 128], F32)
            bc_sb = sb("bc_sb", [128, 128], F32)
            pT = sb("pT", [128, NTT * 128], BF16)
            nmA = [sb(f"nmA{i}", [128, 1], F32) for i in range(2)]
            nmB = [sb(f"nmB{i}", [128, 1], F32) for i in range(2)]
            lA2 = [sb(f"lA2{i}", [128, 1], F32) for i in range(2)]
            lB2 = [sb(f"lB2{i}", [128, 1], F32) for i in range(2)]
            lpA = [sb(f"lpA{i}", [128, 4], F32) for i in range(2)]
            lpB = [sb(f"lpB{i}", [128, 4], F32) for i in range(2)]
            nmpA = [sb(f"nmpA{i}", [128, 4], F32) for i in range(2)]
            nmpB = [sb(f"nmpB{i}", [128, 4], F32) for i in range(2)]
            mask = sb("mask_sb", [128, 4 * 512], F32)
            cat_b = [sb(f"cat_b{i}", [128, 128], BF16) for i in range(2)]
            ps_sA = [ps(f"ps_sA{i}", [128, 512], F32) for i in range(2)]
            ps_sB = [ps(f"ps_sB{i}", [128, 512], F32) for i in range(2)]
            ps_t = [ps(f"ps_t{i}", [128, 1024], BF16) for i in range(3)]
            ps_pv = ps("ps_pv", [128, 512], F32)

            c_mask = p.op("sync", lambda e: e.dma_start(out=mask[:], in_=MASK), "s_ld", 16)
            pair_ld = [0, 0]
            pair_free = [0, 0]
            cat_st = [0, 0]

            def load_pair(pr, buf):
                p.op("sync", lambda e, pr=pr, buf=buf: e.dma_start(
                    out=qp[buf][:], in_=QT[pr * 128:(pr + 1) * 128, :]), "s_ld", 16)
                p.op("sync", lambda e, pr=pr, buf=buf: e.dma_start(
                    out=kp[buf][:], in_=KT[pr * 128:(pr + 1) * 128, :]), "s_ld", 16)
                c = None
                for half in range(2):
                    c = p.op("sync", lambda e, pr=pr, buf=buf, half=half: e.dma_start(
                        out=vp[buf][:].rearrange("p (tt d) -> p tt d", tt=NTT)[:, half * 16:(half + 1) * 16, :],
                        in_=VH.rearrange("(tt p) d -> p tt d", p=128)[:, half * 16:(half + 1) * 16,
                            pr * 128:(pr + 1) * 128]), "s_ld", 16)
                return c

            p.wait("sync", "s_st", p.c("s_st"))
            pair_ld[0] = load_pair(0, 0)

            sc_evA = [0, 0]
            sc_evAd = [0, 0]
            sc_evB = [0, 0]
            evA_t = [0, 0, 0]
            evB_t = [0, 0, 0]
            exp_done3 = [0, 0, 0]
            lt_pe = [0, 0]
            tr_ev = [("s_dve", 0), ("s_act", 0), ("s_dve", 0)]
            pv_ev = 0
            exp_done = [0, 0]
            nm_done = [0, 0]
            l_done = [0, 0]
            pbf_used = [0, 0]
            for pr in range(NH // 2):
                buf = pr % 2
                if pr + 1 < NH // 2:
                    nbuf = (pr + 1) % 2
                    p.wait("sync", "s_pe", pair_free[nbuf])
                    pair_ld[nbuf] = load_pair(pr + 1, nbuf)
                p.wait("tensor", "s_ld", pair_ld[buf])
                p.wait("vector", "s_ld", c_mask)
                for ii in range(NTT + 3):
                    if ii < NTT:
                        i = ii
                        p3 = i % 3
                        jd = i // 4
                        r = i % 4
                        for j in range(jd + 1):
                            pbk = j % 2
                            p.wait("tensor", "s_act", sc_evA[pbk])
                            p.wait("tensor", "s_dve", max(sc_evB[pbk], sc_evAd[pbk]))
                            diag = (j == jd)
                            p.op("tensor", lambda e, i=i, j=j, pbk=pbk, buf=buf: e.matmul(
                                ps_sA[pbk][:], qp[buf][0:64, i * 128:(i + 1) * 128],
                                kp[buf][0:64, j * 512:(j + 1) * 512],
                                start=True, stop=True, tile_position=(0, 0)))
                            c_s = p.op("tensor", lambda e, i=i, j=j, pbk=pbk, buf=buf: e.matmul(
                                ps_sB[pbk][:], qp[buf][64:128, i * 128:(i + 1) * 128],
                                kp[buf][64:128, j * 512:(j + 1) * 512],
                                start=True, stop=True, tile_position=(64, 0)), "s_pe")
                            if not diag:
                                p.wait("scalar", "s_pe", c_s)
                                p.wait("scalar", "s_act", exp_done3[p3])
                                sc_evA[pbk] = p.op("scalar", lambda e, j=j, pbk=pbk, p3=p3: e.copy(
                                    s_A[p3][:, j * 512:(j + 1) * 512], ps_sA[pbk][:]), "s_act")
                                p.wait("vector", "s_pe", c_s)
                                p.wait("vector", "s_act", exp_done3[p3])
                                sc_evB[pbk] = p.op("vector", lambda e, j=j, pbk=pbk, p3=p3: e.tensor_copy(
                                    s_B[p3][:, j * 512:(j + 1) * 512], ps_sB[pbk][:]), "s_dve")
                            else:
                                p.wait("vector", "s_pe", c_s)
                                p.wait("vector", "s_act", exp_done3[p3])
                                sc_evAd[pbk] = p.op("vector", lambda e, j=j, r=r, pbk=pbk, p3=p3: e.tensor_tensor(
                                    s_A[p3][:, j * 512:(j + 1) * 512], ps_sA[pbk][:],
                                    mask[:, r * 512:(r + 1) * 512], op=ALU.add), "s_dve")
                                sc_evB[pbk] = p.op("vector", lambda e, j=j, r=r, pbk=pbk, p3=p3: e.tensor_tensor(
                                    s_B[p3][:, j * 512:(j + 1) * 512], ps_sB[pbk][:],
                                    mask[:, r * 512:(r + 1) * 512], op=ALU.add), "s_dve")
                        evA_t[p3] = max(sc_evA)
                        evB_t[p3] = max(max(sc_evB), max(sc_evAd))
                    if 3 <= ii:
                        i2 = ii - 3
                        par2 = i2 % 2
                        ktiles = i2 + 1
                        c_pv = None
                        p.wait("tensor", "s_dve", l_done[par2])
                        p.wait("tensor", "s_dve", pv_ev)
                        p.op("tensor", lambda e, par2=par2: e.transpose(
                            ps_pv[0:1, 256:384], lA2[par2][:], idf[:]))
                        c_lt = p.op("tensor", lambda e, par2=par2: e.transpose(
                            ps_pv[0:1, 384:512], lB2[par2][:], idf[:]), "s_pe")
                        lt_pe[par2] = c_lt
                        p.wait("vector", "s_pe", c_lt)
                        p.op("vector", lambda e: e.reciprocal(li_rA[:], ps_pv[0:1, 256:384]))
                        c_li = p.op("vector", lambda e: e.reciprocal(li_rB[:], ps_pv[0:1, 384:512]), "s_dve")
                        for hh in range(2):
                            pbuf = (pbA if hh == 0 else pbB)[par2]
                            last_dve = 0
                            last_act = 0
                            ngrp = (ktiles + 3) // 4
                            for gi in range(ngrp):
                                kt0 = gi * 4
                                kt1 = min(ktiles, kt0 + 4)
                                tb = gi % 3
                                sem_n, cnt = tr_ev[tb]
                                p.wait("tensor", sem_n, cnt)
                                c_tr = None
                                for kt in range(kt0, kt1):
                                    c_tr = p.op("tensor", lambda e, kt=kt, kt0=kt0, tb=tb, pbuf=pbuf: e.transpose(
                                        ps_t[tb][:, (kt - kt0) * 128:(kt - kt0 + 1) * 128],
                                        pbuf[:, kt * 128:(kt + 1) * 128],
                                        idb[:]), "s_pe")
                                if kt1 == ktiles and hh == 1:
                                    pbf_used[par2] = c_tr
                                eng = "vector" if gi % 2 == 0 else "scalar"
                                p.wait(eng, "s_pe", c_tr)
                                w = (kt1 - kt0) * 128
                                if eng == "vector":
                                    c_ev = p.op("vector", lambda e, kt0=kt0, w=w, tb=tb: e.tensor_copy(
                                        pT[:, kt0 * 128:kt0 * 128 + w], ps_t[tb][:, 0:w]), "s_dve")
                                    last_dve = c_ev
                                    tr_ev[tb] = ("s_dve", c_ev)
                                else:
                                    c_ev = p.op("scalar", lambda e, kt0=kt0, w=w, tb=tb: e.copy(
                                        pT[:, kt0 * 128:kt0 * 128 + w], ps_t[tb][:, 0:w]), "s_act")
                                    last_act = c_ev
                                    tr_ev[tb] = ("s_act", c_ev)
                            p.wait("tensor", "s_dve", last_dve)
                            p.wait("tensor", "s_act", last_act)
                            off = 0 if hh == 0 else 64
                            tp = (0, 0) if hh == 0 else (0, 64)
                            for kt in range(ktiles):
                                if kt == ktiles - 1:
                                    c_pv = p.op("tensor", lambda e, kt=kt, off=off, tp=tp, buf=buf: e.matmul(
                                        ps_pv[off:off + 64, 0:128],
                                        vp[buf][:, kt * 128 + off:kt * 128 + off + 64],
                                        pT[:, kt * 128:(kt + 1) * 128],
                                        start=(kt == 0), stop=True, tile_position=tp), "s_pe")
                                else:
                                    p.op("tensor", lambda e, kt=kt, off=off, tp=tp, buf=buf: e.matmul(
                                        ps_pv[off:off + 64, 0:128],
                                        vp[buf][:, kt * 128 + off:kt * 128 + off + 64],
                                        pT[:, kt * 128:(kt + 1) * 128],
                                        start=(kt == 0), stop=False, tile_position=tp))
                        p.wait("tensor", "s_dve", c_li)
                        p.op("tensor", lambda e: e.matmul(
                            ps_pv[0:64, 128:256], onef[0:1, 0:64], li_rA[:],
                            start=True, stop=True, tile_position=(0, 0)))
                        c_bc = p.op("tensor", lambda e: e.matmul(
                            ps_pv[64:128, 128:256], onef[0:1, 0:64], li_rB[:],
                            start=True, stop=True, tile_position=(0, 64)), "s_pe")
                        if i2 == NTT - 1:
                            pair_free[buf] = c_bc
                        cb = i2 % 2
                        p.wait("vector", "s_pe", c_bc)
                        c_bce = p.op("vector", lambda e: e.tensor_copy(
                            bc_sb[:], ps_pv[:, 128:256]), "s_dve")
                        p.wait("vector", "s_dve", c_bce)
                        p.wait("vector", "s_st", cat_st[cb])
                        c_cat = p.op("vector", lambda e, cb=cb: e.tensor_tensor(
                            cat_b[cb][:], ps_pv[:, 0:128], bc_sb[:], op=ALU.mult), "s_dve")
                        pv_ev = c_cat
                        p.wait("scalar", "s_dve", c_cat)
                        cat_st[cb] = p.op("scalar", lambda e, pr=pr, i2=i2, cb=cb: e.dma_start(
                            out=CATH[pr * 128:(pr + 1) * 128, i2 * 128:(i2 + 1) * 128],
                            in_=cat_b[cb][:]), "s_st", 16)
                    if 1 <= ii <= NTT:
                        x = ii - 1
                        p3 = x % 3
                        par = x % 2
                        ktot = (x // 4 + 1) * 512
                        npc = (ktot + 1023) // 1024
                        p.wait("vector", "s_act", evA_t[p3])
                        p.wait("vector", "s_dve", evB_t[p3])
                        p.wait("vector", "s_act", exp_done[par])
                        c_nmp = None
                        for pc in range(npc):
                            w0 = pc * 1024
                            w1 = min(ktot, w0 + 1024)
                            p.op("vector", lambda e, par=par, p3=p3, w0=w0, w1=w1, pc=pc: e.tensor_reduce(
                                nmpA[par][:, pc:pc + 1], s_A[p3][:, w0:w1], axis=AX.X, op=ALU.max))
                            c_nmp = p.op("vector", lambda e, par=par, p3=p3, w0=w0, w1=w1, pc=pc: e.tensor_reduce(
                                nmpB[par][:, pc:pc + 1], s_B[p3][:, w0:w1], axis=AX.X, op=ALU.max), "s_dve")
                        p.wait("vector", "s_dve", c_nmp)
                        p.op("vector", lambda e, par=par, npc=npc: e.tensor_reduce(
                            nmA[par][:], nmpA[par][:, 0:npc], axis=AX.X, op=ALU.max, negate=True))
                        c_nmB = p.op("vector", lambda e, par=par, npc=npc: e.tensor_reduce(
                            nmB[par][:], nmpB[par][:, 0:npc], axis=AX.X, op=ALU.max, negate=True), "s_dve")
                        p.wait("vector", "s_dve", c_nmB)
                        p.op("vector", lambda e, par=par: e.tensor_scalar(
                            nmA[par][:], nmA[par][:], 0.125, None, op0=ALU.mult))
                        nm_done[par] = p.op("vector", lambda e, par=par: e.tensor_scalar(
                            nmB[par][:], nmB[par][:], 0.125, None, op0=ALU.mult), "s_dve")
                    if 2 <= ii <= NTT + 1:
                        x = ii - 2
                        p3 = x % 3
                        par = x % 2
                        ktot1 = (x // 4 + 1) * 512
                        npc1 = (ktot1 + 1023) // 1024
                        p.wait("scalar", "s_dve", nm_done[par])
                        p.wait("scalar", "s_pe", pbf_used[par])
                        p.wait("scalar", "s_dve", l_done[par])
                        c_eB = None
                        for pc in range(npc1):
                            w0 = pc * 1024
                            w1 = min(ktot1, w0 + 1024)
                            p.op("scalar", lambda e, par=par, p3=p3, w0=w0, w1=w1, pc=pc: e.activation(
                                pbA[par][:, w0:w1], s_A[p3][:, w0:w1], AF.Exp,
                                bias=nmA[par][:], scale=0.125, accum_out=lpA[par][:, pc:pc + 1]))
                            c_eB = p.op("scalar", lambda e, par=par, p3=p3, w0=w0, w1=w1, pc=pc: e.activation(
                                pbB[par][:, w0:w1], s_B[p3][:, w0:w1], AF.Exp,
                                bias=nmB[par][:], scale=0.125, accum_out=lpB[par][:, pc:pc + 1]), "s_act")
                        exp_done[par] = c_eB
                        exp_done3[p3] = c_eB
                        p.wait("vector", "s_act", c_eB)
                        p.wait("vector", "s_pe", lt_pe[par])
                        p.op("vector", lambda e, par=par, npc=npc1: e.reduce_sum(
                            lA2[par][:], lpA[par][:, 0:npc], axis=AX.X))
                        l_done[par] = p.op("vector", lambda e, par=par, npc=npc1: e.reduce_sum(
                            lB2[par][:], lpB[par][:, 0:npc], axis=AX.X), "s_dve")
            p.flush()

        with ExitStack() as ph:
            sb = lambda n, s, d: ph.enter_context(nc.sbuf_tensor(n, s, d))
            ps = lambda n, s, d: ph.enter_context(nc.psum_tensor(n, s, d))
            cat = sb("cat", [128, NC_ * T], BF16)
            wo = sb("wo", [128, NC_ * D], BF16)
            bor = sb("bor", [1, D], BF16)
            at_b = [sb(f"at_b{i}", [128, 512], F32) for i in range(2)]
            ps_o = [ps(f"ps_o{i}", [128, 512], F32) for i in range(2)]

            p.op("gpsimd", lambda e: e.dma_start(
                out=wo[:].rearrange("p (c m) -> p c m", c=NC_),
                in_=WOT.rearrange("(c p) m -> p c m", p=128)), "s_cast", 16)
            c_bor = p.op("gpsimd", lambda e: e.dma_start(out=bor[:], in_=BO), "s_cast", 16)
            p.wait("sync", "s_st", p.c("s_st"))
            c_cat_ld = 0
            for half in range(2):
                c_cat_ld = p.op("sync", lambda e, half=half: e.dma_start(
                    out=cat[:].rearrange("p (c t) -> p c t", c=NC_)[:, half * 4:(half + 1) * 4, :],
                    in_=CATH.rearrange("(c p) t -> p c t", p=128)[:, half * 4:(half + 1) * 4, :]),
                    "s_ld", 16)
            at_ev = [0, 0]
            at_st = [0, 0]
            p.wait("tensor", "s_ld", c_cat_ld)
            p.wait("tensor", "s_cast", c_bor)
            for tt in range(NTT):
                for n in range(2):
                    pb = (tt * 2 + n) % 2
                    p.wait("tensor", "s_dve", at_ev[pb])
                    p.op("tensor", lambda e, n=n, pb=pb: e.matmul(
                        ps_o[pb][:], oneb[:], bor[:, n * 512:(n + 1) * 512],
                        start=True, stop=False))
                    for c_ in range(NC_):
                        if c_ == NC_ - 1:
                            c_mm = p.op("tensor", lambda e, tt=tt, n=n, c_=c_, pb=pb: e.matmul(
                                ps_o[pb][:], cat[:, c_ * T + tt * 128:c_ * T + (tt + 1) * 128],
                                wo[:, c_ * D + n * 512:c_ * D + (n + 1) * 512],
                                start=False, stop=True), "s_pe")
                        else:
                            p.op("tensor", lambda e, tt=tt, n=n, c_=c_, pb=pb: e.matmul(
                                ps_o[pb][:], cat[:, c_ * T + tt * 128:c_ * T + (tt + 1) * 128],
                                wo[:, c_ * D + n * 512:c_ * D + (n + 1) * 512],
                                start=False, stop=False))
                    p.wait("vector", "s_pe", c_mm)
                    p.wait("vector", "s_st", at_st[pb])
                    c_ev = p.op("vector", lambda e, pb=pb: e.tensor_copy(
                        at_b[pb][:], ps_o[pb][:]), "s_dve")
                    at_ev[pb] = c_ev
                    p.wait("scalar", "s_dve", c_ev)
                    at_st[pb] = p.op("scalar", lambda e, tt=tt, n=n, pb=pb: e.dma_start(
                        out=ATTH[tt * 128:(tt + 1) * 128, n * 512:(n + 1) * 512],
                        in_=at_b[pb][:]), "s_st", 16)
            p.flush()

        if _VERBOSE:
            print("P4 sbuf_remaining", nc.sbuf_bytes_remaining)
        with ExitStack() as ph:
            sb = lambda n, s, d: ph.enter_context(nc.sbuf_tensor(n, s, d))
            ps = lambda n, s, d: ph.enter_context(nc.psum_tensor(n, s, d))
            w2 = sb("w2", [128, 32 * D], BF16)
            w1blk = [sb(f"w1blk{i}", [128, NC_ * 128], BF16) for i in range(2)]
            xng = sb("xng", [128, NC_ * GRP], BF16)
            h_sb = sb("h_sb", [128, 32 * GRP], BF16)
            y_sb = sb("y_sb", [128, NC_ * GRP], F32)
            b1f = [sb(f"b1f{i}", [128, 32], F32) for i in range(2)]
            b2s = sb("b2s", [NE, D], BF16)
            x_t = sb("x_t", [128, D], F32)
            a_t = sb("a_t", [128, D], F32)
            o_t = [sb("o_t0", [128, D], F32)]
            ps_h = [ps(f"ps_h{i}", [128, 1024], F32) for i in range(2)]
            ps_oo = [ps(f"ps_oo{i}", [128, 512], F32) for i in range(2)]
            ps_i = ps("ps_i", [128, 512], F32)

            c_b2s = p.op("gpsimd", lambda e: e.dma_start(out=b2s[:], in_=B2), "s_cast", 16)
            p.wait("sync", "s_cast", c_wcache)
            p.wait("sync", "s_st", c_xnt_sp)

            h_ev = [0, 0]
            o_ev = [0, 0]
            i_ev = 0
            w1_ld = [0, 0]
            w1_free = [0, 0]
            w2_ld = 0
            w2_free = 0
            b1_ld = [0, 0]
            b1_free = [0, 0]
            xng_ld = 0
            xng_free = 0
            y_init = [0] * 16
            y_last = [0] * 16
            comb_dve = 0
            out_st = [0, 0]
            x_ld = [0, 0]
            att_ok = p.c("s_st")

            def load_w2(e_):
                c = None
                for q4 in range(4):
                    c = p.op("sync", lambda e, e_=e_, q4=q4: e.dma_start(
                        out=w2[:, q4 * 8 * D:(q4 + 1) * 8 * D],
                        in_=W2C[e_, :, q4 * 8 * D:(q4 + 1) * 8 * D]), "s_ld", 16)
                return c

            def load_w1(e_, mi):
                rs = mi % 2
                p.wait("sync", "s_pe", w1_free[rs])
                w1_ld[rs] = p.op("sync", lambda e, e_=e_, mi=mi, rs=rs: e.dma_start(
                    out=w1blk[rs][:], in_=W1C[e_, mi]), "s_ld", 16)

            for g in range(NGRP):
                p.wait("sync", "s_pe", xng_free)
                xng_ld = p.op("sync", lambda e, g=g: e.dma_start(
                    out=xng[:].rearrange("p (c t) -> p c t", c=NC_),
                    in_=XNT.rearrange("(c p) t -> p c t", p=128)[:, :, g * GRP:(g + 1) * GRP]),
                    "s_ld", 16)
                def emit_y_init(idx, g=g):
                    nonlocal i_ev
                    tt, n = idx // 2, idx % 2
                    p.wait("tensor", "s_dve", i_ev)
                    if g == 0 and idx == 0:
                        p.wait("tensor", "s_cast", c_b2s)
                    c_mm = p.op("tensor", lambda e, g=g, tt=tt, n=n: e.matmul(
                        ps_i[:], gatesT[0:NE, (g * NC_ + tt) * 128:(g * NC_ + tt + 1) * 128],
                        b2s[0:NE, n * 512:(n + 1) * 512], start=True, stop=True), "s_pe")
                    p.wait("vector", "s_pe", c_mm)
                    p.wait("vector", "s_dve", comb_dve)
                    i_ev = p.op("vector", lambda e, tt=tt, n=n: e.tensor_copy(
                        y_sb[:, tt * D + n * 512:tt * D + (n + 1) * 512], ps_i[:]), "s_dve")
                    y_init[idx] = i_ev
                for e_ in range(NE):
                    bb = e_ % 2
                    p.wait("sync", "s_pe", w2_free)
                    w2_ld = load_w2(e_)
                    p.wait("sync", "s_act", b1_free[bb])
                    b1_ld[bb] = p.op("sync", lambda e, e_=e_, bb=bb: e.dma_start(
                        out=b1f[bb][:], in_=B1F[e_]), "s_ld", 16)
                    for mi in range(2):
                        load_w1(e_, mi)
                    for mi in range(32):
                        rs = mi % 2
                        hb = mi % 2
                        p.wait("tensor", "s_ld", w1_ld[rs])
                        if mi == 0:
                            p.wait("tensor", "s_ld", xng_ld)
                            p.wait("tensor", "s_ld", b1_ld[bb])
                        p.wait("tensor", "s_act", h_ev[hb])
                        c_mm = None
                        for n in range(2):
                            for c_ in range(NC_):
                                if c_ == NC_ - 1:
                                    c_mm = p.op("tensor", lambda e, mi=mi, n=n, c_=c_, rs=rs, hb=hb: e.matmul(
                                        ps_h[hb][:, n * 512:(n + 1) * 512], w1blk[rs][:, c_ * 128:(c_ + 1) * 128],
                                        xng[:, c_ * GRP + n * 512:c_ * GRP + (n + 1) * 512],
                                        start=False, stop=True), "s_pe")
                                else:
                                    p.op("tensor", lambda e, mi=mi, n=n, c_=c_, rs=rs, hb=hb: e.matmul(
                                        ps_h[hb][:, n * 512:(n + 1) * 512], w1blk[rs][:, c_ * 128:(c_ + 1) * 128],
                                        xng[:, c_ * GRP + n * 512:c_ * GRP + (n + 1) * 512],
                                        start=(c_ == 0), stop=False))
                        p.wait("scalar", "s_pe", c_mm)
                        h_ev[hb] = p.op("scalar", lambda e, mi=mi, hb=hb, bb=bb: e.activation(
                            h_sb[:, mi * GRP:(mi + 1) * GRP],
                            ps_h[hb][:], AF.Gelu, bias=b1f[bb][:, mi:mi + 1]), "s_act")
                        w1_free[rs] = c_mm
                        if mi == 31:
                            xng_free = c_mm
                        if e_ == 0 and mi < 16:
                            emit_y_init(mi)
                        if mi + 2 < 32:
                            load_w1(e_, mi + 2)
                    b1_free[bb] = h_ev[1]
                    c_h_all = p.c("s_act")
                    p.wait("tensor", "s_act", c_h_all)
                    p.wait("tensor", "s_ld", w2_ld)
                    for n2 in range(2):
                        for tt in range(NC_):
                            ob = (n2 * NC_ + tt) % 2
                            p.wait("tensor", "s_dve", o_ev[ob])
                            for ic in range(32):
                                if ic == 31:
                                    c_mm = p.op("tensor", lambda e, n2=n2, tt=tt, ic=ic, ob=ob: e.matmul(
                                        ps_oo[ob][:], h_sb[:, ic * GRP + tt * 128:ic * GRP + (tt + 1) * 128],
                                        w2[:, ic * D + n2 * 512:ic * D + (n2 + 1) * 512],
                                        start=False, stop=True), "s_pe")
                                else:
                                    p.op("tensor", lambda e, n2=n2, tt=tt, ic=ic, ob=ob: e.matmul(
                                        ps_oo[ob][:], h_sb[:, ic * GRP + tt * 128:ic * GRP + (tt + 1) * 128],
                                        w2[:, ic * D + n2 * 512:ic * D + (n2 + 1) * 512],
                                        start=(ic == 0), stop=False))
                            if n2 == 1 and tt == NC_ - 1:
                                w2_free = c_mm
                            p.wait("vector", "s_pe", c_mm)
                            if e_ == 0:
                                p.wait("vector", "s_dve", y_init[tt * 2 + n2])
                            else:
                                p.wait("vector", "s_dve", y_last[tt * 2 + n2])
                            gcol = (g * NC_ + tt) * NE + e_
                            o_ev[ob] = y_last[tt * 2 + n2] = p.op("vector", lambda e, n2=n2, tt=tt, ob=ob, gcol=gcol: e.scalar_tensor_tensor(
                                y_sb[:, tt * D + n2 * 512:tt * D + (n2 + 1) * 512],
                                ps_oo[ob][:], gates[:, gcol:gcol + 1],
                                y_sb[:, tt * D + n2 * 512:tt * D + (n2 + 1) * 512],
                                op0=ALU.mult, op1=ALU.add), "s_dve")
                c_moe_done = p.c("s_dve")
                for tt in range(NC_):
                    tb = 0
                    gt = g * NC_ + tt
                    p.wait("sync", "s_st", att_ok)
                    p.wait("sync", "s_dve", x_ld[0])
                    p.op("sync", lambda e, gt=gt: e.dma_start(
                        out=x_t[:], in_=X[gt * 128:(gt + 1) * 128, :]), "s_ld", 16)
                    c_xa = p.op("sync", lambda e, gt=gt: e.dma_start(
                        out=a_t[:], in_=ATTH[gt * 128:(gt + 1) * 128, :]), "s_ld", 16)
                    p.wait("vector", "s_ld", c_xa)
                    p.wait("vector", "s_dve", c_moe_done)
                    p.wait("vector", "s_st", out_st[tb])
                    c_s1 = p.op("vector", lambda e, tb=tb: e.tensor_tensor(
                        o_t[tb][:], x_t[:], a_t[:], op=ALU.add), "s_dve")
                    p.wait("vector", "s_dve", c_s1)
                    c_o = p.op("vector", lambda e, tt=tt, tb=tb: e.tensor_tensor(
                        o_t[tb][:], o_t[tb][:], y_sb[:, tt * D:(tt + 1) * D], op=ALU.add), "s_dve")
                    x_ld[0] = c_o
                    comb_dve = c_o
                    p.wait("scalar", "s_dve", c_o)
                    out_st[tb] = p.op("scalar", lambda e, gt=gt, tb=tb: e.dma_start(
                        out=OUT[gt * 128:(gt + 1) * 128, :], in_=o_t[tb][:]), "s_st", 16)
            if _DEBUG_TAPS:
                p.wait("gpsimd", "s_st", p.c("s_st"))
                p.wait("gpsimd", "s_dve", p.c("s_dve"))
                for tname in _DEBUG_TAPS:
                    ap, shape = scratch_regs[tname]
                    dbg = nc.dram_tensor("dbg_" + tname, list(shape), F32,
                                         kind="ExternalOutput").ap()
                    p.op("gpsimd", lambda e, dbg=dbg, ap=ap: e.dma_start(
                        out=dbg, in_=ap), "s_cast", 16)
            p.wait("scalar", "s_st", p.c("s_st"))
            p.wait("gpsimd", "s_cast", p.c("s_cast"))
            p.flush()

    return nc


def _host_prep(L0_attn_W_q, L0_attn_b_q, L0_attn_W_k, L0_attn_b_k, L0_attn_W_v,
               L0_attn_W_o, L0_ffn_Wg, L0_ffn_W1, L0_ffn_b1, L0_ffn_W2):
    perm = []
    for h in range(NH):
        base = DH * h
        perm += [base + 2 * j for j in range(32)] + [base + 2 * j + 1 for j in range(32)]
    perm = np.array(perm)
    ac = np.ascontiguousarray
    Wq = np.asarray(L0_attn_W_q)
    Wk = np.asarray(L0_attn_W_k)
    common = {}
    common["wqt"] = ac(Wq.T[:, perm])
    common["wkt"] = ac(Wk.T[:, perm])
    common["wvt"] = ac(np.asarray(L0_attn_W_v).T)
    common["wot"] = ac(np.asarray(L0_attn_W_o).T)
    common["wgt"] = ac(np.asarray(L0_ffn_Wg).T)
    common["bq"] = ac(np.asarray(L0_attn_b_q)[perm].reshape(NC_, 128).T)
    common["bk"] = ac(np.asarray(L0_attn_b_k)[perm].reshape(NC_, 128).T)
    common["w1p"] = ac(np.asarray(L0_ffn_W1).reshape(NE, NC_, 128, 32, 128)
                       .transpose(0, 3, 2, 1, 4).reshape(NE, 32, 128, NC_ * 128))
    common["w2p"] = ac(np.asarray(L0_ffn_W2).reshape(NE, 32, 128, D)
                       .transpose(0, 2, 1, 3).reshape(NE, 128, 32 * D))
    common["b1f"] = ac(np.asarray(L0_ffn_b1).reshape(NE, 32, 128).transpose(0, 2, 1))
    j = np.arange(32, dtype=np.float64)
    theta = 10000.0 ** (-j / 32.0)
    t = np.arange(T, dtype=np.float64)
    ang = t[None, :] * theta[:, None]
    cosv = np.cos(ang)
    sinv = np.sin(ang)
    ccr = np.tile(cosv, (4, 1))
    ssr = np.concatenate([-sinv, sinv, -sinv, sinv], axis=0)
    common["cc"] = ccr.astype(BF)
    common["ss"] = ssr.astype(BF)
    psw = np.zeros((128, 128), dtype=BF)
    for pp in range(128):
        jj = pp // 32
        swp = pp + 32 if jj % 2 == 0 else pp - 32
        psw[swp, pp] = 1.0
    common["psw"] = psw
    common["idb"] = np.eye(128, dtype=BF)
    common["idf"] = np.eye(128, dtype=np.float32)
    common["oneb"] = np.ones((1, 128), dtype=BF)
    common["onef"] = np.ones((1, 128), dtype=np.float32)
    m = np.zeros((128, 4 * 512), dtype=np.float32)
    qq = np.arange(128)[:, None]
    kk = np.arange(512)[None, :]
    for r in range(4):
        m[:, r * 512:(r + 1) * 512] = np.where(kk <= 128 * r + qq, 0.0, NEG)
    common["mask"] = m
    return common


def kernel(L0_attn_x, L0_attn_W_q, L0_attn_b_q, L0_attn_W_k, L0_attn_b_k,
           L0_attn_W_v, L0_attn_b_v, L0_attn_W_o, L0_attn_b_o,
           L0_ffn_Wg, L0_ffn_W1, L0_ffn_b1, L0_ffn_W2, L0_ffn_b2):
    common = _host_prep(L0_attn_W_q, L0_attn_b_q, L0_attn_W_k, L0_attn_b_k,
                        L0_attn_W_v, L0_attn_W_o, L0_ffn_Wg, L0_ffn_W1,
                        L0_ffn_b1, L0_ffn_W2)
    common["bv"] = np.ascontiguousarray(np.asarray(L0_attn_b_v, dtype=np.float32)[None, :])
    common["bo"] = np.ascontiguousarray(np.asarray(L0_attn_b_o, dtype=np.float32)[None, :])
    common["b2"] = np.ascontiguousarray(np.asarray(L0_ffn_b2, dtype=np.float32))
    common = {k: (np.asarray(v, dtype=np.float32) if np.asarray(v).dtype == np.float64 else np.asarray(v))
              for k, v in common.items()}
    x = np.asarray(L0_attn_x, dtype=np.float32)
    in_maps = []
    for c in range(8):
        m = dict(common)
        m["x"] = np.ascontiguousarray(x[c])
        in_maps.append(m)
    nc = build_program()
    res = run_bass_kernel_spmd(nc, in_maps, core_ids=list(range(8)))
    out = np.stack([np.asarray(res.results[c]["out"]) for c in range(8)], axis=0)
    return out
